# version 08
# baseline (speedup 1.0000x reference)
import contextlib
import numpy as np
import ml_dtypes
import concourse.bass as bass
import concourse.mybir as mybir
from concourse.bass_utils import run_bass_kernel_spmd

F32 = mybir.dt.float32
BF16 = mybir.dt.bfloat16
AF = mybir.ActivationFunctionType
ALU = mybir.AluOpType
AX = mybir.AxisListType

D = 2048
L = 4096
H = 16
ND = 16
NT = 32
HF = 8192
CH = 128
SBW = 32
NCH = 32
QSC = float(128.0 ** -0.5)
EPS_LN = 1e-5
EPS_RMS = float(np.finfo(np.float32).eps)
VW = [SBW * (s + 1) for s in range(4)]
VOFF = [0, 1024, 3072, 6144]
KTVW = 10240
NPAIR = 64


def build(debug=False):
    nc = bass.Bass("TRN2", target_bir_lowering=False)

    def dram(name, shape, dt, kind="Internal"):
        if debug and kind == "Internal":
            kind = "ExternalOutput"
        return nc.dram_tensor(name, shape, dt, kind=kind).ap()

    x = dram("x", [L, D], F32, "ExternalInput")
    wq = dram("wq", [D, D], F32, "ExternalInput")
    wk = dram("wk", [D, D], F32, "ExternalInput")
    wv = dram("wv", [D, D], F32, "ExternalInput")
    wg = dram("wg", [D, D], F32, "ExternalInput")
    wo = dram("wo", [D, D], F32, "ExternalInput")
    gamma = dram("gamma", [D], F32, "ExternalInput")
    beta = dram("beta", [D], F32, "ExternalInput")
    wgu = dram("wgu", [2 * HF, D], F32, "ExternalInput")
    wpr = dram("wpr", [D, HF], F32, "ExternalInput")
    identc = dram("identc", [128, 128], BF16, "ExternalInput")
    maskc = dram("maskc", [128, 128], mybir.dt.uint8, "ExternalInput")
    ones1c = dram("ones1c", [128, 1], F32, "ExternalInput")
    out = dram("out", [L, D], F32, "ExternalOutput")

    xbf = dram("xbf", [L, D], BF16)
    wqb = dram("wqb", [D, D], BF16)
    wkb = dram("wkb", [D, D], BF16)
    wvb = dram("wvb", [D, D], BF16)
    wgb = dram("wgb", [D, D], BF16)
    wob = dram("wob", [D, D], BF16)
    wguq = dram("wguq", [2 * HF, D], BF16)
    wprq = dram("wprq", [D, HF], BF16)
    qt = dram("qt", [D, L], BF16)
    kt = dram("kt", [D, L], BF16)
    gsp = dram("gsp", [D, L], F32)
    vtok = dram("vtok", [H, L, 128], BF16)
    ysp = dram("ysp", [H, L, 128], BF16)
    lny = dram("lny", [L, D], BF16)
    x1f = dram("x1f", [L, D], F32)
    x1b = dram("x1b", [L, D], BF16)
    hsp = dram("hsp", [NPAIR, NT, 128, 128], BF16)
    prt = dram("prt", [L, D], F32)
    ssqd = dram("ssqd", [L], F32)

    g = contextlib.ExitStack()
    with g:
        sems = {}

        def sem(name):
            sems[name] = g.enter_context(nc.semaphore(name))
            return sems[name]

        vself = g.enter_context(nc.semaphore("vself"))
        vctr = [0]

        def vsync(v, instr):
            vctr[0] += 1
            instr.then_inc(vself, 1)
            v.wait_ge(vself, vctr[0])
            return instr

        identsb = g.enter_context(nc.sbuf_tensor("identsb", [128, 128], BF16))
        masksb = g.enter_context(nc.sbuf_tensor("masksb", [128, 128],
                                                 mybir.dt.uint8))
        ones1sb = g.enter_context(nc.sbuf_tensor("ones1sb", [128, 1], F32))

        s0 = sem("s0")
        sc0 = sem("sc0")

        pstk = contextlib.ExitStack()
        with pstk:
            xts = pstk.enter_context(
                nc.sbuf_tensor("xts", [128, ND * L], BF16, is_transpose_dest=True))
            wgT = [pstk.enter_context(
                nc.sbuf_tensor(f"wgT{i}", [128, ND * 512], BF16,
                               is_transpose_dest=True)) for i in range(2)]
            psA = pstk.enter_context(nc.psum_tensor("psA", [128, 4096], F32))
            sxts = sem("sxts")

            gstk = contextlib.ExitStack()
            with gstk:
                gstg = [gstk.enter_context(
                    nc.sbuf_tensor(f"gstg{i}", [128, L], F32)) for i in range(2)]
                gab = [gstk.enter_context(
                    nc.sbuf_tensor(f"gab{i}", [128, 512], F32)) for i in range(2)]
                geb = [gstk.enter_context(
                    nc.sbuf_tensor(f"geb{i}", [128, 512], F32)) for i in range(2)]
                glb = [gstk.enter_context(
                    nc.sbuf_tensor(f"glb{i}", [128, 512], F32)) for i in range(2)]

                g_ld = sem("g_ld")
                g_pe = sem("g_pe")
                g_ab = sem("g_ab")
                g_ln = sem("g_ln")
                g_sb = sem("g_sb")
                g_st = sem("g_st")

                with nc.Block() as block:
                    @block.sync
                    def _(sy):
                        sy.dma_start(identsb[:], identc).then_inc(sc0, 16)
                        sy.dma_start(masksb[:], maskc).then_inc(sc0, 16)
                        sy.dma_start(ones1sb[:], ones1c).then_inc(sc0, 16)
                        for grp in range(2):
                            sy.wait_ge(s0, 16)
                            for d in range(ND):
                                sy.dma_start_transpose(
                                    wgT[grp % 2][:, d * 512:(d + 1) * 512],
                                    wgb[grp * 512:(grp + 1) * 512,
                                        d * 128:(d + 1) * 128]).then_inc(g_ld, 16)
                        for d in range(ND):
                            sy.wait_ge(s0, 16 + 16 * (d + 1))
                            sy.dma_start_transpose(
                                xts[:, d * L:(d + 1) * L],
                                xbf[:, d * 128:(d + 1) * 128]).then_inc(sxts, 16)
                        for grp in range(2, 4):
                            sy.wait_ge(g_pe, 32 * (grp - 1))
                            for d in range(ND):
                                sy.dma_start_transpose(
                                    wgT[grp % 2][:, d * 512:(d + 1) * 512],
                                    wgb[grp * 512:(grp + 1) * 512,
                                        d * 128:(d + 1) * 128]).then_inc(g_ld, 16)
                        sy.wait_ge(sc0, 48)

                    @block.tensor
                    def _(te):
                        te.wait_ge(sxts, 16 * ND)
                        for dout in range(ND):
                            grp, di = dout // 4, dout % 4
                            if di == 0:
                                te.wait_ge(g_ld, 16 * ND * (grp + 1))
                            if dout > 0:
                                te.wait_ge(g_ab, 8 * dout)
                                te.wait_ge(g_sb, 8 * dout)
                            for d in range(ND):
                                for tc in range(8):
                                    mm = te.matmul(
                                        psA[:, tc * 512:(tc + 1) * 512],
                                        wgT[grp % 2][:, d * 512 + di * 128:
                                                     d * 512 + di * 128 + 128],
                                        xts[:, d * L + tc * 512:
                                            d * L + tc * 512 + 512],
                                        start=(d == 0), stop=(d == ND - 1))
                                    if d == ND - 1:
                                        mm.then_inc(g_pe, 1)

                    @block.scalar
                    def _(sc):
                        for dout in range(ND):
                            for tc in range(8):
                                i = dout * 8 + tc
                                sc.wait_ge(g_pe, i + 1)
                                ps = psA[:, tc * 512:(tc + 1) * 512]
                                sc.activation(gab[tc % 2][:], ps,
                                              AF.Abs).then_inc(g_ab, 1)
                                sc.activation(geb[tc % 2][:], gab[tc % 2][:],
                                              AF.Exp, scale=-1.0)
                                if i >= 2:
                                    sc.wait_ge(g_sb, i - 1)
                                sc.activation(glb[tc % 2][:], geb[tc % 2][:],
                                              AF.Ln, bias=1.0).then_inc(g_ln, 1)

                    @block.vector
                    def _(v):
                        for dout in range(ND):
                            if dout >= 2:
                                v.wait_ge(g_st, 16 * (dout - 1))
                            for tc in range(8):
                                i = dout * 8 + tc
                                v.wait_ge(g_pe, i + 1)
                                v.wait_ge(g_ln, i + 1)
                                ps = psA[:, tc * 512:(tc + 1) * 512]
                                v.scalar_tensor_tensor(
                                    gstg[dout % 2][:, tc * 512:(tc + 1) * 512],
                                    ps, 0.0, glb[tc % 2][:],
                                    ALU.min, ALU.subtract).then_inc(g_sb, 1)

                    @block.gpsimd
                    def _(gp):
                        gp.dma_start(wgb, wg).then_inc(s0, 16)
                        for d in range(ND):
                            gp.dma_start(xbf[:, d * 128:(d + 1) * 128],
                                         x[:, d * 128:(d + 1) * 128]
                                         ).then_inc(s0, 16)
                        casts = {0: (wqb, wq), 4: (wkb, wk),
                                 8: (wvb, wv), 12: (wob, wo)}
                        for dout in range(ND):
                            if dout in casts:
                                dst_, src_ = casts[dout]
                                gp.dma_start(dst_, src_).then_inc(s0, 16)
                            gp.wait_ge(g_sb, 8 * (dout + 1))
                            gp.dma_start(gsp[dout * 128:(dout + 1) * 128, :],
                                         gstg[dout % 2][:]).then_inc(g_st, 16)
                        gp.wait_ge(g_st, 16 * ND)

            qstk = contextlib.ExitStack()
            with qstk:
                ostg = [qstk.enter_context(
                    nc.sbuf_tensor(f"ostg{i}", [128, L], BF16)) for i in range(2)]
                qwt = [qstk.enter_context(
                    nc.sbuf_tensor(f"qwt{i}", [128, D], BF16)) for i in range(2)]
                qwh = [qstk.enter_context(
                    nc.sbuf_tensor(f"qwh{i}", [128, D], BF16)) for i in range(2)]
                qred = qstk.enter_context(nc.sbuf_tensor("qred", [128, 32], F32))
                qsbf = qstk.enter_context(nc.sbuf_tensor("qsbf", [128, 32], BF16))
                qrq = qstk.enter_context(nc.sbuf_tensor("qrq", [128, 32], F32))
                qrr = qstk.enter_context(nc.sbuf_tensor("qrr", [128, D], BF16))
                qlt = qstk.enter_context(nc.sbuf_tensor("qlt", [128, D], BF16))
                qgt = qstk.enter_context(nc.sbuf_tensor("qgt", [128, D], BF16))

                CFGS = [("q", wqb, qt, QSC, 0), ("k", wkb, kt, None, 64)]
                QS_ = {}
                for tag, _, _, _, _ in CFGS:
                    for snm in ("ld", "pe", "ev", "st", "qld", "qdv", "qst"):
                        QS_[f"{tag}_{snm}"] = sem(f"{tag}_{snm}")

                with nc.Block() as block:
                    @block.sync
                    def _(sy):
                        for pi_, (tag, wsrc, dst, scale, tile0) in enumerate(CFGS):
                            sld, qdv, qst = (QS_[f"{tag}_ld"], QS_[f"{tag}_qdv"],
                                             QS_[f"{tag}_qst"])
                            sy.wait_ge(s0, 16 * (18 + pi_))
                            for grp in range(4):
                                if pi_ == 1 and grp < 2:
                                    sy.wait_ge(QS_["q_pe"], 96 + 32 * grp)
                                if grp >= 2:
                                    sy.wait_ge(QS_[f"{tag}_pe"], 32 * (grp - 1))
                                for d in range(ND):
                                    sy.dma_start_transpose(
                                        wgT[grp % 2][:, d * 512:(d + 1) * 512],
                                        wsrc[grp * 512:(grp + 1) * 512,
                                             d * 128:(d + 1) * 128]
                                    ).then_inc(sld, 16)
                                lo = [0, 8, 16, 40][grp]
                                hi = [8, 16, 40, 64][grp]
                                for i in range(lo, hi):
                                    sy.wait_ge(qdv, i + 1)
                                    sy.dma_start(
                                        wguq[(tile0 + i) * 128:
                                             (tile0 + i + 1) * 128, :],
                                        qwh[i % 2][:]).then_inc(qst, 16)
                            sy.wait_ge(qst, 16 * 64)

                    @block.tensor
                    def _(te):
                        te.wait_ge(sxts, 16 * ND)
                        for pi_, (tag, wsrc, dst, scale, tile0) in enumerate(CFGS):
                            sld, spe, sev = (QS_[f"{tag}_ld"], QS_[f"{tag}_pe"],
                                             QS_[f"{tag}_ev"])
                            if pi_ == 1:
                                te.wait_ge(QS_["q_ev"], 128)
                            for dout in range(ND):
                                grp, di = dout // 4, dout % 4
                                if di == 0:
                                    te.wait_ge(sld, 16 * ND * (grp + 1))
                                if dout > 0:
                                    te.wait_ge(sev, 8 * dout)
                                for d in range(ND):
                                    for tc in range(8):
                                        mm = te.matmul(
                                            psA[:, tc * 512:(tc + 1) * 512],
                                            wgT[grp % 2][:, d * 512 + di * 128:
                                                         d * 512 + di * 128 + 128],
                                            xts[:, d * L + tc * 512:
                                                d * L + tc * 512 + 512],
                                            start=(d == 0), stop=(d == ND - 1))
                                        if d == ND - 1:
                                            mm.then_inc(spe, 1)

                    @block.scalar
                    def _(sc):
                        for pi_, (tag, wsrc, dst, scale, tile0) in enumerate(CFGS):
                            spe, sev, sst = (QS_[f"{tag}_pe"], QS_[f"{tag}_ev"],
                                             QS_[f"{tag}_st"])
                            if pi_ == 1:
                                sc.wait_ge(QS_["q_st"], 16 * 16)
                            for dout in range(ND):
                                if dout >= 2:
                                    sc.wait_ge(sst, 16 * (dout - 1))
                                for tc in range(8):
                                    sc.wait_ge(spe, dout * 8 + tc + 1)
                                    ps = psA[:, tc * 512:(tc + 1) * 512]
                                    dsl = ostg[dout % 2][:, tc * 512:(tc + 1) * 512]
                                    if scale is None:
                                        sc.copy(dsl, ps).then_inc(sev, 1)
                                    else:
                                        sc.mul(dsl, ps, scale).then_inc(sev, 1)

                    @block.gpsimd
                    def _(gp):
                        for pi_, (tag, wsrc, dst, scale, tile0) in enumerate(CFGS):
                            sev, sst, qld = (QS_[f"{tag}_ev"], QS_[f"{tag}_st"],
                                             QS_[f"{tag}_qld"])
                            if pi_ == 1:
                                gp.wait_ge(QS_["q_qdv"], 64)
                            for dout in range(ND):
                                gp.wait_ge(sev, 8 * (dout + 1))
                                gp.dma_start(dst[dout * 128:(dout + 1) * 128, :],
                                             ostg[dout % 2][:]).then_inc(sst, 16)
                                for i in range(4 * dout, 4 * dout + 4):
                                    if i >= 2:
                                        gp.wait_ge(QS_[f"{tag}_qst"], 16 * (i - 1))
                                    gp.dma_start(
                                        qwt[i % 2][:],
                                        wgu[(tile0 + i) * 128:
                                            (tile0 + i + 1) * 128, :]
                                    ).then_inc(qld, 16)
                            gp.wait_ge(sst, 16 * ND)

                    @block.vector
                    def _(v):
                        for pi_, (tag, wsrc, dst, scale, tile0) in enumerate(CFGS):
                            qld, qdv = QS_[f"{tag}_qld"], QS_[f"{tag}_qdv"]
                            if pi_ == 1:
                                v.wait_ge(QS_["q_qst"], 16 * 64)
                            for i in range(64):
                                v.wait_ge(qld, 16 * (i + 1))
                                w3 = qwt[i % 2][:].rearrange("p (a j) -> p a j",
                                                             a=32, j=64)
                                vsync(v, v.reduce_sum(
                                    qred[:].rearrange("p (a o) -> p a o",
                                                      a=32, o=1),
                                    w3, AX.X, apply_absolute_value=True))
                                vsync(v, v.tensor_single_scalar(
                                    qred[:], qred[:], 1.0 / 64, ALU.mult))
                                vsync(v, v.tensor_single_scalar(
                                    qsbf[:], qred[:], 1e-8, ALU.max))
                                vsync(v, v.reciprocal(qrq[:], qsbf[:]))
                                v.tensor_tensor(
                                    qrr[:].rearrange("p (a j) -> p a j",
                                                     a=32, j=64),
                                    w3,
                                    qrq[:].unsqueeze(2).broadcast_to(
                                        [128, 32, 64]), ALU.mult)
                                v.tensor_single_scalar(qlt[:], qrr[:], -0.5,
                                                       ALU.is_lt)
                                v.tensor_single_scalar(qgt[:], qrr[:], 0.5,
                                                       ALU.is_gt)
                                v.tensor_sub(qgt[:], qgt[:], qlt[:])
                                if i >= 2:
                                    v.wait_ge(QS_[f"{tag}_qst"], 16 * (i - 1))
                                v.tensor_tensor(
                                    qwh[i % 2][:].rearrange("p (a j) -> p a j",
                                                            a=32, j=64),
                                    qgt[:].rearrange("p (a j) -> p a j",
                                                     a=32, j=64),
                                    qsbf[:].unsqueeze(2).broadcast_to(
                                        [128, 32, 64]),
                                    ALU.mult).then_inc(qdv, 1)

            vstk = contextlib.ExitStack()
            with vstk:
                vstg = [vstk.enter_context(
                    nc.sbuf_tensor(f"vstg{i}", [128, D], BF16)) for i in range(2)]
                wvT2 = [vstk.enter_context(
                    nc.sbuf_tensor(f"wvT2{i}", [128, ND * 512], BF16,
                                   is_transpose_dest=True)) for i in range(2)]
                sv_ld = sem("sv_ld")
                sv_pe = sem("sv_pe")
                sv_ev = sem("sv_ev")
                sv_st = sem("sv_st")

                with nc.Block() as block:
                    @block.sync
                    def _(sy):
                        sy.wait_ge(s0, 16 * 20)
                        wdst = [wgT[0], wgT[1], wvT2[0], wvT2[1]]
                        for vc in range(4):
                            for d in range(ND):
                                sy.dma_start_transpose(
                                    wdst[vc][:, d * 512:(d + 1) * 512],
                                    wvb[vc * 512:(vc + 1) * 512,
                                        d * 128:(d + 1) * 128]).then_inc(sv_ld, 16)

                    @block.tensor
                    def _(te):
                        te.wait_ge(sv_ld, 16 * 4 * ND)
                        wsrc = [wgT[0], wgT[1], wvT2[0], wvT2[1]]
                        for tt in range(NT):
                            if tt >= 2:
                                te.wait_ge(sv_ev, 4 * (tt - 1))
                            for d in range(ND):
                                for vc in range(4):
                                    bank = (tt % 2) * 4 + vc
                                    mm = te.matmul(
                                        psA[:, bank * 512:(bank + 1) * 512],
                                        xts[:, d * L + tt * 128:
                                            d * L + tt * 128 + 128],
                                        wsrc[vc][:, d * 512:(d + 1) * 512],
                                        start=(d == 0), stop=(d == ND - 1))
                                    if d == ND - 1:
                                        mm.then_inc(sv_pe, 1)

                    @block.scalar
                    def _(sc):
                        for tt in range(NT):
                            if tt >= 2:
                                sc.wait_ge(sv_st, 16 * (tt - 1))
                            for vc in range(4):
                                sc.wait_ge(sv_pe, tt * 4 + vc + 1)
                                bank = (tt % 2) * 4 + vc
                                sc.copy(vstg[tt % 2][:, vc * 512:(vc + 1) * 512],
                                        psA[:, bank * 512:(bank + 1) * 512]
                                        ).then_inc(sv_ev, 1)

                    @block.gpsimd
                    def _(gp):
                        vdst = vtok.rearrange("h t v -> t h v")
                        for tt in range(NT):
                            gp.wait_ge(sv_ev, 4 * (tt + 1))
                            gp.dma_start(
                                vdst[tt * 128:(tt + 1) * 128, :, :],
                                vstg[tt % 2][:].rearrange("p (h v) -> p h v",
                                                          h=H, v=128)
                            ).then_inc(sv_st, 16)
                        gp.wait_ge(sv_st, 16 * NT)

        astk = contextlib.ExitStack()
        with astk:
            a_g = astk.enter_context(nc.sbuf_tensor("a_g", [128, L], F32))
            a_q = [astk.enter_context(nc.sbuf_tensor(f"a_q{i}", [128, L], BF16))
                   for i in range(2)]
            a_k = [astk.enter_context(nc.sbuf_tensor(f"a_k{i}", [128, L], BF16))
                   for i in range(2)]
            a_gg = astk.enter_context(nc.sbuf_tensor("a_gg", [128, L + 1], F32))
            p_ar = [astk.enter_context(nc.sbuf_tensor(f"p_ar{i}", [128, 2048], F32))
                    for i in range(2)]
            a_e = [astk.enter_context(nc.sbuf_tensor(f"a_e{i}", [128, 2048], BF16))
                   for i in range(2)]
            a_qt = [astk.enter_context(nc.sbuf_tensor(f"a_qt{i}", [128, L], BF16))
                    for i in range(2)]
            a_qh = [astk.enter_context(nc.sbuf_tensor(f"a_qh{i}", [128, L], BF16))
                    for i in range(2)]
            a_kh = [astk.enter_context(nc.sbuf_tensor(f"a_kh{i}", [128, L], BF16))
                    for i in range(2)]
            a_kv = [astk.enter_context(nc.sbuf_tensor(f"a_kv{i}", [128, KTVW], BF16))
                    for i in range(2)]
            a_dc = [astk.enter_context(nc.sbuf_tensor(f"a_dc{i}", [128, NCH], F32))
                    for i in range(2)]
            a_da = astk.enter_context(nc.sbuf_tensor("a_da", [128, NCH], F32))
            b_v = [astk.enter_context(nc.sbuf_tensor(f"b_v{i}", [128, L], BF16))
                   for i in range(2)]
            b_s = [astk.enter_context(nc.sbuf_tensor(f"b_s{i}", [128, 128], BF16))
                   for i in range(2)]
            b_a = [astk.enter_context(nc.sbuf_tensor(f"b_a{i}", [128, 128], BF16))
                   for i in range(2)]
            b_z = astk.enter_context(nc.sbuf_tensor("b_z", [128, 128], BF16))
            b_kt = [astk.enter_context(nc.sbuf_tensor(f"b_kt{i}", [128, 128], BF16))
                    for i in range(2)]
            b_y = astk.enter_context(nc.sbuf_tensor("b_y", [128, 1024], BF16))
            pAm = [astk.enter_context(nc.psum_tensor(f"pAm{i}", [128, 128], F32))
                   for i in range(2)]
            pY = [astk.enter_context(nc.psum_tensor(f"pY{i}", [128, 128], F32))
                  for i in range(2)]
            pS = [astk.enter_context(nc.psum_tensor(f"pS{i}", [128, 128], F32))
                  for i in range(2)]
            pT = [astk.enter_context(nc.psum_tensor(f"pT{i}", [128, 128], BF16))
                  for i in range(2)]

            sgl = sem("sgl")
            sql = sem("sql")
            sa_sc = sem("sa_sc")
            sa_pg = sem("sa_pg")
            sa_ag = sem("sa_ag")
            sa_ex = sem("sa_ex")
            sa_mu = sem("sa_mu")
            bl = sem("bl")
            bpa = sem("bpa")
            bpt = sem("bpt")
            bpy = sem("bpy")
            bps = sem("bps")
            bda = sem("bda")
            bds = sem("bds")
            bkt = sem("bkt")
            bay = sem("bay")
            byst = sem("byst")
            NK = 32 * H

            def SD(K):
                return K + (K // 32) + 2

            VSPEC = [(0, 0, 32), (1, 0, 32), (2, 0, 16), (2, 1, 16),
                     (3, 0, 16), (3, 1, 16)]

            def stage_arg(v, h, k, ab=None):
                if k < 6:
                    which, hf = k // 2, k % 2
                    base = hf * 2048
                    ob = ab[k % 2]
                    i0 = a_gg[:, 1 + base:1 + base + 2048]
                    if which == 0:
                        ref = a_gg[:, base:base + 2048].rearrange(
                            "p (c j) -> p c j", c=16, j=128)[:, :, 0:1]
                        return v.tensor_tensor(
                            ob[:].rearrange("p (c j) -> p c j", c=16, j=128),
                            i0.rearrange("p (c j) -> p c j", c=16, j=128),
                            ref.broadcast_to([128, 16, 128]), ALU.subtract)
                    if which == 1:
                        ref = a_gg[:, base:base + 2048].rearrange(
                            "p (b j) -> p b j", b=64, j=32)[:, :, 0:1]
                        return v.tensor_tensor(
                            ob[:].rearrange("p (b j) -> p b j", b=64, j=32),
                            i0.rearrange("p (b j) -> p b j", b=64, j=32),
                            ref.broadcast_to([128, 64, 32]), ALU.subtract)
                    ref = a_gg[:, 1 + base:1 + base + 2048].rearrange(
                        "p (c j) -> p c j", c=16, j=128)[:, :, 127:128]
                    return v.tensor_tensor(
                        ob[:].rearrange("p (c j) -> p c j", c=16, j=128),
                        ref.broadcast_to([128, 16, 128]),
                        i0.rearrange("p (c j) -> p c j", c=16, j=128),
                        ALU.subtract)
                if k < 12:
                    s, part, ncb = VSPEC[k - 6]
                    w = VW[s]
                    cs = part * 16
                    gj = a_gg[:, 1 + cs * 128:1 + (cs + ncb) * 128].rearrange(
                        "p (c j) -> p c j", c=ncb, j=128)[:, :, 0:w]
                    ref = a_gg[:, 0:L].rearrange(
                        "p (c j) -> p c j", c=NCH, j=128)[:, cs:cs + ncb,
                                                          SBW * s:SBW * s + 1]
                    ob = ab[k % 2]
                    return v.tensor_tensor(
                        ob[:, 0:ncb * w].rearrange(
                            "p (c j) -> p c j", c=ncb, j=w),
                        ref.broadcast_to([128, ncb, w]), gj, ALU.subtract)
                ends = a_gg[:, 1:1 + L].rearrange(
                    "p (c j) -> p c j", c=NCH, j=128)[:, :, 127:128]
                starts = a_gg[:, 0:L].rearrange(
                    "p (c j) -> p c j", c=NCH, j=128)[:, :, 0:1]
                return v.tensor_tensor(
                    a_da[:].rearrange("p (c o) -> p c o", c=NCH, o=1),
                    ends, starts, ALU.subtract)

            def stage_w(k):
                if k < 6:
                    return 2048
                if k < 12:
                    s, part, ncb = VSPEC[k - 6]
                    return ncb * VW[s]
                return NCH

            def stage_mul(v, h, k):
                hb = h % 2
                if k < 6:
                    which, hf = k // 2, k % 2
                    srcqk = a_q[hb] if which < 2 else a_k[hb]
                    dst = (a_qt, a_qh, a_kh)[which][hb]
                    return v.tensor_tensor(dst[:, hf * 2048:(hf + 1) * 2048],
                                           srcqk[:, hf * 2048:(hf + 1) * 2048],
                                           a_e[k % 2][:], ALU.mult)
                s, part, ncb = VSPEC[k - 6]
                w = VW[s]
                cs = part * 16
                ks = a_k[hb][:, cs * 128:(cs + ncb) * 128].rearrange(
                    "p (c j) -> p c j", c=ncb, j=128)[:, :, 0:w]
                return v.tensor_tensor(
                    a_kv[hb][:, VOFF[s] + cs * w:
                             VOFF[s] + (cs + ncb) * w].rearrange(
                        "p (c j) -> p c j", c=ncb, j=w),
                    ks,
                    a_e[k % 2][:, 0:ncb * w].rearrange("p (c j) -> p c j",
                                                       c=ncb, j=w),
                    ALU.mult)


            def prep_dve_slots(h):
                def guard(v, h=h):
                    if h >= 1:
                        v.wait_ge(bpy, 32 * (h - 1))
                        v.wait_ge(bps, 32 * (h - 1))
                        v.wait_ge(sa_pg, 12 * h)
                    v.wait_ge(sgl, 16 * (h + 1))
                    v.memset(a_gg[:, 0:1], 0.0)

                def scanq(qtr):
                    def op(v, h=h, qtr=qtr):
                        ini = 0.0 if qtr == 0 else a_gg[:, qtr * 1024:qtr * 1024 + 1]
                        vsync(v, v.tensor_tensor_scan(
                            a_gg[:, 1 + qtr * 1024:1 + (qtr + 1) * 1024],
                            a_g[:, qtr * 1024:(qtr + 1) * 1024],
                            a_g[:, qtr * 1024:(qtr + 1) * 1024],
                            ini, ALU.add, ALU.bypass))
                        if qtr == 3:
                            v.engine_nop().then_inc(sa_sc, 1)
                    return op

                def arg(k):
                    def op(v, h=h, k=k):
                        if h >= 1:
                            v.wait_ge(sa_ex, 13 * h)
                        stage_arg(v, h, k).then_inc(sa_ag, 1)
                    return op

                def mul(k):
                    def op(v, h=h, k=k):
                        if k == 0:
                            v.wait_ge(sql, 32 * (h + 1))
                        v.wait_ge(sa_ex, 13 * h + k + 1)
                        stage_mul(v, h, k).then_inc(sa_mu, 1)
                    return op

                slots = {1: [guard], 2: [scanq(0), scanq(1)],
                         3: [scanq(2), scanq(3)], 4: [arg(12)]}
                for k in range(12):
                    slots.setdefault(6 + 2 * k, []).append(mul(k))
                return slots

            def prep_act_ops(h):
                ops = []
                for k in range(13):
                    def ex(sc, h=h, k=k):
                        if k < 12:
                            sc.wait_ge(sa_pg, 12 * h + k + 1)
                        else:
                            sc.wait_ge(sa_ag, h + 1)
                        if k >= 2 and k < 12:
                            sc.wait_ge(sa_mu, 12 * h + k - 1)
                        elif k == 0 and h >= 1:
                            sc.wait_ge(sa_mu, 12 * h)
                        if k < 12:
                            w = stage_w(k)
                            sc.activation(a_e[k % 2][:, 0:w],
                                          p_ar[k % 2][:, 0:w],
                                          AF.Exp).then_inc(sa_ex, 1)
                        else:
                            if h >= 2:
                                sc.wait_ge(bds, 33 * (h - 1))
                            sc.activation(a_dc[h % 2][:], a_da[:],
                                          AF.Exp).then_inc(sa_ex, 1)
                    ops.append(ex)
                return ops

            def emit_A(te, K):
                h, c = K // 32, K % 32
                te.wait_ge(bda, K + 1)
                for s in range(4):
                    w = VW[s]
                    mm = te.matmul(
                        pAm[K % 2][0:w, SBW * s:SBW * s + SBW],
                        a_kv[h % 2][:, VOFF[s] + c * w:VOFF[s] + (c + 1) * w],
                        a_qh[h % 2][:, c * 128 + SBW * s:c * 128 + SBW * s + SBW],
                        start=True, stop=True)
                    if s == 3:
                        mm.then_inc(bpa, 1)

            with nc.Block() as block:
                @block.sync
                def _(sy):
                    for h in range(H):
                        if h >= 1:
                            sy.wait_ge(sa_sc, h)
                        sy.dma_start(a_g[:],
                                     gsp[h * 128:(h + 1) * 128, :]).then_inc(sgl, 16)
                        if h >= 2:
                            sy.wait_ge(sa_mu, 12 * (h - 1))
                        sy.dma_start(a_q[h % 2][:],
                                     qt[h * 128:(h + 1) * 128, :]).then_inc(sql, 16)
                        sy.dma_start(a_k[h % 2][:],
                                     kt[h * 128:(h + 1) * 128, :]).then_inc(sql, 16)
                        if h >= 2:
                            sy.wait_ge(bpy, 32 * (h - 1))
                            sy.wait_ge(bps, 32 * (h - 1))
                        sy.dma_start(
                            b_v[h % 2][:].rearrange("p (c v) -> p c v",
                                                    c=NCH, v=128),
                            vtok[h, :, :].rearrange("(c p) v -> p c v",
                                                    c=NCH, p=128)).then_inc(bl, 16)

                @block.gpsimd
                def _(gp):
                    for h in range(H):
                        gp.wait_ge(sa_sc, h + 1)
                        for k in range(12):
                            if k >= 2:
                                gp.wait_ge(sa_ex, 13 * h + k - 1)
                            elif h >= 1:
                                gp.wait_ge(sa_ex, 13 * (h - 1) + 11 + k)
                            stage_arg(gp, h, k, p_ar).then_inc(sa_pg, 1)

                @block.vector
                def _(v):
                    v.memset(b_z[:], 0.0)
                    v.memset(pAm[0][:], 0.0).then_inc(bda, 1)
                    v.memset(pAm[1][:], 0.0).then_inc(bda, 1)
                    for c2 in range(1, 30):
                        for op in prep_dve_slots(0).get(c2, []):
                            op(v)
                    v.wait_ge(bpa, 1)
                    v.select(b_a[0][:], masksb[:], pAm[0][:],
                             b_z[:]).then_inc(bda, 1)
                    for h in range(H):
                        pq = prep_dve_slots(h + 1) if h + 1 < H else {}
                        v.wait_ge(bpy, 32 * h)
                        v.memset(b_s[0][:], 0.0).then_inc(bds, 1)
                        for c in range(NCH):
                            K = 32 * h + c
                            v.wait_ge(bps, K + 1)
                            v.wait_ge(bpy, K)
                            v.scalar_tensor_tensor(
                                b_s[(c + 1) % 2][:], b_s[c % 2][:],
                                a_dc[h % 2][:, c:c + 1], pS[K % 2][:],
                                ALU.mult, ALU.add).then_inc(bds, 1)
                            v.wait_ge(bds, SD(K))
                            if K + 1 < NK:
                                v.wait_ge(bpa, K + 2)
                                v.wait_ge(bpy, K)
                                v.select(b_a[(K + 1) % 2][:], masksb[:],
                                         pAm[(K + 1) % 2][:],
                                         b_z[:]).then_inc(bda, 1)
                            for op in pq.get(c, []):
                                op(v)

                @block.scalar
                def _(sc):
                    for op in prep_act_ops(0):
                        op(sc)
                    for h in range(H):
                        aq = prep_act_ops(h + 1) if h + 1 < H else []
                        aq = ({5 + 2 * k: [aq[k]] for k in range(12)}
                              | {28: [aq[12]]}) if aq else {}
                        for c in range(NCH):
                            K = 32 * h + c
                            sc.wait_ge(bpt, K + 1)
                            if K >= 2:
                                sc.wait_ge(bps, K - 1)
                            sc.copy(b_kt[K % 2][:], pT[K % 2][:]).then_inc(bkt, 1)
                            sc.wait_ge(bpy, K + 1)
                            if K // 4 >= 1:
                                sc.wait_ge(byst, 16 * (K // 4 - 1))
                            sc.copy(b_y[:, (K % 8) * 128:(K % 8 + 1) * 128],
                                    pY[K % 2][:]).then_inc(bay, 1)
                            if K % 4 == 3:
                                hb = ((K // 4) % 2) * 512
                                sc.dma_start(
                                    ysp[h, (c - 3) * 128:(c + 1) * 128, :].rearrange(
                                        "(b p) v -> p b v", b=4, p=128),
                                    b_y[:, hb:hb + 512].rearrange(
                                        "p (b v) -> p b v", b=4, v=128)
                                ).then_inc(byst, 16)
                            for op in aq.get(c, []):
                                op(sc)
                    sc.wait_ge(byst, 16 * (NK // 4))

                @block.tensor
                def _(te):
                    te.wait_ge(sa_mu, 12)
                    te.wait_ge(sa_ex, 13)
                    te.wait_ge(bl, 16)
                    emit_A(te, 0)
                    te.wait_ge(bkt, 0)
                    te.transpose(pT[0][:], a_kh[0][:, 0:128],
                                 identsb[:]).then_inc(bpt, 1)
                    for K in range(NK):
                        h, c = K // 32, K % 32
                        if c == 0 and h > 0:
                            te.wait_ge(sa_mu, 12 * (h + 1))
                            te.wait_ge(sa_ex, 13 * (h + 1))
                            te.wait_ge(bl, 16 * (h + 1))
                        te.wait_ge(bkt, K + 1)
                        if K >= 2:
                            te.wait_ge(bds, SD(K - 2))
                        te.matmul(pS[K % 2][:], b_kt[K % 2][:],
                                  b_v[h % 2][:, c * 128:(c + 1) * 128],
                                  start=True, stop=True).then_inc(bps, 1)
                        if K + 1 < NK:
                            K1 = K + 1
                            h1, c1 = K1 // 32, K1 % 32
                            if c1 == 0:
                                te.wait_ge(sa_mu, 12 * (h1 + 1))
                                te.wait_ge(sa_ex, 13 * (h1 + 1))
                                te.wait_ge(bl, 16 * (h1 + 1))
                            emit_A(te, K1)
                            if K1 >= 2:
                                te.wait_ge(bkt, K1 - 1)
                            te.transpose(pT[K1 % 2][:],
                                         a_kh[h1 % 2][:, c1 * 128:(c1 + 1) * 128],
                                         identsb[:]).then_inc(bpt, 1)
                        te.wait_ge(bda, K + 3)
                        te.wait_ge(bds, SD(K - 1) if c > 0 else 33 * h + 1)
                        if K >= 1:
                            te.wait_ge(bay, K - 1)
                        te.matmul(pY[K % 2][:], b_a[K % 2][:],
                                  b_v[h % 2][:, c * 128:(c + 1) * 128],
                                  start=True, stop=False)
                        te.matmul(pY[K % 2][:], a_qt[h % 2][:, c * 128:(c + 1) * 128],
                                  b_s[c % 2][:], start=False,
                                  stop=True).then_inc(bpy, 1)
        castk = contextlib.ExitStack()
        with castk:
            c_y = [castk.enter_context(nc.sbuf_tensor(f"c_y{i}", [128, D], BF16))
                   for i in range(2)]
            c_yc = castk.enter_context(nc.sbuf_tensor("c_yc", [128, D], F32))
            c_t1 = castk.enter_context(nc.sbuf_tensor("c_t1", [128, D], F32))
            c_junk = castk.enter_context(nc.sbuf_tensor("c_junk", [128, D], BF16))
            c_ln = [castk.enter_context(nc.sbuf_tensor(f"c_ln{i}", [128, D], BF16))
                    for i in range(2)]
            c_st = castk.enter_context(nc.sbuf_tensor("c_st", [128, 8], F32))
            c_gam = castk.enter_context(nc.sbuf_tensor("c_gam", [128, D], F32))
            c_bet = castk.enter_context(nc.sbuf_tensor("c_bet", [128, D], F32))
            pwt2 = castk.enter_context(nc.sbuf_tensor("pwt2", [128, HF], BF16))
            pwh2 = castk.enter_context(nc.sbuf_tensor("pwh2", [128, HF], BF16))
            pred2 = castk.enter_context(nc.sbuf_tensor("pred2", [128, 128], F32))
            psbf2 = castk.enter_context(nc.sbuf_tensor("psbf2", [128, 128], BF16))
            prq2 = castk.enter_context(nc.sbuf_tensor("prq2", [128, 128], F32))
            prr2 = castk.enter_context(nc.sbuf_tensor("prr2", [128, HF], BF16))
            plt2 = castk.enter_context(nc.sbuf_tensor("plt2", [128, HF], BF16))
            pgt2 = castk.enter_context(nc.sbuf_tensor("pgt2", [128, HF], BF16))

            ca_ld = sem("ca_ld")
            ca_d1 = sem("ca_d1")
            ca_a1 = sem("ca_a1")
            ca_d2 = sem("ca_d2")
            ca_a2 = sem("ca_a2")
            ca_dn = sem("ca_dn")
            ca_st = sem("ca_st")
            cq_ld = sem("cq_ld")
            cq_dv = sem("cq_dv")
            cq_st = sem("cq_st")

            with nc.Block() as block:
                @block.sync
                def _(sy):
                    sy.dma_start(c_gam[:],
                                 gamma.unsqueeze(0).broadcast_to([128, D])
                                 ).then_inc(ca_ld, 16)
                    sy.dma_start(c_bet[:],
                                 beta.unsqueeze(0).broadcast_to([128, D])
                                 ).then_inc(ca_ld, 16)
                    ysrc = ysp.rearrange("h t v -> t h v")
                    for tt in range(NT):
                        if tt >= 2:
                            sy.wait_ge(ca_dn, tt - 1)
                        sy.dma_start(
                            c_y[tt % 2][:].rearrange("p (h v) -> p h v", h=H, v=128),
                            ysrc[tt * 128:(tt + 1) * 128, :, :]).then_inc(ca_ld, 16)
                        if tt % 4 == 0 and 1 <= tt // 4:
                            i = tt // 4 - 1
                            sy.wait_ge(cq_dv, i + 1)
                            sy.dma_start(wprq[(8 + i) * 128:(9 + i) * 128, :],
                                         pwh2[:]).then_inc(cq_st, 16)
                    sy.wait_ge(cq_dv, 8)
                    sy.dma_start(wprq[15 * 128:16 * 128, :],
                                 pwh2[:]).then_inc(cq_st, 16)
                    sy.wait_ge(cq_st, 16 * 8)

                @block.vector
                def _(v):
                    for tt in range(NT):
                        v.wait_ge(ca_ld, 16 * (tt + 3))
                        vsync(v, v.reduce_sum(c_st[:, 1:2], c_y[tt % 2][:], AX.X))
                        vsync(v, v.tensor_single_scalar(c_st[:, 0:1],
                                                        c_st[:, 1:2],
                                                        1.0 / D, ALU.mult))
                        v.tensor_scalar(c_yc[:], c_y[tt % 2][:], c_st[:, 0:1],
                                        None, ALU.subtract).then_inc(ca_d1, 1)
                        v.wait_ge(ca_a1, tt + 1)
                        vsync(v, v.tensor_single_scalar(c_st[:, 2:3],
                                                        c_st[:, 4:5],
                                                        1.0 / D, ALU.mult))
                        v.tensor_single_scalar(c_st[:, 2:3], c_st[:, 2:3],
                                               EPS_LN, ALU.add).then_inc(ca_d2, 1)
                        v.wait_ge(ca_a2, tt + 1)
                        vsync(v, v.reciprocal(c_st[:, 3:4], c_st[:, 5:6]))
                        v.scalar_tensor_tensor(c_t1[:], c_yc[:], c_st[:, 3:4],
                                               c_gam[:], ALU.mult, ALU.mult)
                        if tt >= 2:
                            v.wait_ge(ca_st, 16 * (tt - 1))
                        v.tensor_tensor(c_ln[tt % 2][:], c_t1[:], c_bet[:],
                                        ALU.add).then_inc(ca_dn, 1)
                        if tt % 4 == 3 and tt // 4 < 8:
                            i = tt // 4
                            v.wait_ge(cq_ld, 16 * (i + 1))
                            if i >= 1:
                                v.wait_ge(cq_st, 16 * i)
                            w3 = pwt2[:].rearrange("p (a j) -> p a j",
                                                   a=128, j=64)
                            vsync(v, v.reduce_sum(
                                pred2[:].rearrange("p (a o) -> p a o", a=128, o=1),
                                w3, AX.X, apply_absolute_value=True))
                            vsync(v, v.tensor_single_scalar(pred2[:], pred2[:],
                                                            1.0 / 64, ALU.mult))
                            vsync(v, v.tensor_single_scalar(psbf2[:], pred2[:],
                                                            1e-8, ALU.max))
                            vsync(v, v.reciprocal(prq2[:], psbf2[:]))
                            v.tensor_tensor(
                                prr2[:].rearrange("p (a j) -> p a j", a=128, j=64),
                                w3,
                                prq2[:].unsqueeze(2).broadcast_to([128, 128, 64]),
                                ALU.mult)
                            v.tensor_single_scalar(plt2[:], prr2[:], -0.5,
                                                   ALU.is_lt)
                            v.tensor_single_scalar(pgt2[:], prr2[:], 0.5,
                                                   ALU.is_gt)
                            v.tensor_sub(pgt2[:], pgt2[:], plt2[:])
                            v.tensor_tensor(
                                pwh2[:].rearrange("p (a j) -> p a j", a=128, j=64),
                                pgt2[:].rearrange("p (a j) -> p a j", a=128, j=64),
                                psbf2[:].unsqueeze(2).broadcast_to([128, 128, 64]),
                                ALU.mult).then_inc(cq_dv, 1)

                @block.scalar
                def _(sc):
                    for tt in range(NT):
                        sc.wait_ge(ca_d1, tt + 1)
                        sc.activation(c_junk[:], c_yc[:], AF.Square,
                                      accum_out=c_st[:, 4:5]).then_inc(ca_a1, 1)
                        sc.wait_ge(ca_d2, tt + 1)
                        sc.activation(c_st[:, 5:6], c_st[:, 2:3],
                                      AF.Sqrt).then_inc(ca_a2, 1)

                @block.gpsimd
                def _(gp):
                    for tt in range(NT):
                        gp.wait_ge(ca_dn, tt + 1)
                        gp.dma_start(lny[tt * 128:(tt + 1) * 128, :],
                                     c_ln[tt % 2][:]).then_inc(ca_st, 16)
                        if tt % 4 == 0 and tt // 4 < 8:
                            i = tt // 4
                            if i >= 1:
                                gp.wait_ge(cq_dv, i)
                            gp.dma_start(pwt2[:],
                                         wpr[(8 + i) * 128:(9 + i) * 128, :]
                                         ).then_inc(cq_ld, 16)
                    gp.wait_ge(ca_st, 16 * NT)

        dstk = contextlib.ExitStack()
        with dstk:
            d_lnyT = dstk.enter_context(
                nc.sbuf_tensor("d_lnyT", [128, ND * 2048], BF16,
                               is_transpose_dest=True))
            d_woT = dstk.enter_context(
                nc.sbuf_tensor("d_woT", [128, ND * 2048], BF16,
                               is_transpose_dest=True))
            d_x = [dstk.enter_context(nc.sbuf_tensor(f"d_x{i}", [128, D], F32))
                   for i in range(2)]
            d_rc = dstk.enter_context(nc.sbuf_tensor("d_rc", [128, D], F32))
            d_junk = dstk.enter_context(nc.sbuf_tensor("d_junk", [128, D], BF16))
            d_x1 = [dstk.enter_context(nc.sbuf_tensor(f"d_x1{i}", [128, D], F32))
                    for i in range(2)]
            d_x1b = [dstk.enter_context(nc.sbuf_tensor(f"d_x1b{i}", [128, D], BF16))
                     for i in range(2)]
            d_st = dstk.enter_context(nc.sbuf_tensor("d_st", [128, 8], F32))
            qw2t = dstk.enter_context(nc.sbuf_tensor("qw2t", [128, 2048], BF16))
            qw2h = [dstk.enter_context(nc.sbuf_tensor(f"qw2h{i}", [128, 2048], BF16))
                    for i in range(2)]
            q2red = dstk.enter_context(nc.sbuf_tensor("q2red", [128, 32], F32))
            q2sbf = dstk.enter_context(nc.sbuf_tensor("q2sbf", [128, 32], BF16))
            q2rq = dstk.enter_context(nc.sbuf_tensor("q2rq", [128, 32], F32))
            q2rr = dstk.enter_context(nc.sbuf_tensor("q2rr", [128, 2048], BF16))
            q2lt = dstk.enter_context(nc.sbuf_tensor("q2lt", [128, 2048], BF16))
            q2gt = dstk.enter_context(nc.sbuf_tensor("q2gt", [128, 2048], BF16))
            psB = dstk.enter_context(nc.psum_tensor("psB", [128, 4096], F32))

            da_wld = sem("da_wld")
            da_yld = sem("da_yld")
            da_xld = sem("da_xld")
            da_pe = sem("da_pe")
            da_d1 = sem("da_d1")
            da_a1 = sem("da_a1")
            da_d2 = sem("da_d2")
            da_a2 = sem("da_a2")
            da_dnv = sem("da_dnv")
            da_dna = sem("da_dna")
            da_st = sem("da_st")
            q2_ld = sem("q2_ld")
            q2_dv = sem("q2_dv")
            q2_st = sem("q2_st")

            with nc.Block() as block:
                @block.sync
                def _(sy):
                    def q2store(u):
                        i, qq = u // 4, u % 4
                        sy.wait_ge(q2_dv, u + 1)
                        sy.dma_start(
                            wprq[i * 128:(i + 1) * 128,
                                 qq * 2048:(qq + 1) * 2048],
                            qw2h[u % 2][:]).then_inc(q2_st, 16)

                    sy.wait_ge(s0, 16 * 21)
                    for d in range(ND):
                        sy.dma_start_transpose(
                            d_woT[:, d * 2048:(d + 1) * 2048],
                            wob[:, d * 128:(d + 1) * 128]).then_inc(da_wld, 16)
                    for d in range(ND):
                        sy.dma_start_transpose(
                            d_lnyT[:, d * 2048:(d + 1) * 2048],
                            lny[0:2048, d * 128:(d + 1) * 128]).then_inc(da_yld, 16)
                    for u in range(14):
                        q2store(u)
                    sy.wait_ge(da_pe, 64)
                    for d in range(ND):
                        sy.dma_start_transpose(
                            d_lnyT[:, d * 2048:(d + 1) * 2048],
                            lny[2048:4096, d * 128:(d + 1) * 128]).then_inc(da_yld, 16)
                    for u in range(14, 32):
                        q2store(u)
                    sy.wait_ge(q2_st, 16 * 32)

                @block.tensor
                def _(te):
                    te.wait_ge(da_wld, 16 * ND)
                    for tt in range(NT):
                        th, ti = tt // 16, tt % 16
                        te.wait_ge(da_yld, 16 * ND * (th + 1))
                        if tt >= 2:
                            te.wait_ge(da_d1, 2 * (tt - 1))
                        for d in range(ND):
                            for oc in range(4):
                                bank = (tt % 2) * 4 + oc
                                mm = te.matmul(
                                    psB[:, bank * 512:(bank + 1) * 512],
                                    d_lnyT[:, d * 2048 + ti * 128:
                                           d * 2048 + ti * 128 + 128],
                                    d_woT[:, d * 2048 + oc * 512:
                                          d * 2048 + oc * 512 + 512],
                                    start=(d == 0), stop=(d == ND - 1))
                                if d == ND - 1:
                                    mm.then_inc(da_pe, 1)

                @block.vector
                def _(v):
                    for tt in range(NT):
                        v.wait_ge(da_pe, 4 * (tt + 1))
                        v.wait_ge(da_xld, 16 * (tt + 1))
                        bank0 = (tt % 2) * 4
                        v.tensor_tensor(d_rc[:],
                                        psB[:, bank0 * 512:(bank0 + 4) * 512],
                                        d_x[tt % 2][:], ALU.add).then_inc(da_d1, 1)
                        vsync(v, v.reduce_sum(d_st[:, 1:2], d_rc[:], AX.X))
                        vsync(v, v.tensor_single_scalar(d_st[:, 0:1],
                                                        d_st[:, 1:2],
                                                        1.0 / D, ALU.mult))
                        v.tensor_scalar(d_rc[:], d_rc[:], d_st[:, 0:1], None,
                                        ALU.subtract).then_inc(da_d1, 1)
                        v.wait_ge(da_a1, tt + 1)
                        vsync(v, v.tensor_single_scalar(d_st[:, 2:3],
                                                        d_st[:, 4:5],
                                                        1.0 / D, ALU.mult))
                        v.tensor_single_scalar(d_st[:, 2:3], d_st[:, 2:3],
                                               EPS_LN, ALU.add).then_inc(da_d2, 1)
                        v.wait_ge(da_a2, tt + 1)
                        vsync(v, v.reciprocal(d_st[:, 3:4], d_st[:, 5:6]))
                        if tt >= 2:
                            v.wait_ge(da_st, 32 * (tt - 1))
                        v.tensor_scalar(d_x1[tt % 2][:], d_rc[:], d_st[:, 3:4],
                                        None, ALU.mult).then_inc(da_dnv, 1)
                        u = tt
                        v.wait_ge(q2_ld, 16 * (u + 1))
                        if u >= 2:
                            v.wait_ge(q2_st, 16 * (u - 1))
                        w3 = qw2t[:].rearrange("p (a j) -> p a j", a=32, j=64)
                        vsync(v, v.reduce_sum(
                            q2red[:].rearrange("p (a o) -> p a o", a=32, o=1),
                            w3, AX.X, apply_absolute_value=True))
                        vsync(v, v.tensor_single_scalar(q2red[:], q2red[:],
                                                        1.0 / 64, ALU.mult))
                        vsync(v, v.tensor_single_scalar(q2sbf[:], q2red[:],
                                                        1e-8, ALU.max))
                        vsync(v, v.reciprocal(q2rq[:], q2sbf[:]))
                        v.tensor_tensor(q2rr[:].rearrange("p (a j) -> p a j",
                                                          a=32, j=64), w3,
                                        q2rq[:].unsqueeze(2).broadcast_to(
                                            [128, 32, 64]), ALU.mult)
                        v.tensor_single_scalar(q2lt[:], q2rr[:], -0.5, ALU.is_lt)
                        v.tensor_single_scalar(q2gt[:], q2rr[:], 0.5, ALU.is_gt)
                        v.tensor_sub(q2gt[:], q2gt[:], q2lt[:])
                        v.tensor_tensor(qw2h[u % 2][:].rearrange(
                                            "p (a j) -> p a j", a=32, j=64),
                                        q2gt[:].rearrange("p (a j) -> p a j",
                                                          a=32, j=64),
                                        q2sbf[:].unsqueeze(2).broadcast_to(
                                            [128, 32, 64]),
                                        ALU.mult).then_inc(q2_dv, 1)

                @block.scalar
                def _(sc):
                    for tt in range(NT):
                        sc.wait_ge(da_d1, 2 * tt + 2)
                        sc.activation(d_junk[:], d_rc[:], AF.Square,
                                      accum_out=d_st[:, 4:5]).then_inc(da_a1, 1)
                        sc.wait_ge(da_d2, tt + 1)
                        sc.activation(d_st[:, 5:6], d_st[:, 2:3],
                                      AF.Sqrt).then_inc(da_a2, 1)
                        sc.wait_ge(da_dnv, tt + 1)
                        if tt >= 2:
                            sc.wait_ge(da_st, 32 * (tt - 1))
                        sc.copy(d_x1b[tt % 2][:],
                                d_x1[tt % 2][:]).then_inc(da_dna, 1)

                @block.gpsimd
                def _(gp):
                    gp.dma_start(d_x[0][:], x[0:128, :]).then_inc(da_xld, 16)
                    gp.dma_start(d_x[1][:], x[128:256, :]).then_inc(da_xld, 16)
                    gp.dma_start(qw2t[:],
                                 wpr[0:128, 0:2048]).then_inc(q2_ld, 16)
                    for tt in range(NT):
                        if tt + 1 < NT:
                            u = tt + 1
                            i, qq = u // 4, u % 4
                            gp.wait_ge(q2_dv, u)
                            gp.dma_start(
                                qw2t[:],
                                wpr[i * 128:(i + 1) * 128,
                                    qq * 2048:(qq + 1) * 2048]).then_inc(q2_ld, 16)
                        gp.wait_ge(da_dnv, tt + 1)
                        gp.wait_ge(da_dna, tt + 1)
                        gp.dma_start(x1f[tt * 128:(tt + 1) * 128, :],
                                     d_x1[tt % 2][:]).then_inc(da_st, 16)
                        gp.dma_start(x1b[tt * 128:(tt + 1) * 128, :],
                                     d_x1b[tt % 2][:]).then_inc(da_st, 16)
                        if tt + 2 < NT:
                            gp.wait_ge(da_d1, 2 * tt + 1)
                            gp.dma_start(d_x[tt % 2][:],
                                         x[(tt + 2) * 128:(tt + 3) * 128, :]
                                         ).then_inc(da_xld, 16)
                    gp.wait_ge(da_st, 32 * NT)

        estk = contextlib.ExitStack()
        with estk:
            e_x1T = estk.enter_context(
                nc.sbuf_tensor("e_x1T", [128, ND * L], BF16,
                               is_transpose_dest=True))
            e_w = [estk.enter_context(
                nc.sbuf_tensor(f"e_w{i}", [128, ND * 512], BF16,
                               is_transpose_dest=True)) for i in range(2)]
            e_s1 = estk.enter_context(nc.sbuf_tensor("e_s1", [128, L], BF16))
            e_h = [estk.enter_context(nc.sbuf_tensor(f"e_h{i}", [128, L], BF16))
                   for i in range(2)]
            e_acc = estk.enter_context(nc.sbuf_tensor("e_acc", [128, L], F32))
            e_ssq = estk.enter_context(nc.sbuf_tensor("e_ssq", [1, 512], F32))
            cstk7 = contextlib.ExitStack()

            ea_xld = sem("ea_xld")
            psC = cstk7.enter_context(nc.psum_tensor("psC", [128, 4096], F32))
            ea_wld = sem("ea_wld")
            ea_pe = sem("ea_pe")
            ea_sl = sem("ea_sl")
            ea_dh = sem("ea_dh")
            ea_da = sem("ea_da")
            ea_st = sem("ea_st")

            with nc.Block() as block:
                @block.sync
                def _(sy):
                    for d in range(ND):
                        sy.dma_start_transpose(
                            e_x1T[:, d * L:(d + 1) * L],
                            x1b[:, d * 128:(d + 1) * 128]).then_inc(ea_xld, 16)
                    for grp in range(32):
                        if grp >= 2:
                            sy.wait_ge(ea_pe, 32 * (grp - 1))
                        for d in range(ND):
                            sy.dma_start_transpose(
                                e_w[grp % 2][:, d * 512:d * 512 + 256],
                                wguq[grp * 256:(grp + 1) * 256,
                                     d * 128:(d + 1) * 128]).then_inc(ea_wld, 16)
                            sy.dma_start_transpose(
                                e_w[grp % 2][:, d * 512 + 256:(d + 1) * 512],
                                wguq[HF + grp * 256:HF + (grp + 1) * 256,
                                     d * 128:(d + 1) * 128]).then_inc(ea_wld, 16)

                @block.tensor
                def _(te):
                    te.wait_ge(ea_xld, 16 * ND)
                    for pair in range(NPAIR):
                        grp, pi = pair // 2, pair % 2
                        te.wait_ge(ea_wld, 16 * 32 * (grp + 1))
                        for kind in range(2):
                            for d in range(ND):
                                for tc in range(8):
                                    if d == 0:
                                        if kind == 0 and pair > 0:
                                            te.wait_ge(ea_dh,
                                                       8 * (pair - 1) + tc + 1)
                                        if kind == 1:
                                            te.wait_ge(ea_sl,
                                                       8 * pair + tc + 1)
                                    col = d * 512 + kind * 256 + pi * 128
                                    mm = te.matmul(
                                        psC[:, tc * 512:(tc + 1) * 512],
                                        e_w[grp % 2][:, col:col + 128],
                                        e_x1T[:, d * L + tc * 512:
                                              d * L + tc * 512 + 512],
                                        start=(d == 0), stop=(d == ND - 1))
                                    if d == ND - 1:
                                        mm.then_inc(ea_pe, 1)

                @block.scalar
                def _(sc):
                    for pair in range(NPAIR):
                        if pair >= 1:
                            sc.wait_ge(ea_da, pair)
                        for tc in range(8):
                            sc.wait_ge(ea_pe, pair * 16 + tc + 1)
                            sc.activation(e_s1[:, tc * 512:(tc + 1) * 512],
                                          psC[:, tc * 512:(tc + 1) * 512],
                                          AF.Silu).then_inc(ea_sl, 1)

                @block.vector
                def _(v):
                    v.memset(e_acc[:], 0.0)
                    for pair in range(NPAIR):
                        if pair >= 2:
                            v.wait_ge(ea_st, 16 * (pair - 1))
                        for tc in range(8):
                            v.wait_ge(ea_pe, pair * 16 + 8 + tc + 1)
                            v.tensor_tensor(
                                e_h[pair % 2][:, tc * 512:(tc + 1) * 512],
                                e_s1[:, tc * 512:(tc + 1) * 512],
                                psC[:, tc * 512:(tc + 1) * 512],
                                ALU.mult).then_inc(ea_dh, 1)
                        v.tensor_tensor(e_s1[:], e_h[pair % 2][:],
                                        e_h[pair % 2][:], ALU.mult)
                        v.tensor_tensor(e_acc[:], e_acc[:], e_s1[:],
                                        ALU.add).then_inc(ea_da, 1)

                @block.gpsimd
                def _(gp):
                    for pair in range(NPAIR):
                        gp.wait_ge(ea_dh, 8 * (pair + 1))
                        gp.dma_start(
                            hsp[pair, :, :, :].rearrange("t h j -> h t j"),
                            e_h[pair % 2][:].rearrange("p (t j) -> p t j",
                                                       t=NT, j=128)
                        ).then_inc(ea_st, 16)
                    gp.wait_ge(ea_st, 16 * NPAIR)

            cstk7.close()
            with (nc.psum_tensor("psR", [1, 512], F32) as psR,
                  nc.semaphore("er_pe") as er_pe,
                  nc.semaphore("er_ev") as er_ev,
                  nc.semaphore("er_st") as er_st):
                with nc.Block() as block:
                    @block.tensor
                    def _(te):
                        for oc in range(8):
                            if oc >= 1:
                                te.wait_ge(er_ev, oc)
                            te.matmul(psR[:], ones1sb[:],
                                      e_acc[:, oc * 512:(oc + 1) * 512],
                                      start=True, stop=True).then_inc(er_pe, 1)

                    @block.scalar
                    def _(sc):
                        for oc in range(8):
                            sc.wait_ge(er_pe, oc + 1)
                            if oc >= 1:
                                sc.wait_ge(er_st, 16 * oc)
                            sc.copy(e_ssq[0:1, :], psR[:]).then_inc(er_ev, 1)

                    @block.gpsimd
                    def _(gp):
                        for oc in range(8):
                            gp.wait_ge(er_ev, oc + 1)
                            gp.dma_start(
                                ssqd[oc * 512:(oc + 1) * 512].unsqueeze(0),
                                e_ssq[0:1, :]).then_inc(er_st, 16)
                        gp.wait_ge(er_st, 16 * 8)

        fstk = contextlib.ExitStack()
        with fstk:
            f_wp = fstk.enter_context(
                nc.sbuf_tensor("f_wp", [128, 32 * 2048], BF16,
                               is_transpose_dest=True))
            f_h = [fstk.enter_context(
                nc.sbuf_tensor(f"f_h{i}", [128, 32 * 128], BF16)) for i in range(2)]
            f_pr = [fstk.enter_context(nc.sbuf_tensor(f"f_pr{i}", [128, D], F32))
                    for i in range(2)]
            f_x1 = [fstk.enter_context(nc.sbuf_tensor(f"f_x1{i}", [128, D], F32))
                    for i in range(2)]
            f_o = fstk.enter_context(nc.sbuf_tensor("f_o", [128, D], F32))
            f_junk = fstk.enter_context(nc.sbuf_tensor("f_junk", [128, D], BF16))
            f_out = [fstk.enter_context(nc.sbuf_tensor(f"f_out{i}", [128, D], F32))
                     for i in range(2)]
            f_inv = fstk.enter_context(nc.sbuf_tensor("f_inv", [128, NT], F32))
            f_sq = fstk.enter_context(nc.sbuf_tensor("f_sq", [128, NT], F32))
            f_st = fstk.enter_context(nc.sbuf_tensor("f_st", [128, 8], F32))
            psD = fstk.enter_context(nc.psum_tensor("psD", [128, 4096], F32))

            for nm in ("f0_wld", "f0_hld", "f0_pe", "f0_ev", "f0_st",
                       "f1_wld", "f1_hld", "f1_pe", "f1_xld", "f1_ivld",
                       "f1_d1", "f1_a1", "f1_d2", "f1_a2", "f1_dn", "f1_iv",
                       "f1_st"):
                sem(nm)

            with nc.Block() as block:
                @block.sync
                def _(sy):
                    for half in range(2):
                        s_wld = sems[f"f{half}_wld"]
                        s_hld = sems[f"f{half}_hld"]
                        s_pe = sems[f"f{half}_pe"]
                        if half == 1:
                            for tt in range(NT):
                                sy.dma_start(
                                    f_inv[:, tt:tt + 1],
                                    ssqd[tt * 128:(tt + 1) * 128].unsqueeze(1)
                                ).then_inc(sems["f1_ivld"], 16)
                            sy.wait_ge(sems["f0_pe"], 4 * NT)
                        for ht in range(32):
                            sy.dma_start_transpose(
                                f_wp[:, ht * 2048:(ht + 1) * 2048],
                                wprq[:, (half * 32 + ht) * 128:
                                     (half * 32 + ht + 1) * 128]
                            ).then_inc(s_wld, 16)
                        for tt in range(NT):
                            if tt >= 2:
                                sy.wait_ge(s_pe, 4 * (tt - 1))
                            sy.dma_start(
                                f_h[tt % 2][:].rearrange("p (b t) -> p b t",
                                                         b=32, t=128),
                                hsp[half * 32:(half + 1) * 32, tt, :, :].rearrange(
                                    "b p t -> p b t")
                            ).then_inc(s_hld, 16)

                @block.tensor
                def _(te):
                    for half in range(2):
                        s_wld = sems[f"f{half}_wld"]
                        s_hld = sems[f"f{half}_hld"]
                        s_pe = sems[f"f{half}_pe"]
                        if half == 1:
                            te.wait_ge(sems["f0_ev"], 4 * NT)
                        te.wait_ge(s_wld, 16 * 32)
                        for tt in range(NT):
                            te.wait_ge(s_hld, 16 * (tt + 1))
                            if tt >= 2:
                                if half == 0:
                                    te.wait_ge(sems["f0_ev"], 4 * (tt - 1))
                                else:
                                    te.wait_ge(sems["f1_d1"], 2 * (tt - 1))
                            for ht in range(32):
                                for oc in range(4):
                                    bank = (tt % 2) * 4 + oc
                                    mm = te.matmul(
                                        psD[:, bank * 512:(bank + 1) * 512],
                                        f_h[tt % 2][:, ht * 128:(ht + 1) * 128],
                                        f_wp[:, ht * 2048 + oc * 512:
                                             ht * 2048 + oc * 512 + 512],
                                        start=(ht == 0), stop=(ht == 31))
                                    if ht == 31:
                                        mm.then_inc(s_pe, 1)

                @block.gpsimd
                def _(gp):
                    for tt in range(NT):
                        gp.wait_ge(sems["f0_ev"], 4 * (tt + 1))
                        gp.dma_start(prt[tt * 128:(tt + 1) * 128, :],
                                     f_pr[tt % 2][:]).then_inc(sems["f0_st"], 16)
                    gp.wait_ge(sems["f0_st"], 16 * NT)
                    for tt in range(NT):
                        if tt >= 2:
                            gp.wait_ge(sems["f1_d1"], 2 * (tt - 1))
                        gp.dma_start(f_pr[tt % 2][:],
                                     prt[tt * 128:(tt + 1) * 128, :]
                                     ).then_inc(sems["f1_xld"], 16)
                        gp.dma_start(f_x1[tt % 2][:],
                                     x1f[tt * 128:(tt + 1) * 128, :]
                                     ).then_inc(sems["f1_xld"], 16)
                        gp.wait_ge(sems["f1_dn"], tt + 1)
                        gp.dma_start(out[tt * 128:(tt + 1) * 128, :],
                                     f_out[tt % 2][:]).then_inc(sems["f1_st"], 16)
                    gp.wait_ge(sems["f1_st"], 16 * NT)

                @block.scalar
                def _(sc):
                    for tt in range(NT):
                        if tt >= 2:
                            sc.wait_ge(sems["f0_st"], 16 * (tt - 1))
                        for oc in range(4):
                            sc.wait_ge(sems["f0_pe"], 4 * tt + oc + 1)
                            bank = (tt % 2) * 4 + oc
                            sc.copy(f_pr[tt % 2][:, oc * 512:(oc + 1) * 512],
                                    psD[:, bank * 512:(bank + 1) * 512]
                                    ).then_inc(sems["f0_ev"], 1)
                    sc.wait_ge(sems["f1_iv"], 1)
                    sc.activation(f_sq[:], f_inv[:],
                                  AF.Sqrt).then_inc(sems["f1_a1"], 1)
                    for tt in range(NT):
                        sc.wait_ge(sems["f1_d1"], 2 * tt + 2)
                        sc.activation(f_junk[:], f_o[:], AF.Square,
                                      accum_out=f_st[:, 4:5]
                                      ).then_inc(sems["f1_a1"], 1)
                        sc.wait_ge(sems["f1_d2"], tt + 1)
                        sc.activation(f_st[:, 5:6], f_st[:, 2:3],
                                      AF.Sqrt).then_inc(sems["f1_a2"], 1)

                @block.vector
                def _(v):
                    v.wait_ge(sems["f1_ivld"], 16 * NT)
                    vsync(v, v.tensor_single_scalar(f_inv[:], f_inv[:],
                                                    1.0 / HF, ALU.mult))
                    v.tensor_single_scalar(f_inv[:], f_inv[:], EPS_RMS,
                                           ALU.add).then_inc(sems["f1_iv"], 1)
                    v.wait_ge(sems["f1_a1"], 1)
                    vsync(v, v.reciprocal(f_inv[:], f_sq[:]))
                    for tt in range(NT):
                        v.wait_ge(sems["f1_pe"], 4 * (tt + 1))
                        v.wait_ge(sems["f1_xld"], 32 * (tt + 1))
                        bank0 = (tt % 2) * 4
                        v.tensor_tensor(
                            f_o[:], psD[:, bank0 * 512:(bank0 + 4) * 512],
                            f_pr[tt % 2][:], ALU.add)
                        v.scalar_tensor_tensor(
                            f_o[:], f_o[:], f_inv[:, tt:tt + 1],
                            f_x1[tt % 2][:], ALU.mult,
                            ALU.add).then_inc(sems["f1_d1"], 1)
                        vsync(v, v.reduce_sum(f_st[:, 1:2], f_o[:], AX.X))
                        vsync(v, v.tensor_single_scalar(
                            f_st[:, 0:1], f_st[:, 1:2], 1.0 / D, ALU.mult))
                        v.tensor_scalar(f_o[:], f_o[:], f_st[:, 0:1],
                                        None, ALU.subtract).then_inc(sems["f1_d1"], 1)
                        v.wait_ge(sems["f1_a1"], tt + 2)
                        vsync(v, v.tensor_single_scalar(
                            f_st[:, 2:3], f_st[:, 4:5], 1.0 / D, ALU.mult))
                        v.tensor_single_scalar(f_st[:, 2:3], f_st[:, 2:3],
                                               EPS_LN,
                                               ALU.add).then_inc(sems["f1_d2"], 1)
                        v.wait_ge(sems["f1_a2"], tt + 1)
                        vsync(v, v.reciprocal(f_st[:, 3:4], f_st[:, 5:6]))
                        if tt >= 2:
                            v.wait_ge(sems["f1_st"], 16 * (tt - 1))
                        v.tensor_scalar(f_out[tt % 2][:], f_o[:],
                                        f_st[:, 3:4], None, ALU.mult
                                        ).then_inc(sems["f1_dn"], 1)

    return nc


def _consts():
    ident = np.eye(128, dtype=np.float32).astype(ml_dtypes.bfloat16)
    jj, ii = np.meshgrid(np.arange(128), np.arange(128), indexing="ij")
    mask = (jj <= ii).astype(np.uint8)
    ones1 = np.ones((128, 1), dtype=np.float32)
    return ident, mask, ones1


def kernel(L0_attn_x, L0_attn_Wq, L0_attn_Wk, L0_attn_Wv, L0_attn_Wg,
           L0_attn_Wo, L0_attn_gamma, L0_attn_beta, L0_ffn_w_gate_up,
           L0_ffn_w_proj):
    ident, mask, ones1 = _consts()
    nc = build()
    xs = np.asarray(L0_attn_x, dtype=np.float32)
    shared = dict(
        wq=np.ascontiguousarray(L0_attn_Wq, dtype=np.float32),
        wk=np.ascontiguousarray(L0_attn_Wk, dtype=np.float32),
        wv=np.ascontiguousarray(L0_attn_Wv, dtype=np.float32),
        wg=np.ascontiguousarray(L0_attn_Wg, dtype=np.float32),
        wo=np.ascontiguousarray(L0_attn_Wo, dtype=np.float32),
        gamma=np.ascontiguousarray(L0_attn_gamma, dtype=np.float32),
        beta=np.ascontiguousarray(L0_attn_beta, dtype=np.float32),
        wgu=np.ascontiguousarray(L0_ffn_w_gate_up, dtype=np.float32),
        wpr=np.ascontiguousarray(L0_ffn_w_proj, dtype=np.float32),
        identc=ident, maskc=mask, ones1c=ones1,
    )
    in_maps = [dict(x=np.ascontiguousarray(xs[i]), **shared) for i in range(8)]
    res = run_bass_kernel_spmd(nc, in_maps, core_ids=list(range(8)))
    return np.stack([res.results[i]["out"] for i in range(8)], axis=0)


# version 09
# speedup vs baseline: 1.0165x; 1.0165x over previous
import contextlib
import numpy as np
import ml_dtypes
import concourse.bass as bass
import concourse.mybir as mybir
from concourse.bass_utils import run_bass_kernel_spmd

F32 = mybir.dt.float32
BF16 = mybir.dt.bfloat16
AF = mybir.ActivationFunctionType
ALU = mybir.AluOpType
AX = mybir.AxisListType

D = 2048
L = 4096
H = 16
ND = 16
NT = 32
HF = 8192
CH = 128
SBW = 32
NCH = 32
QSC = float(128.0 ** -0.5)
EPS_LN = 1e-5
EPS_RMS = float(np.finfo(np.float32).eps)
VW = [SBW * (s + 1) for s in range(4)]
VOFF = [0, 1024, 3072, 6144]
KTVW = 10240
NPAIR = 64


def build(debug=False):
    nc = bass.Bass("TRN2", target_bir_lowering=False)

    def dram(name, shape, dt, kind="Internal"):
        if debug and kind == "Internal":
            kind = "ExternalOutput"
        return nc.dram_tensor(name, shape, dt, kind=kind).ap()

    x = dram("x", [L, D], F32, "ExternalInput")
    wq = dram("wq", [D, D], F32, "ExternalInput")
    wk = dram("wk", [D, D], F32, "ExternalInput")
    wv = dram("wv", [D, D], F32, "ExternalInput")
    wg = dram("wg", [D, D], F32, "ExternalInput")
    wo = dram("wo", [D, D], F32, "ExternalInput")
    gamma = dram("gamma", [D], F32, "ExternalInput")
    beta = dram("beta", [D], F32, "ExternalInput")
    wgu = dram("wgu", [2 * HF, D], F32, "ExternalInput")
    wpr = dram("wpr", [D, HF], F32, "ExternalInput")
    identc = dram("identc", [128, 128], BF16, "ExternalInput")
    maskc = dram("maskc", [128, 128], mybir.dt.uint8, "ExternalInput")
    ones1c = dram("ones1c", [128, 1], F32, "ExternalInput")
    out = dram("out", [L, D], F32, "ExternalOutput")

    xbf = dram("xbf", [L, D], BF16)
    wqb = dram("wqb", [D, D], BF16)
    wkb = dram("wkb", [D, D], BF16)
    wvb = dram("wvb", [D, D], BF16)
    wgb = dram("wgb", [D, D], BF16)
    wob = dram("wob", [D, D], BF16)
    wguq = dram("wguq", [2 * HF, D], BF16)
    wprq = dram("wprq", [D, HF], BF16)
    qt = dram("qt", [D, L], BF16)
    kt = dram("kt", [D, L], BF16)
    gsp = dram("gsp", [D, L], F32)
    vtok = dram("vtok", [H, L, 128], BF16)
    ysp = dram("ysp", [H, L, 128], BF16)
    lny = dram("lny", [L, D], BF16)
    x1f = dram("x1f", [L, D], F32)
    x1b = dram("x1b", [L, D], BF16)
    hsp = dram("hsp", [NPAIR, NT, 128, 128], BF16)
    prt = dram("prt", [L, D], F32)
    ssqd = dram("ssqd", [L], F32)

    g = contextlib.ExitStack()
    with g:
        sems = {}

        def sem(name):
            sems[name] = g.enter_context(nc.semaphore(name))
            return sems[name]

        vself = g.enter_context(nc.semaphore("vself"))
        vctr = [0]

        def vsync(v, instr):
            vctr[0] += 1
            instr.then_inc(vself, 1)
            v.wait_ge(vself, vctr[0])
            return instr

        identsb = g.enter_context(nc.sbuf_tensor("identsb", [128, 128], BF16))
        masksb = g.enter_context(nc.sbuf_tensor("masksb", [128, 128],
                                                 mybir.dt.uint8))
        ones1sb = g.enter_context(nc.sbuf_tensor("ones1sb", [128, 1], F32))

        s0 = sem("s0")
        sc0 = sem("sc0")

        pstk = contextlib.ExitStack()
        with pstk:
            xts = pstk.enter_context(
                nc.sbuf_tensor("xts", [128, ND * L], BF16, is_transpose_dest=True))
            wgT = [pstk.enter_context(
                nc.sbuf_tensor(f"wgT{i}", [128, ND * 512], BF16,
                               is_transpose_dest=True)) for i in range(2)]
            psA = pstk.enter_context(nc.psum_tensor("psA", [128, 4096], F32))
            sxts = sem("sxts")

            gstk = contextlib.ExitStack()
            with gstk:
                gstg = [gstk.enter_context(
                    nc.sbuf_tensor(f"gstg{i}", [128, L], F32)) for i in range(2)]
                gab = [gstk.enter_context(
                    nc.sbuf_tensor(f"gab{i}", [128, 512], F32)) for i in range(2)]
                geb = [gstk.enter_context(
                    nc.sbuf_tensor(f"geb{i}", [128, 512], F32)) for i in range(2)]
                glb = [gstk.enter_context(
                    nc.sbuf_tensor(f"glb{i}", [128, 512], F32)) for i in range(2)]

                g_ld = sem("g_ld")
                g_pe = sem("g_pe")
                g_ab = sem("g_ab")
                g_ln = sem("g_ln")
                g_sb = sem("g_sb")
                g_st = sem("g_st")

                with nc.Block() as block:
                    @block.sync
                    def _(sy):
                        sy.dma_start(identsb[:], identc).then_inc(sc0, 16)
                        sy.dma_start(masksb[:], maskc).then_inc(sc0, 16)
                        sy.dma_start(ones1sb[:], ones1c).then_inc(sc0, 16)
                        for grp in range(2):
                            sy.wait_ge(s0, 16)
                            for d in range(ND):
                                sy.dma_start_transpose(
                                    wgT[grp % 2][:, d * 512:(d + 1) * 512],
                                    wgb[grp * 512:(grp + 1) * 512,
                                        d * 128:(d + 1) * 128]).then_inc(g_ld, 16)
                        for d in range(ND):
                            sy.wait_ge(s0, 16 + 16 * (d + 1))
                            sy.dma_start_transpose(
                                xts[:, d * L:(d + 1) * L],
                                xbf[:, d * 128:(d + 1) * 128]).then_inc(sxts, 16)
                        for grp in range(2, 4):
                            sy.wait_ge(g_pe, 32 * (grp - 1))
                            for d in range(ND):
                                sy.dma_start_transpose(
                                    wgT[grp % 2][:, d * 512:(d + 1) * 512],
                                    wgb[grp * 512:(grp + 1) * 512,
                                        d * 128:(d + 1) * 128]).then_inc(g_ld, 16)
                        sy.wait_ge(sc0, 48)

                    @block.tensor
                    def _(te):
                        te.wait_ge(sxts, 16 * ND)
                        for dout in range(ND):
                            grp, di = dout // 4, dout % 4
                            if di == 0:
                                te.wait_ge(g_ld, 16 * ND * (grp + 1))
                            if dout > 0:
                                te.wait_ge(g_ab, 8 * dout)
                                te.wait_ge(g_sb, 8 * dout)
                            for d in range(ND):
                                for tc in range(8):
                                    mm = te.matmul(
                                        psA[:, tc * 512:(tc + 1) * 512],
                                        wgT[grp % 2][:, d * 512 + di * 128:
                                                     d * 512 + di * 128 + 128],
                                        xts[:, d * L + tc * 512:
                                            d * L + tc * 512 + 512],
                                        start=(d == 0), stop=(d == ND - 1))
                                    if d == ND - 1:
                                        mm.then_inc(g_pe, 1)

                    @block.scalar
                    def _(sc):
                        for dout in range(ND):
                            for tc in range(8):
                                i = dout * 8 + tc
                                sc.wait_ge(g_pe, i + 1)
                                ps = psA[:, tc * 512:(tc + 1) * 512]
                                sc.activation(gab[tc % 2][:], ps,
                                              AF.Abs).then_inc(g_ab, 1)
                                sc.activation(geb[tc % 2][:], gab[tc % 2][:],
                                              AF.Exp, scale=-1.0)
                                if i >= 2:
                                    sc.wait_ge(g_sb, i - 1)
                                sc.activation(glb[tc % 2][:], geb[tc % 2][:],
                                              AF.Ln, bias=1.0).then_inc(g_ln, 1)

                    @block.vector
                    def _(v):
                        for dout in range(ND):
                            if dout >= 2:
                                v.wait_ge(g_st, 16 * (dout - 1))
                            for tc in range(8):
                                i = dout * 8 + tc
                                v.wait_ge(g_pe, i + 1)
                                v.wait_ge(g_ln, i + 1)
                                ps = psA[:, tc * 512:(tc + 1) * 512]
                                v.scalar_tensor_tensor(
                                    gstg[dout % 2][:, tc * 512:(tc + 1) * 512],
                                    ps, 0.0, glb[tc % 2][:],
                                    ALU.min, ALU.subtract).then_inc(g_sb, 1)

                    @block.gpsimd
                    def _(gp):
                        gp.dma_start(wgb, wg).then_inc(s0, 16)
                        for d in range(ND):
                            gp.dma_start(xbf[:, d * 128:(d + 1) * 128],
                                         x[:, d * 128:(d + 1) * 128]
                                         ).then_inc(s0, 16)
                        casts = {0: (wqb, wq), 4: (wkb, wk),
                                 8: (wvb, wv), 12: (wob, wo)}
                        for dout in range(ND):
                            if dout in casts:
                                dst_, src_ = casts[dout]
                                gp.dma_start(dst_, src_).then_inc(s0, 16)
                            gp.wait_ge(g_sb, 8 * (dout + 1))
                            gp.dma_start(gsp[dout * 128:(dout + 1) * 128, :],
                                         gstg[dout % 2][:]).then_inc(g_st, 16)
                        gp.wait_ge(g_st, 16 * ND)

            qstk = contextlib.ExitStack()
            with qstk:
                ostg = [qstk.enter_context(
                    nc.sbuf_tensor(f"ostg{i}", [128, L], BF16)) for i in range(2)]
                qwt = [qstk.enter_context(
                    nc.sbuf_tensor(f"qwt{i}", [128, D], BF16)) for i in range(2)]
                qwh = [qstk.enter_context(
                    nc.sbuf_tensor(f"qwh{i}", [128, D], BF16)) for i in range(2)]
                qred = qstk.enter_context(nc.sbuf_tensor("qred", [128, 32], F32))
                qsbf = [qstk.enter_context(nc.sbuf_tensor(f"qsbf{i}", [128, 32],
                                                          BF16)) for i in range(2)]
                qrq = qstk.enter_context(nc.sbuf_tensor("qrq", [128, 32], F32))
                qrr = qstk.enter_context(nc.sbuf_tensor("qrr", [128, D], BF16))
                qlt = qstk.enter_context(nc.sbuf_tensor("qlt", [128, D], BF16))
                qgt = qstk.enter_context(nc.sbuf_tensor("qgt", [128, D], BF16))

                CFGS = [("q", wqb, qt, QSC, 0), ("k", wkb, kt, None, 64)]
                QS_ = {}
                for tag, _, _, _, _ in CFGS:
                    for snm in ("ld", "pe", "ev", "st", "qld", "qdv", "qst",
                                "qpw"):
                        QS_[f"{tag}_{snm}"] = sem(f"{tag}_{snm}")

                with nc.Block() as block:
                    @block.sync
                    def _(sy):
                        for pi_, (tag, wsrc, dst, scale, tile0) in enumerate(CFGS):
                            sld, qdv, qst = (QS_[f"{tag}_ld"], QS_[f"{tag}_qdv"],
                                             QS_[f"{tag}_qst"])
                            qpw = QS_[f"{tag}_qpw"]
                            sy.wait_ge(s0, 16 * (18 + pi_))
                            for grp in range(4):
                                if pi_ == 1 and grp < 2:
                                    sy.wait_ge(QS_["q_pe"], 96 + 32 * grp)
                                if grp >= 2:
                                    sy.wait_ge(QS_[f"{tag}_pe"], 32 * (grp - 1))
                                for d in range(ND):
                                    sy.dma_start_transpose(
                                        wgT[grp % 2][:, d * 512:(d + 1) * 512],
                                        wsrc[grp * 512:(grp + 1) * 512,
                                             d * 128:(d + 1) * 128]
                                    ).then_inc(sld, 16)
                                lo = [0, 8, 16, 40][grp]
                                hi = [8, 16, 40, 64][grp]
                                for i in range(lo, hi):
                                    sy.wait_ge(qpw, i + 1)
                                    sy.dma_start(
                                        wguq[(tile0 + i) * 128:
                                             (tile0 + i + 1) * 128, :],
                                        qwh[i % 2][:]).then_inc(qst, 16)
                            sy.wait_ge(qst, 16 * 64)

                    @block.tensor
                    def _(te):
                        te.wait_ge(sxts, 16 * ND)
                        for pi_, (tag, wsrc, dst, scale, tile0) in enumerate(CFGS):
                            sld, spe, sev = (QS_[f"{tag}_ld"], QS_[f"{tag}_pe"],
                                             QS_[f"{tag}_ev"])
                            if pi_ == 1:
                                te.wait_ge(QS_["q_ev"], 128)
                            for dout in range(ND):
                                grp, di = dout // 4, dout % 4
                                if di == 0:
                                    te.wait_ge(sld, 16 * ND * (grp + 1))
                                if dout > 0:
                                    te.wait_ge(sev, 8 * dout)
                                for d in range(ND):
                                    for tc in range(8):
                                        mm = te.matmul(
                                            psA[:, tc * 512:(tc + 1) * 512],
                                            wgT[grp % 2][:, d * 512 + di * 128:
                                                         d * 512 + di * 128 + 128],
                                            xts[:, d * L + tc * 512:
                                                d * L + tc * 512 + 512],
                                            start=(d == 0), stop=(d == ND - 1))
                                        if d == ND - 1:
                                            mm.then_inc(spe, 1)

                    @block.scalar
                    def _(sc):
                        for pi_, (tag, wsrc, dst, scale, tile0) in enumerate(CFGS):
                            spe, sev, sst = (QS_[f"{tag}_pe"], QS_[f"{tag}_ev"],
                                             QS_[f"{tag}_st"])
                            if pi_ == 1:
                                sc.wait_ge(QS_["q_st"], 16 * 16)
                            for dout in range(ND):
                                if dout >= 2:
                                    sc.wait_ge(sst, 16 * (dout - 1))
                                for tc in range(8):
                                    sc.wait_ge(spe, dout * 8 + tc + 1)
                                    ps = psA[:, tc * 512:(tc + 1) * 512]
                                    dsl = ostg[dout % 2][:, tc * 512:(tc + 1) * 512]
                                    if scale is None:
                                        sc.copy(dsl, ps).then_inc(sev, 1)
                                    else:
                                        sc.mul(dsl, ps, scale).then_inc(sev, 1)

                    @block.gpsimd
                    def _(gp):
                        for pi_, (tag, wsrc, dst, scale, tile0) in enumerate(CFGS):
                            sev, sst, qld = (QS_[f"{tag}_ev"], QS_[f"{tag}_st"],
                                             QS_[f"{tag}_qld"])
                            qdv, qst, qpw = (QS_[f"{tag}_qdv"], QS_[f"{tag}_qst"],
                                             QS_[f"{tag}_qpw"])

                            def poolw(i):
                                gp.wait_ge(qdv, i + 1)
                                if i >= 2:
                                    gp.wait_ge(qst, 16 * (i - 1))
                                gp.tensor_tensor(
                                    qwh[i % 2][:].rearrange("p (a j) -> p a j",
                                                            a=32, j=64),
                                    qgt[:].rearrange("p (a j) -> p a j",
                                                     a=32, j=64),
                                    qsbf[i % 2][:].unsqueeze(2).broadcast_to(
                                        [128, 32, 64]),
                                    ALU.mult).then_inc(qpw, 1)

                            if pi_ == 1:
                                gp.wait_ge(QS_["q_qdv"], 64)
                            for dout in range(ND):
                                gp.wait_ge(sev, 8 * (dout + 1))
                                gp.dma_start(dst[dout * 128:(dout + 1) * 128, :],
                                             ostg[dout % 2][:]).then_inc(sst, 16)
                                for i in range(4 * dout, 4 * dout + 4):
                                    if i >= 2:
                                        gp.wait_ge(qdv, i - 1)
                                    gp.dma_start(
                                        qwt[i % 2][:],
                                        wgu[(tile0 + i) * 128:
                                            (tile0 + i + 1) * 128, :]
                                    ).then_inc(qld, 16)
                                    if i >= 2:
                                        poolw(i - 2)
                            poolw(62)
                            poolw(63)
                            gp.wait_ge(sst, 16 * ND)

                    @block.vector
                    def _(v):
                        for pi_, (tag, wsrc, dst, scale, tile0) in enumerate(CFGS):
                            qld, qdv = QS_[f"{tag}_qld"], QS_[f"{tag}_qdv"]
                            qpw = QS_[f"{tag}_qpw"]
                            if pi_ == 1:
                                v.wait_ge(QS_["q_qst"], 16 * 64)
                            for i in range(64):
                                v.wait_ge(qld, 16 * (i + 1))
                                w3 = qwt[i % 2][:].rearrange("p (a j) -> p a j",
                                                             a=32, j=64)
                                vsync(v, v.reduce_sum(
                                    qred[:].rearrange("p (a o) -> p a o",
                                                      a=32, o=1),
                                    w3, AX.X, apply_absolute_value=True))
                                vsync(v, v.tensor_single_scalar(
                                    qred[:], qred[:], 1.0 / 64, ALU.mult))
                                vsync(v, v.tensor_single_scalar(
                                    qsbf[i % 2][:], qred[:], 1e-8, ALU.max))
                                vsync(v, v.reciprocal(qrq[:], qsbf[i % 2][:]))
                                v.tensor_tensor(
                                    qrr[:].rearrange("p (a j) -> p a j",
                                                     a=32, j=64),
                                    w3,
                                    qrq[:].unsqueeze(2).broadcast_to(
                                        [128, 32, 64]), ALU.mult)
                                v.tensor_single_scalar(qlt[:], qrr[:], -0.5,
                                                       ALU.is_lt)
                                if i >= 1:
                                    v.wait_ge(qpw, i)
                                v.tensor_single_scalar(qgt[:], qrr[:], 0.5,
                                                       ALU.is_gt)
                                v.tensor_sub(qgt[:], qgt[:],
                                             qlt[:]).then_inc(qdv, 1)

            vstk = contextlib.ExitStack()
            with vstk:
                vstg = [vstk.enter_context(
                    nc.sbuf_tensor(f"vstg{i}", [128, D], BF16)) for i in range(2)]
                wvT2 = [vstk.enter_context(
                    nc.sbuf_tensor(f"wvT2{i}", [128, ND * 512], BF16,
                                   is_transpose_dest=True)) for i in range(2)]
                sv_ld = sem("sv_ld")
                sv_pe = sem("sv_pe")
                sv_ev = sem("sv_ev")
                sv_st = sem("sv_st")

                with nc.Block() as block:
                    @block.sync
                    def _(sy):
                        sy.wait_ge(s0, 16 * 20)
                        wdst = [wgT[0], wgT[1], wvT2[0], wvT2[1]]
                        for vc in range(4):
                            for d in range(ND):
                                sy.dma_start_transpose(
                                    wdst[vc][:, d * 512:(d + 1) * 512],
                                    wvb[vc * 512:(vc + 1) * 512,
                                        d * 128:(d + 1) * 128]).then_inc(sv_ld, 16)

                    @block.tensor
                    def _(te):
                        te.wait_ge(sv_ld, 16 * 4 * ND)
                        wsrc = [wgT[0], wgT[1], wvT2[0], wvT2[1]]
                        for tt in range(NT):
                            if tt >= 2:
                                te.wait_ge(sv_ev, 4 * (tt - 1))
                            for d in range(ND):
                                for vc in range(4):
                                    bank = (tt % 2) * 4 + vc
                                    mm = te.matmul(
                                        psA[:, bank * 512:(bank + 1) * 512],
                                        xts[:, d * L + tt * 128:
                                            d * L + tt * 128 + 128],
                                        wsrc[vc][:, d * 512:(d + 1) * 512],
                                        start=(d == 0), stop=(d == ND - 1))
                                    if d == ND - 1:
                                        mm.then_inc(sv_pe, 1)

                    @block.scalar
                    def _(sc):
                        for tt in range(NT):
                            if tt >= 2:
                                sc.wait_ge(sv_st, 16 * (tt - 1))
                            for vc in range(4):
                                sc.wait_ge(sv_pe, tt * 4 + vc + 1)
                                bank = (tt % 2) * 4 + vc
                                sc.copy(vstg[tt % 2][:, vc * 512:(vc + 1) * 512],
                                        psA[:, bank * 512:(bank + 1) * 512]
                                        ).then_inc(sv_ev, 1)

                    @block.gpsimd
                    def _(gp):
                        vdst = vtok.rearrange("h t v -> t h v")
                        for tt in range(NT):
                            gp.wait_ge(sv_ev, 4 * (tt + 1))
                            gp.dma_start(
                                vdst[tt * 128:(tt + 1) * 128, :, :],
                                vstg[tt % 2][:].rearrange("p (h v) -> p h v",
                                                          h=H, v=128)
                            ).then_inc(sv_st, 16)
                        gp.wait_ge(sv_st, 16 * NT)

        astk = contextlib.ExitStack()
        with astk:
            a_g = astk.enter_context(nc.sbuf_tensor("a_g", [128, L], F32))
            a_q = [astk.enter_context(nc.sbuf_tensor(f"a_q{i}", [128, L], BF16))
                   for i in range(2)]
            a_k = [astk.enter_context(nc.sbuf_tensor(f"a_k{i}", [128, L], BF16))
                   for i in range(2)]
            a_gg = astk.enter_context(nc.sbuf_tensor("a_gg", [128, L + 1], F32))
            p_ar = [astk.enter_context(nc.sbuf_tensor(f"p_ar{i}", [128, 2048], F32))
                    for i in range(2)]
            a_e = [astk.enter_context(nc.sbuf_tensor(f"a_e{i}", [128, 2048], BF16))
                   for i in range(2)]
            a_qt = [astk.enter_context(nc.sbuf_tensor(f"a_qt{i}", [128, L], BF16))
                    for i in range(2)]
            a_qh = [astk.enter_context(nc.sbuf_tensor(f"a_qh{i}", [128, L], BF16))
                    for i in range(2)]
            a_kh = [astk.enter_context(nc.sbuf_tensor(f"a_kh{i}", [128, L], BF16))
                    for i in range(2)]
            a_kv = [astk.enter_context(nc.sbuf_tensor(f"a_kv{i}", [128, KTVW], BF16))
                    for i in range(2)]
            a_dc = [astk.enter_context(nc.sbuf_tensor(f"a_dc{i}", [128, NCH], F32))
                    for i in range(2)]
            a_da = astk.enter_context(nc.sbuf_tensor("a_da", [128, NCH], F32))
            b_v = [astk.enter_context(nc.sbuf_tensor(f"b_v{i}", [128, L], BF16))
                   for i in range(2)]
            b_s = [astk.enter_context(nc.sbuf_tensor(f"b_s{i}", [128, 128], BF16))
                   for i in range(2)]
            b_a = [astk.enter_context(nc.sbuf_tensor(f"b_a{i}", [128, 128], BF16))
                   for i in range(2)]
            b_z = astk.enter_context(nc.sbuf_tensor("b_z", [128, 128], BF16))
            b_kt = [astk.enter_context(nc.sbuf_tensor(f"b_kt{i}", [128, 128], BF16))
                    for i in range(2)]
            b_y = astk.enter_context(nc.sbuf_tensor("b_y", [128, 1024], BF16))
            pAm = [astk.enter_context(nc.psum_tensor(f"pAm{i}", [128, 128], F32))
                   for i in range(2)]
            pY = [astk.enter_context(nc.psum_tensor(f"pY{i}", [128, 128], F32))
                  for i in range(2)]
            pS = [astk.enter_context(nc.psum_tensor(f"pS{i}", [128, 128], F32))
                  for i in range(2)]
            pT = [astk.enter_context(nc.psum_tensor(f"pT{i}", [128, 128], BF16))
                  for i in range(2)]

            sgl = sem("sgl")
            sql = sem("sql")
            sa_sc = sem("sa_sc")
            sa_pg = sem("sa_pg")
            sa_ag = sem("sa_ag")
            sa_ex = sem("sa_ex")
            sa_mu = sem("sa_mu")
            bl = sem("bl")
            bpa = sem("bpa")
            bpt = sem("bpt")
            bpy = sem("bpy")
            bps = sem("bps")
            bda = sem("bda")
            bds = sem("bds")
            bkt = sem("bkt")
            bay = sem("bay")
            byst = sem("byst")
            NK = 32 * H

            def SD(K):
                return K + (K // 32) + 2

            VSPEC = [(0, 0, 32), (1, 0, 32), (2, 0, 16), (2, 1, 16),
                     (3, 0, 16), (3, 1, 16)]

            def stage_arg(v, h, k, ab=None):
                if k < 6:
                    which, hf = k // 2, k % 2
                    base = hf * 2048
                    ob = ab[k % 2]
                    i0 = a_gg[:, 1 + base:1 + base + 2048]
                    if which == 0:
                        ref = a_gg[:, base:base + 2048].rearrange(
                            "p (c j) -> p c j", c=16, j=128)[:, :, 0:1]
                        return v.tensor_tensor(
                            ob[:].rearrange("p (c j) -> p c j", c=16, j=128),
                            i0.rearrange("p (c j) -> p c j", c=16, j=128),
                            ref.broadcast_to([128, 16, 128]), ALU.subtract)
                    if which == 1:
                        ref = a_gg[:, base:base + 2048].rearrange(
                            "p (b j) -> p b j", b=64, j=32)[:, :, 0:1]
                        return v.tensor_tensor(
                            ob[:].rearrange("p (b j) -> p b j", b=64, j=32),
                            i0.rearrange("p (b j) -> p b j", b=64, j=32),
                            ref.broadcast_to([128, 64, 32]), ALU.subtract)
                    ref = a_gg[:, 1 + base:1 + base + 2048].rearrange(
                        "p (c j) -> p c j", c=16, j=128)[:, :, 127:128]
                    return v.tensor_tensor(
                        ob[:].rearrange("p (c j) -> p c j", c=16, j=128),
                        ref.broadcast_to([128, 16, 128]),
                        i0.rearrange("p (c j) -> p c j", c=16, j=128),
                        ALU.subtract)
                if k < 12:
                    s, part, ncb = VSPEC[k - 6]
                    w = VW[s]
                    cs = part * 16
                    gj = a_gg[:, 1 + cs * 128:1 + (cs + ncb) * 128].rearrange(
                        "p (c j) -> p c j", c=ncb, j=128)[:, :, 0:w]
                    ref = a_gg[:, 0:L].rearrange(
                        "p (c j) -> p c j", c=NCH, j=128)[:, cs:cs + ncb,
                                                          SBW * s:SBW * s + 1]
                    ob = ab[k % 2]
                    return v.tensor_tensor(
                        ob[:, 0:ncb * w].rearrange(
                            "p (c j) -> p c j", c=ncb, j=w),
                        ref.broadcast_to([128, ncb, w]), gj, ALU.subtract)
                ends = a_gg[:, 1:1 + L].rearrange(
                    "p (c j) -> p c j", c=NCH, j=128)[:, :, 127:128]
                starts = a_gg[:, 0:L].rearrange(
                    "p (c j) -> p c j", c=NCH, j=128)[:, :, 0:1]
                return v.tensor_tensor(
                    a_da[:].rearrange("p (c o) -> p c o", c=NCH, o=1),
                    ends, starts, ALU.subtract)

            def stage_w(k):
                if k < 6:
                    return 2048
                if k < 12:
                    s, part, ncb = VSPEC[k - 6]
                    return ncb * VW[s]
                return NCH

            def stage_mul(v, h, k):
                hb = h % 2
                if k < 6:
                    which, hf = k // 2, k % 2
                    srcqk = a_q[hb] if which < 2 else a_k[hb]
                    dst = (a_qt, a_qh, a_kh)[which][hb]
                    return v.tensor_tensor(dst[:, hf * 2048:(hf + 1) * 2048],
                                           srcqk[:, hf * 2048:(hf + 1) * 2048],
                                           a_e[k % 2][:], ALU.mult)
                s, part, ncb = VSPEC[k - 6]
                w = VW[s]
                cs = part * 16
                ks = a_k[hb][:, cs * 128:(cs + ncb) * 128].rearrange(
                    "p (c j) -> p c j", c=ncb, j=128)[:, :, 0:w]
                return v.tensor_tensor(
                    a_kv[hb][:, VOFF[s] + cs * w:
                             VOFF[s] + (cs + ncb) * w].rearrange(
                        "p (c j) -> p c j", c=ncb, j=w),
                    ks,
                    a_e[k % 2][:, 0:ncb * w].rearrange("p (c j) -> p c j",
                                                       c=ncb, j=w),
                    ALU.mult)


            def prep_dve_slots(h):
                def guard(v, h=h):
                    if h >= 1:
                        v.wait_ge(bpy, 32 * (h - 1))
                        v.wait_ge(bps, 32 * (h - 1))
                        v.wait_ge(sa_pg, 12 * h)
                    v.wait_ge(sgl, 16 * (h + 1))
                    v.memset(a_gg[:, 0:1], 0.0)

                def scanq(qtr):
                    def op(v, h=h, qtr=qtr):
                        ini = 0.0 if qtr == 0 else a_gg[:, qtr * 1024:qtr * 1024 + 1]
                        vsync(v, v.tensor_tensor_scan(
                            a_gg[:, 1 + qtr * 1024:1 + (qtr + 1) * 1024],
                            a_g[:, qtr * 1024:(qtr + 1) * 1024],
                            a_g[:, qtr * 1024:(qtr + 1) * 1024],
                            ini, ALU.add, ALU.bypass))
                        if qtr == 3:
                            v.engine_nop().then_inc(sa_sc, 1)
                    return op

                def arg(k):
                    def op(v, h=h, k=k):
                        if h >= 1:
                            v.wait_ge(sa_ex, 13 * h)
                        stage_arg(v, h, k).then_inc(sa_ag, 1)
                    return op

                def mul(k):
                    def op(v, h=h, k=k):
                        if k == 0:
                            v.wait_ge(sql, 32 * (h + 1))
                        v.wait_ge(sa_ex, 13 * h + k + 1)
                        stage_mul(v, h, k).then_inc(sa_mu, 1)
                    return op

                slots = {1: [guard], 2: [scanq(0), scanq(1)],
                         3: [scanq(2), scanq(3)], 4: [arg(12)]}
                for k in range(12):
                    slots.setdefault(6 + 2 * k, []).append(mul(k))
                return slots

            def prep_act_ops(h):
                ops = []
                for k in range(13):
                    def ex(sc, h=h, k=k):
                        if k < 12:
                            sc.wait_ge(sa_pg, 12 * h + k + 1)
                        else:
                            sc.wait_ge(sa_ag, h + 1)
                        if k >= 2 and k < 12:
                            sc.wait_ge(sa_mu, 12 * h + k - 1)
                        elif k == 0 and h >= 1:
                            sc.wait_ge(sa_mu, 12 * h)
                        if k < 12:
                            w = stage_w(k)
                            sc.activation(a_e[k % 2][:, 0:w],
                                          p_ar[k % 2][:, 0:w],
                                          AF.Exp).then_inc(sa_ex, 1)
                        else:
                            if h >= 2:
                                sc.wait_ge(bds, 33 * (h - 1))
                            sc.activation(a_dc[h % 2][:], a_da[:],
                                          AF.Exp).then_inc(sa_ex, 1)
                    ops.append(ex)
                return ops

            def emit_A(te, K):
                h, c = K // 32, K % 32
                te.wait_ge(bda, K + 1)
                for s in range(4):
                    w = VW[s]
                    mm = te.matmul(
                        pAm[K % 2][0:w, SBW * s:SBW * s + SBW],
                        a_kv[h % 2][:, VOFF[s] + c * w:VOFF[s] + (c + 1) * w],
                        a_qh[h % 2][:, c * 128 + SBW * s:c * 128 + SBW * s + SBW],
                        start=True, stop=True)
                    if s == 3:
                        mm.then_inc(bpa, 1)

            with nc.Block() as block:
                @block.sync
                def _(sy):
                    for h in range(H):
                        if h >= 1:
                            sy.wait_ge(sa_sc, h)
                        sy.dma_start(a_g[:],
                                     gsp[h * 128:(h + 1) * 128, :]).then_inc(sgl, 16)
                        if h >= 2:
                            sy.wait_ge(sa_mu, 12 * (h - 1))
                        sy.dma_start(a_q[h % 2][:],
                                     qt[h * 128:(h + 1) * 128, :]).then_inc(sql, 16)
                        sy.dma_start(a_k[h % 2][:],
                                     kt[h * 128:(h + 1) * 128, :]).then_inc(sql, 16)
                        if h >= 2:
                            sy.wait_ge(bpy, 32 * (h - 1))
                            sy.wait_ge(bps, 32 * (h - 1))
                        sy.dma_start(
                            b_v[h % 2][:].rearrange("p (c v) -> p c v",
                                                    c=NCH, v=128),
                            vtok[h, :, :].rearrange("(c p) v -> p c v",
                                                    c=NCH, p=128)).then_inc(bl, 16)

                @block.gpsimd
                def _(gp):
                    for h in range(H):
                        gp.wait_ge(sa_sc, h + 1)
                        for k in range(12):
                            if k >= 2:
                                gp.wait_ge(sa_ex, 13 * h + k - 1)
                            elif h >= 1:
                                gp.wait_ge(sa_ex, 13 * (h - 1) + 11 + k)
                            stage_arg(gp, h, k, p_ar).then_inc(sa_pg, 1)

                @block.vector
                def _(v):
                    v.memset(b_z[:], 0.0)
                    v.memset(pAm[0][:], 0.0).then_inc(bda, 1)
                    v.memset(pAm[1][:], 0.0).then_inc(bda, 1)
                    for c2 in range(1, 30):
                        for op in prep_dve_slots(0).get(c2, []):
                            op(v)
                    v.wait_ge(bpa, 1)
                    v.select(b_a[0][:], masksb[:], pAm[0][:],
                             b_z[:]).then_inc(bda, 1)
                    for h in range(H):
                        pq = prep_dve_slots(h + 1) if h + 1 < H else {}
                        v.wait_ge(bpy, 32 * h)
                        v.memset(b_s[0][:], 0.0).then_inc(bds, 1)
                        for c in range(NCH):
                            K = 32 * h + c
                            v.wait_ge(bps, K + 1)
                            v.wait_ge(bpy, K)
                            v.scalar_tensor_tensor(
                                b_s[(c + 1) % 2][:], b_s[c % 2][:],
                                a_dc[h % 2][:, c:c + 1], pS[K % 2][:],
                                ALU.mult, ALU.add).then_inc(bds, 1)
                            v.wait_ge(bds, SD(K))
                            if K + 1 < NK:
                                v.wait_ge(bpa, K + 2)
                                v.wait_ge(bpy, K)
                                v.select(b_a[(K + 1) % 2][:], masksb[:],
                                         pAm[(K + 1) % 2][:],
                                         b_z[:]).then_inc(bda, 1)
                            for op in pq.get(c, []):
                                op(v)

                @block.scalar
                def _(sc):
                    for op in prep_act_ops(0):
                        op(sc)
                    for h in range(H):
                        aq = prep_act_ops(h + 1) if h + 1 < H else []
                        aq = ({5 + 2 * k: [aq[k]] for k in range(12)}
                              | {28: [aq[12]]}) if aq else {}
                        for c in range(NCH):
                            K = 32 * h + c
                            sc.wait_ge(bpt, K + 1)
                            if K >= 2:
                                sc.wait_ge(bps, K - 1)
                            sc.copy(b_kt[K % 2][:], pT[K % 2][:]).then_inc(bkt, 1)
                            sc.wait_ge(bpy, K + 1)
                            if K // 4 >= 1:
                                sc.wait_ge(byst, 16 * (K // 4 - 1))
                            sc.copy(b_y[:, (K % 8) * 128:(K % 8 + 1) * 128],
                                    pY[K % 2][:]).then_inc(bay, 1)
                            if K % 4 == 3:
                                hb = ((K // 4) % 2) * 512
                                sc.dma_start(
                                    ysp[h, (c - 3) * 128:(c + 1) * 128, :].rearrange(
                                        "(b p) v -> p b v", b=4, p=128),
                                    b_y[:, hb:hb + 512].rearrange(
                                        "p (b v) -> p b v", b=4, v=128)
                                ).then_inc(byst, 16)
                            for op in aq.get(c, []):
                                op(sc)
                    sc.wait_ge(byst, 16 * (NK // 4))

                @block.tensor
                def _(te):
                    te.wait_ge(sa_mu, 12)
                    te.wait_ge(sa_ex, 13)
                    te.wait_ge(bl, 16)
                    emit_A(te, 0)
                    te.wait_ge(bkt, 0)
                    te.transpose(pT[0][:], a_kh[0][:, 0:128],
                                 identsb[:]).then_inc(bpt, 1)
                    for K in range(NK):
                        h, c = K // 32, K % 32
                        if c == 0 and h > 0:
                            te.wait_ge(sa_mu, 12 * (h + 1))
                            te.wait_ge(sa_ex, 13 * (h + 1))
                            te.wait_ge(bl, 16 * (h + 1))
                        te.wait_ge(bkt, K + 1)
                        if K >= 2:
                            te.wait_ge(bds, SD(K - 2))
                        te.matmul(pS[K % 2][:], b_kt[K % 2][:],
                                  b_v[h % 2][:, c * 128:(c + 1) * 128],
                                  start=True, stop=True).then_inc(bps, 1)
                        if K + 1 < NK:
                            K1 = K + 1
                            h1, c1 = K1 // 32, K1 % 32
                            if c1 == 0:
                                te.wait_ge(sa_mu, 12 * (h1 + 1))
                                te.wait_ge(sa_ex, 13 * (h1 + 1))
                                te.wait_ge(bl, 16 * (h1 + 1))
                            emit_A(te, K1)
                            if K1 >= 2:
                                te.wait_ge(bkt, K1 - 1)
                            te.transpose(pT[K1 % 2][:],
                                         a_kh[h1 % 2][:, c1 * 128:(c1 + 1) * 128],
                                         identsb[:]).then_inc(bpt, 1)
                        te.wait_ge(bda, K + 3)
                        te.wait_ge(bds, SD(K - 1) if c > 0 else 33 * h + 1)
                        if K >= 1:
                            te.wait_ge(bay, K - 1)
                        te.matmul(pY[K % 2][:], b_a[K % 2][:],
                                  b_v[h % 2][:, c * 128:(c + 1) * 128],
                                  start=True, stop=False)
                        te.matmul(pY[K % 2][:], a_qt[h % 2][:, c * 128:(c + 1) * 128],
                                  b_s[c % 2][:], start=False,
                                  stop=True).then_inc(bpy, 1)
        castk = contextlib.ExitStack()
        with castk:
            c_y = [castk.enter_context(nc.sbuf_tensor(f"c_y{i}", [128, D], BF16))
                   for i in range(2)]
            c_yc = castk.enter_context(nc.sbuf_tensor("c_yc", [128, D], F32))
            c_t1 = castk.enter_context(nc.sbuf_tensor("c_t1", [128, D], F32))
            c_junk = castk.enter_context(nc.sbuf_tensor("c_junk", [128, D], BF16))
            c_ln = [castk.enter_context(nc.sbuf_tensor(f"c_ln{i}", [128, D], BF16))
                    for i in range(2)]
            c_st = castk.enter_context(nc.sbuf_tensor("c_st", [128, 8], F32))
            c_gam = castk.enter_context(nc.sbuf_tensor("c_gam", [128, D], F32))
            c_bet = castk.enter_context(nc.sbuf_tensor("c_bet", [128, D], F32))
            pwt2 = castk.enter_context(nc.sbuf_tensor("pwt2", [128, HF], BF16))
            pwh2 = castk.enter_context(nc.sbuf_tensor("pwh2", [128, HF], BF16))
            pred2 = castk.enter_context(nc.sbuf_tensor("pred2", [128, 128], F32))
            psbf2 = castk.enter_context(nc.sbuf_tensor("psbf2", [128, 128], BF16))
            prq2 = castk.enter_context(nc.sbuf_tensor("prq2", [128, 128], F32))
            prr2 = castk.enter_context(nc.sbuf_tensor("prr2", [128, HF], BF16))
            plt2 = castk.enter_context(nc.sbuf_tensor("plt2", [128, HF], BF16))
            pgt2 = castk.enter_context(nc.sbuf_tensor("pgt2", [128, HF], BF16))

            ca_ld = sem("ca_ld")
            ca_d1 = sem("ca_d1")
            ca_a1 = sem("ca_a1")
            ca_d2 = sem("ca_d2")
            ca_a2 = sem("ca_a2")
            ca_dn = sem("ca_dn")
            ca_st = sem("ca_st")
            cq_ld = sem("cq_ld")
            cq_dv = sem("cq_dv")
            cq_pl = sem("cq_pl")
            cq_st = sem("cq_st")

            with nc.Block() as block:
                @block.sync
                def _(sy):
                    sy.dma_start(c_gam[:],
                                 gamma.unsqueeze(0).broadcast_to([128, D])
                                 ).then_inc(ca_ld, 16)
                    sy.dma_start(c_bet[:],
                                 beta.unsqueeze(0).broadcast_to([128, D])
                                 ).then_inc(ca_ld, 16)
                    ysrc = ysp.rearrange("h t v -> t h v")
                    for tt in range(NT):
                        if tt >= 2:
                            sy.wait_ge(ca_dn, tt - 1)
                        sy.dma_start(
                            c_y[tt % 2][:].rearrange("p (h v) -> p h v", h=H, v=128),
                            ysrc[tt * 128:(tt + 1) * 128, :, :]).then_inc(ca_ld, 16)
                        if tt % 4 == 0 and 1 <= tt // 4:
                            i = tt // 4 - 1
                            sy.wait_ge(cq_pl, i + 1)
                            sy.dma_start(wprq[(8 + i) * 128:(9 + i) * 128, :],
                                         pwh2[:]).then_inc(cq_st, 16)
                    sy.wait_ge(cq_pl, 8)
                    sy.dma_start(wprq[15 * 128:16 * 128, :],
                                 pwh2[:]).then_inc(cq_st, 16)
                    sy.wait_ge(cq_st, 16 * 8)

                @block.vector
                def _(v):
                    for tt in range(NT):
                        v.wait_ge(ca_ld, 16 * (tt + 3))
                        vsync(v, v.reduce_sum(c_st[:, 1:2], c_y[tt % 2][:], AX.X))
                        vsync(v, v.tensor_single_scalar(c_st[:, 0:1],
                                                        c_st[:, 1:2],
                                                        1.0 / D, ALU.mult))
                        v.tensor_scalar(c_yc[:], c_y[tt % 2][:], c_st[:, 0:1],
                                        None, ALU.subtract).then_inc(ca_d1, 1)
                        v.wait_ge(ca_a1, tt + 1)
                        vsync(v, v.tensor_single_scalar(c_st[:, 2:3],
                                                        c_st[:, 4:5],
                                                        1.0 / D, ALU.mult))
                        v.tensor_single_scalar(c_st[:, 2:3], c_st[:, 2:3],
                                               EPS_LN, ALU.add).then_inc(ca_d2, 1)
                        v.wait_ge(ca_a2, tt + 1)
                        vsync(v, v.reciprocal(c_st[:, 3:4], c_st[:, 5:6]))
                        v.scalar_tensor_tensor(c_t1[:], c_yc[:], c_st[:, 3:4],
                                               c_gam[:], ALU.mult, ALU.mult)
                        if tt >= 2:
                            v.wait_ge(ca_st, 16 * (tt - 1))
                        v.tensor_tensor(c_ln[tt % 2][:], c_t1[:], c_bet[:],
                                        ALU.add).then_inc(ca_dn, 1)
                        if tt % 4 == 3 and tt // 4 < 8:
                            i = tt // 4
                            v.wait_ge(cq_ld, 16 * (i + 1))
                            w3 = pwt2[:].rearrange("p (a j) -> p a j",
                                                   a=128, j=64)
                            vsync(v, v.reduce_sum(
                                pred2[:].rearrange("p (a o) -> p a o", a=128, o=1),
                                w3, AX.X, apply_absolute_value=True))
                            vsync(v, v.tensor_single_scalar(pred2[:], pred2[:],
                                                            1.0 / 64, ALU.mult))
                            if i >= 1:
                                v.wait_ge(cq_pl, i)
                            vsync(v, v.tensor_single_scalar(psbf2[:], pred2[:],
                                                            1e-8, ALU.max))
                            vsync(v, v.reciprocal(prq2[:], psbf2[:]))
                            v.tensor_tensor(
                                prr2[:].rearrange("p (a j) -> p a j", a=128, j=64),
                                w3,
                                prq2[:].unsqueeze(2).broadcast_to([128, 128, 64]),
                                ALU.mult)
                            v.tensor_single_scalar(plt2[:], prr2[:], -0.5,
                                                   ALU.is_lt)
                            v.tensor_single_scalar(pgt2[:], prr2[:], 0.5,
                                                   ALU.is_gt)
                            v.tensor_sub(pgt2[:], pgt2[:],
                                         plt2[:]).then_inc(cq_dv, 1)

                @block.scalar
                def _(sc):
                    for tt in range(NT):
                        sc.wait_ge(ca_d1, tt + 1)
                        sc.activation(c_junk[:], c_yc[:], AF.Square,
                                      accum_out=c_st[:, 4:5]).then_inc(ca_a1, 1)
                        sc.wait_ge(ca_d2, tt + 1)
                        sc.activation(c_st[:, 5:6], c_st[:, 2:3],
                                      AF.Sqrt).then_inc(ca_a2, 1)

                @block.gpsimd
                def _(gp):
                    for tt in range(NT):
                        gp.wait_ge(ca_dn, tt + 1)
                        gp.dma_start(lny[tt * 128:(tt + 1) * 128, :],
                                     c_ln[tt % 2][:]).then_inc(ca_st, 16)
                        if tt % 4 == 0 and tt // 4 < 8:
                            i = tt // 4
                            if i >= 1:
                                gp.wait_ge(cq_dv, i)
                            gp.dma_start(pwt2[:],
                                         wpr[(8 + i) * 128:(9 + i) * 128, :]
                                         ).then_inc(cq_ld, 16)
                        if tt % 4 == 3 and tt // 4 < 8:
                            i = tt // 4
                            gp.wait_ge(cq_dv, i + 1)
                            if i >= 1:
                                gp.wait_ge(cq_st, 16 * i)
                            gp.tensor_tensor(
                                pwh2[:].rearrange("p (a j) -> p a j", a=128, j=64),
                                pgt2[:].rearrange("p (a j) -> p a j", a=128, j=64),
                                psbf2[:].unsqueeze(2).broadcast_to([128, 128, 64]),
                                ALU.mult).then_inc(cq_pl, 1)
                    gp.wait_ge(ca_st, 16 * NT)

        dstk = contextlib.ExitStack()
        with dstk:
            d_lnyT = dstk.enter_context(
                nc.sbuf_tensor("d_lnyT", [128, ND * 2048], BF16,
                               is_transpose_dest=True))
            d_woT = dstk.enter_context(
                nc.sbuf_tensor("d_woT", [128, ND * 2048], BF16,
                               is_transpose_dest=True))
            d_x = [dstk.enter_context(nc.sbuf_tensor(f"d_x{i}", [128, D], F32))
                   for i in range(2)]
            d_rc = dstk.enter_context(nc.sbuf_tensor("d_rc", [128, D], F32))
            d_junk = dstk.enter_context(nc.sbuf_tensor("d_junk", [128, D], BF16))
            d_x1 = [dstk.enter_context(nc.sbuf_tensor(f"d_x1{i}", [128, D], F32))
                    for i in range(2)]
            d_x1b = [dstk.enter_context(nc.sbuf_tensor(f"d_x1b{i}", [128, D], BF16))
                     for i in range(2)]
            d_st = dstk.enter_context(nc.sbuf_tensor("d_st", [128, 8], F32))
            qw2t = dstk.enter_context(nc.sbuf_tensor("qw2t", [128, 2048], BF16))
            qw2h = [dstk.enter_context(nc.sbuf_tensor(f"qw2h{i}", [128, 2048], BF16))
                    for i in range(2)]
            q2red = dstk.enter_context(nc.sbuf_tensor("q2red", [128, 32], F32))
            q2sbf = dstk.enter_context(nc.sbuf_tensor("q2sbf", [128, 32], BF16))
            q2rq = dstk.enter_context(nc.sbuf_tensor("q2rq", [128, 32], F32))
            q2rr = dstk.enter_context(nc.sbuf_tensor("q2rr", [128, 2048], BF16))
            q2lt = dstk.enter_context(nc.sbuf_tensor("q2lt", [128, 2048], BF16))
            q2gt = dstk.enter_context(nc.sbuf_tensor("q2gt", [128, 2048], BF16))
            psB = dstk.enter_context(nc.psum_tensor("psB", [128, 4096], F32))

            da_wld = sem("da_wld")
            da_yld = sem("da_yld")
            da_xld = sem("da_xld")
            da_pe = sem("da_pe")
            da_d1 = sem("da_d1")
            da_a1 = sem("da_a1")
            da_d2 = sem("da_d2")
            da_a2 = sem("da_a2")
            da_dnv = sem("da_dnv")
            da_dna = sem("da_dna")
            da_st = sem("da_st")
            q2_ld = sem("q2_ld")
            q2_dv = sem("q2_dv")
            q2_pl = sem("q2_pl")
            q2_st = sem("q2_st")

            with nc.Block() as block:
                @block.sync
                def _(sy):
                    def q2store(u):
                        i, qq = u // 4, u % 4
                        sy.wait_ge(q2_pl, u + 1)
                        sy.dma_start(
                            wprq[i * 128:(i + 1) * 128,
                                 qq * 2048:(qq + 1) * 2048],
                            qw2h[u % 2][:]).then_inc(q2_st, 16)

                    sy.wait_ge(s0, 16 * 21)
                    for d in range(ND):
                        sy.dma_start_transpose(
                            d_woT[:, d * 2048:(d + 1) * 2048],
                            wob[:, d * 128:(d + 1) * 128]).then_inc(da_wld, 16)
                    for d in range(ND):
                        sy.dma_start_transpose(
                            d_lnyT[:, d * 2048:(d + 1) * 2048],
                            lny[0:2048, d * 128:(d + 1) * 128]).then_inc(da_yld, 16)
                    for u in range(14):
                        q2store(u)
                    sy.wait_ge(da_pe, 64)
                    for d in range(ND):
                        sy.dma_start_transpose(
                            d_lnyT[:, d * 2048:(d + 1) * 2048],
                            lny[2048:4096, d * 128:(d + 1) * 128]).then_inc(da_yld, 16)
                    for u in range(14, 32):
                        q2store(u)
                    sy.wait_ge(q2_st, 16 * 32)

                @block.tensor
                def _(te):
                    te.wait_ge(da_wld, 16 * ND)
                    for tt in range(NT):
                        th, ti = tt // 16, tt % 16
                        te.wait_ge(da_yld, 16 * ND * (th + 1))
                        if tt >= 2:
                            te.wait_ge(da_d1, 2 * (tt - 1))
                        for d in range(ND):
                            for oc in range(4):
                                bank = (tt % 2) * 4 + oc
                                mm = te.matmul(
                                    psB[:, bank * 512:(bank + 1) * 512],
                                    d_lnyT[:, d * 2048 + ti * 128:
                                           d * 2048 + ti * 128 + 128],
                                    d_woT[:, d * 2048 + oc * 512:
                                          d * 2048 + oc * 512 + 512],
                                    start=(d == 0), stop=(d == ND - 1))
                                if d == ND - 1:
                                    mm.then_inc(da_pe, 1)

                @block.vector
                def _(v):
                    for tt in range(NT):
                        v.wait_ge(da_pe, 4 * (tt + 1))
                        v.wait_ge(da_xld, 16 * (tt + 1))
                        bank0 = (tt % 2) * 4
                        v.tensor_tensor(d_rc[:],
                                        psB[:, bank0 * 512:(bank0 + 4) * 512],
                                        d_x[tt % 2][:], ALU.add).then_inc(da_d1, 1)
                        vsync(v, v.reduce_sum(d_st[:, 1:2], d_rc[:], AX.X))
                        vsync(v, v.tensor_single_scalar(d_st[:, 0:1],
                                                        d_st[:, 1:2],
                                                        1.0 / D, ALU.mult))
                        v.tensor_scalar(d_rc[:], d_rc[:], d_st[:, 0:1], None,
                                        ALU.subtract).then_inc(da_d1, 1)
                        v.wait_ge(da_a1, tt + 1)
                        vsync(v, v.tensor_single_scalar(d_st[:, 2:3],
                                                        d_st[:, 4:5],
                                                        1.0 / D, ALU.mult))
                        v.tensor_single_scalar(d_st[:, 2:3], d_st[:, 2:3],
                                               EPS_LN, ALU.add).then_inc(da_d2, 1)
                        v.wait_ge(da_a2, tt + 1)
                        vsync(v, v.reciprocal(d_st[:, 3:4], d_st[:, 5:6]))
                        if tt >= 2:
                            v.wait_ge(da_st, 32 * (tt - 1))
                        v.tensor_scalar(d_x1[tt % 2][:], d_rc[:], d_st[:, 3:4],
                                        None, ALU.mult).then_inc(da_dnv, 1)
                        u = tt
                        v.wait_ge(q2_ld, 16 * (u + 1))
                        w3 = qw2t[:].rearrange("p (a j) -> p a j", a=32, j=64)
                        vsync(v, v.reduce_sum(
                            q2red[:].rearrange("p (a o) -> p a o", a=32, o=1),
                            w3, AX.X, apply_absolute_value=True))
                        vsync(v, v.tensor_single_scalar(q2red[:], q2red[:],
                                                        1.0 / 64, ALU.mult))
                        if u >= 1:
                            v.wait_ge(q2_pl, u)
                        vsync(v, v.tensor_single_scalar(q2sbf[:], q2red[:],
                                                        1e-8, ALU.max))
                        vsync(v, v.reciprocal(q2rq[:], q2sbf[:]))
                        v.tensor_tensor(q2rr[:].rearrange("p (a j) -> p a j",
                                                          a=32, j=64), w3,
                                        q2rq[:].unsqueeze(2).broadcast_to(
                                            [128, 32, 64]), ALU.mult)
                        v.tensor_single_scalar(q2lt[:], q2rr[:], -0.5, ALU.is_lt)
                        v.tensor_single_scalar(q2gt[:], q2rr[:], 0.5, ALU.is_gt)
                        v.tensor_sub(q2gt[:], q2gt[:],
                                     q2lt[:]).then_inc(q2_dv, 1)

                @block.scalar
                def _(sc):
                    for tt in range(NT):
                        sc.wait_ge(da_d1, 2 * tt + 2)
                        sc.activation(d_junk[:], d_rc[:], AF.Square,
                                      accum_out=d_st[:, 4:5]).then_inc(da_a1, 1)
                        sc.wait_ge(da_d2, tt + 1)
                        sc.activation(d_st[:, 5:6], d_st[:, 2:3],
                                      AF.Sqrt).then_inc(da_a2, 1)
                        sc.wait_ge(da_dnv, tt + 1)
                        if tt >= 2:
                            sc.wait_ge(da_st, 32 * (tt - 1))
                        sc.copy(d_x1b[tt % 2][:],
                                d_x1[tt % 2][:]).then_inc(da_dna, 1)

                @block.gpsimd
                def _(gp):
                    gp.dma_start(d_x[0][:], x[0:128, :]).then_inc(da_xld, 16)
                    gp.dma_start(d_x[1][:], x[128:256, :]).then_inc(da_xld, 16)
                    gp.dma_start(qw2t[:],
                                 wpr[0:128, 0:2048]).then_inc(q2_ld, 16)
                    for tt in range(NT):
                        if tt + 1 < NT:
                            u = tt + 1
                            i, qq = u // 4, u % 4
                            gp.wait_ge(q2_dv, u)
                            gp.dma_start(
                                qw2t[:],
                                wpr[i * 128:(i + 1) * 128,
                                    qq * 2048:(qq + 1) * 2048]).then_inc(q2_ld, 16)
                        gp.wait_ge(q2_dv, tt + 1)
                        if tt >= 2:
                            gp.wait_ge(q2_st, 16 * (tt - 1))
                        gp.tensor_tensor(qw2h[tt % 2][:].rearrange(
                                             "p (a j) -> p a j", a=32, j=64),
                                         q2gt[:].rearrange("p (a j) -> p a j",
                                                           a=32, j=64),
                                         q2sbf[:].unsqueeze(2).broadcast_to(
                                             [128, 32, 64]),
                                         ALU.mult).then_inc(q2_pl, 1)
                        gp.wait_ge(da_dnv, tt + 1)
                        gp.wait_ge(da_dna, tt + 1)
                        gp.dma_start(x1f[tt * 128:(tt + 1) * 128, :],
                                     d_x1[tt % 2][:]).then_inc(da_st, 16)
                        gp.dma_start(x1b[tt * 128:(tt + 1) * 128, :],
                                     d_x1b[tt % 2][:]).then_inc(da_st, 16)
                        if tt + 2 < NT:
                            gp.wait_ge(da_d1, 2 * tt + 1)
                            gp.dma_start(d_x[tt % 2][:],
                                         x[(tt + 2) * 128:(tt + 3) * 128, :]
                                         ).then_inc(da_xld, 16)
                    gp.wait_ge(da_st, 32 * NT)

        estk = contextlib.ExitStack()
        with estk:
            e_x1T = estk.enter_context(
                nc.sbuf_tensor("e_x1T", [128, ND * L], BF16,
                               is_transpose_dest=True))
            e_w = [estk.enter_context(
                nc.sbuf_tensor(f"e_w{i}", [128, ND * 512], BF16,
                               is_transpose_dest=True)) for i in range(2)]
            e_s1 = estk.enter_context(nc.sbuf_tensor("e_s1", [128, L], BF16))
            e_h = [estk.enter_context(nc.sbuf_tensor(f"e_h{i}", [128, L], BF16))
                   for i in range(2)]
            e_acc = estk.enter_context(nc.sbuf_tensor("e_acc", [128, L], F32))
            e_ssq = estk.enter_context(nc.sbuf_tensor("e_ssq", [1, 512], F32))
            cstk7 = contextlib.ExitStack()

            ea_xld = sem("ea_xld")
            psC = cstk7.enter_context(nc.psum_tensor("psC", [128, 4096], F32))
            ea_wld = sem("ea_wld")
            ea_pe = sem("ea_pe")
            ea_sl = sem("ea_sl")
            ea_dh = sem("ea_dh")
            ea_da = sem("ea_da")
            ea_st = sem("ea_st")

            with nc.Block() as block:
                @block.sync
                def _(sy):
                    for d in range(ND):
                        sy.dma_start_transpose(
                            e_x1T[:, d * L:(d + 1) * L],
                            x1b[:, d * 128:(d + 1) * 128]).then_inc(ea_xld, 16)
                    for grp in range(32):
                        if grp >= 2:
                            sy.wait_ge(ea_pe, 32 * (grp - 1))
                        for d in range(ND):
                            sy.dma_start_transpose(
                                e_w[grp % 2][:, d * 512:d * 512 + 256],
                                wguq[grp * 256:(grp + 1) * 256,
                                     d * 128:(d + 1) * 128]).then_inc(ea_wld, 16)
                            sy.dma_start_transpose(
                                e_w[grp % 2][:, d * 512 + 256:(d + 1) * 512],
                                wguq[HF + grp * 256:HF + (grp + 1) * 256,
                                     d * 128:(d + 1) * 128]).then_inc(ea_wld, 16)

                @block.tensor
                def _(te):
                    te.wait_ge(ea_xld, 16 * ND)
                    for pair in range(NPAIR):
                        grp, pi = pair // 2, pair % 2
                        te.wait_ge(ea_wld, 16 * 32 * (grp + 1))
                        for kind in range(2):
                            for d in range(ND):
                                for tc in range(8):
                                    if d == 0:
                                        if kind == 0 and pair > 0:
                                            te.wait_ge(ea_dh,
                                                       8 * (pair - 1) + tc + 1)
                                        if kind == 1:
                                            te.wait_ge(ea_sl,
                                                       8 * pair + tc + 1)
                                    col = d * 512 + kind * 256 + pi * 128
                                    mm = te.matmul(
                                        psC[:, tc * 512:(tc + 1) * 512],
                                        e_w[grp % 2][:, col:col + 128],
                                        e_x1T[:, d * L + tc * 512:
                                              d * L + tc * 512 + 512],
                                        start=(d == 0), stop=(d == ND - 1))
                                    if d == ND - 1:
                                        mm.then_inc(ea_pe, 1)

                @block.scalar
                def _(sc):
                    for pair in range(NPAIR):
                        if pair >= 1:
                            sc.wait_ge(ea_da, pair)
                        for tc in range(8):
                            sc.wait_ge(ea_pe, pair * 16 + tc + 1)
                            sc.activation(e_s1[:, tc * 512:(tc + 1) * 512],
                                          psC[:, tc * 512:(tc + 1) * 512],
                                          AF.Silu).then_inc(ea_sl, 1)

                @block.vector
                def _(v):
                    v.memset(e_acc[:], 0.0)
                    for pair in range(NPAIR):
                        if pair >= 2:
                            v.wait_ge(ea_st, 16 * (pair - 1))
                        for tc in range(8):
                            v.wait_ge(ea_pe, pair * 16 + 8 + tc + 1)
                            v.tensor_tensor(
                                e_h[pair % 2][:, tc * 512:(tc + 1) * 512],
                                e_s1[:, tc * 512:(tc + 1) * 512],
                                psC[:, tc * 512:(tc + 1) * 512],
                                ALU.mult).then_inc(ea_dh, 1)
                        v.tensor_tensor(e_s1[:], e_h[pair % 2][:],
                                        e_h[pair % 2][:], ALU.mult)
                        v.tensor_tensor(e_acc[:], e_acc[:], e_s1[:],
                                        ALU.add).then_inc(ea_da, 1)

                @block.gpsimd
                def _(gp):
                    for pair in range(NPAIR):
                        gp.wait_ge(ea_dh, 8 * (pair + 1))
                        gp.dma_start(
                            hsp[pair, :, :, :].rearrange("t h j -> h t j"),
                            e_h[pair % 2][:].rearrange("p (t j) -> p t j",
                                                       t=NT, j=128)
                        ).then_inc(ea_st, 16)
                    gp.wait_ge(ea_st, 16 * NPAIR)

            cstk7.close()
            with (nc.psum_tensor("psR", [1, 512], F32) as psR,
                  nc.semaphore("er_pe") as er_pe,
                  nc.semaphore("er_ev") as er_ev,
                  nc.semaphore("er_st") as er_st):
                with nc.Block() as block:
                    @block.tensor
                    def _(te):
                        for oc in range(8):
                            if oc >= 1:
                                te.wait_ge(er_ev, oc)
                            te.matmul(psR[:], ones1sb[:],
                                      e_acc[:, oc * 512:(oc + 1) * 512],
                                      start=True, stop=True).then_inc(er_pe, 1)

                    @block.scalar
                    def _(sc):
                        for oc in range(8):
                            sc.wait_ge(er_pe, oc + 1)
                            if oc >= 1:
                                sc.wait_ge(er_st, 16 * oc)
                            sc.copy(e_ssq[0:1, :], psR[:]).then_inc(er_ev, 1)

                    @block.gpsimd
                    def _(gp):
                        for oc in range(8):
                            gp.wait_ge(er_ev, oc + 1)
                            gp.dma_start(
                                ssqd[oc * 512:(oc + 1) * 512].unsqueeze(0),
                                e_ssq[0:1, :]).then_inc(er_st, 16)
                        gp.wait_ge(er_st, 16 * 8)

        fstk = contextlib.ExitStack()
        with fstk:
            f_wp = fstk.enter_context(
                nc.sbuf_tensor("f_wp", [128, 32 * 2048], BF16,
                               is_transpose_dest=True))
            f_h = [fstk.enter_context(
                nc.sbuf_tensor(f"f_h{i}", [128, 32 * 128], BF16)) for i in range(2)]
            f_pr = [fstk.enter_context(nc.sbuf_tensor(f"f_pr{i}", [128, D], F32))
                    for i in range(2)]
            f_x1 = [fstk.enter_context(nc.sbuf_tensor(f"f_x1{i}", [128, D], F32))
                    for i in range(2)]
            f_o = fstk.enter_context(nc.sbuf_tensor("f_o", [128, D], F32))
            f_junk = fstk.enter_context(nc.sbuf_tensor("f_junk", [128, D], BF16))
            f_out = [fstk.enter_context(nc.sbuf_tensor(f"f_out{i}", [128, D], F32))
                     for i in range(2)]
            f_inv = fstk.enter_context(nc.sbuf_tensor("f_inv", [128, NT], F32))
            f_sq = fstk.enter_context(nc.sbuf_tensor("f_sq", [128, NT], F32))
            f_st = fstk.enter_context(nc.sbuf_tensor("f_st", [128, 8], F32))
            psD = fstk.enter_context(nc.psum_tensor("psD", [128, 4096], F32))

            for nm in ("f0_wld", "f0_hld", "f0_pe", "f0_ev", "f0_st",
                       "f1_wld", "f1_hld", "f1_pe", "f1_xld", "f1_ivld",
                       "f1_d1", "f1_a1", "f1_d2", "f1_a2", "f1_dn", "f1_iv",
                       "f1_st"):
                sem(nm)

            with nc.Block() as block:
                @block.sync
                def _(sy):
                    for half in range(2):
                        s_wld = sems[f"f{half}_wld"]
                        s_hld = sems[f"f{half}_hld"]
                        s_pe = sems[f"f{half}_pe"]
                        if half == 1:
                            for tt in range(NT):
                                sy.dma_start(
                                    f_inv[:, tt:tt + 1],
                                    ssqd[tt * 128:(tt + 1) * 128].unsqueeze(1)
                                ).then_inc(sems["f1_ivld"], 16)
                            sy.wait_ge(sems["f0_pe"], 4 * NT)
                        for ht in range(32):
                            sy.dma_start_transpose(
                                f_wp[:, ht * 2048:(ht + 1) * 2048],
                                wprq[:, (half * 32 + ht) * 128:
                                     (half * 32 + ht + 1) * 128]
                            ).then_inc(s_wld, 16)
                        for tt in range(NT):
                            if tt >= 2:
                                sy.wait_ge(s_pe, 4 * (tt - 1))
                            sy.dma_start(
                                f_h[tt % 2][:].rearrange("p (b t) -> p b t",
                                                         b=32, t=128),
                                hsp[half * 32:(half + 1) * 32, tt, :, :].rearrange(
                                    "b p t -> p b t")
                            ).then_inc(s_hld, 16)

                @block.tensor
                def _(te):
                    for half in range(2):
                        s_wld = sems[f"f{half}_wld"]
                        s_hld = sems[f"f{half}_hld"]
                        s_pe = sems[f"f{half}_pe"]
                        if half == 1:
                            te.wait_ge(sems["f0_ev"], 4 * NT)
                        te.wait_ge(s_wld, 16 * 32)
                        for tt in range(NT):
                            te.wait_ge(s_hld, 16 * (tt + 1))
                            if tt >= 2:
                                if half == 0:
                                    te.wait_ge(sems["f0_ev"], 4 * (tt - 1))
                                else:
                                    te.wait_ge(sems["f1_d1"], 2 * (tt - 1))
                            for ht in range(32):
                                for oc in range(4):
                                    bank = (tt % 2) * 4 + oc
                                    mm = te.matmul(
                                        psD[:, bank * 512:(bank + 1) * 512],
                                        f_h[tt % 2][:, ht * 128:(ht + 1) * 128],
                                        f_wp[:, ht * 2048 + oc * 512:
                                             ht * 2048 + oc * 512 + 512],
                                        start=(ht == 0), stop=(ht == 31))
                                    if ht == 31:
                                        mm.then_inc(s_pe, 1)

                @block.gpsimd
                def _(gp):
                    for tt in range(NT):
                        gp.wait_ge(sems["f0_ev"], 4 * (tt + 1))
                        gp.dma_start(prt[tt * 128:(tt + 1) * 128, :],
                                     f_pr[tt % 2][:]).then_inc(sems["f0_st"], 16)
                    gp.wait_ge(sems["f0_st"], 16 * NT)
                    for tt in range(NT):
                        if tt >= 2:
                            gp.wait_ge(sems["f1_d1"], 2 * (tt - 1))
                        gp.dma_start(f_pr[tt % 2][:],
                                     prt[tt * 128:(tt + 1) * 128, :]
                                     ).then_inc(sems["f1_xld"], 16)
                        gp.dma_start(f_x1[tt % 2][:],
                                     x1f[tt * 128:(tt + 1) * 128, :]
                                     ).then_inc(sems["f1_xld"], 16)
                        gp.wait_ge(sems["f1_dn"], tt + 1)
                        gp.dma_start(out[tt * 128:(tt + 1) * 128, :],
                                     f_out[tt % 2][:]).then_inc(sems["f1_st"], 16)
                    gp.wait_ge(sems["f1_st"], 16 * NT)

                @block.scalar
                def _(sc):
                    for tt in range(NT):
                        if tt >= 2:
                            sc.wait_ge(sems["f0_st"], 16 * (tt - 1))
                        for oc in range(4):
                            sc.wait_ge(sems["f0_pe"], 4 * tt + oc + 1)
                            bank = (tt % 2) * 4 + oc
                            sc.copy(f_pr[tt % 2][:, oc * 512:(oc + 1) * 512],
                                    psD[:, bank * 512:(bank + 1) * 512]
                                    ).then_inc(sems["f0_ev"], 1)
                    sc.wait_ge(sems["f1_iv"], 1)
                    sc.activation(f_sq[:], f_inv[:],
                                  AF.Sqrt).then_inc(sems["f1_a1"], 1)
                    for tt in range(NT):
                        sc.wait_ge(sems["f1_d1"], 2 * tt + 2)
                        sc.activation(f_junk[:], f_o[:], AF.Square,
                                      accum_out=f_st[:, 4:5]
                                      ).then_inc(sems["f1_a1"], 1)
                        sc.wait_ge(sems["f1_d2"], tt + 1)
                        sc.activation(f_st[:, 5:6], f_st[:, 2:3],
                                      AF.Sqrt).then_inc(sems["f1_a2"], 1)

                @block.vector
                def _(v):
                    v.wait_ge(sems["f1_ivld"], 16 * NT)
                    vsync(v, v.tensor_single_scalar(f_inv[:], f_inv[:],
                                                    1.0 / HF, ALU.mult))
                    v.tensor_single_scalar(f_inv[:], f_inv[:], EPS_RMS,
                                           ALU.add).then_inc(sems["f1_iv"], 1)
                    v.wait_ge(sems["f1_a1"], 1)
                    vsync(v, v.reciprocal(f_inv[:], f_sq[:]))
                    for tt in range(NT):
                        v.wait_ge(sems["f1_pe"], 4 * (tt + 1))
                        v.wait_ge(sems["f1_xld"], 32 * (tt + 1))
                        bank0 = (tt % 2) * 4
                        v.tensor_tensor(
                            f_o[:], psD[:, bank0 * 512:(bank0 + 4) * 512],
                            f_pr[tt % 2][:], ALU.add)
                        v.scalar_tensor_tensor(
                            f_o[:], f_o[:], f_inv[:, tt:tt + 1],
                            f_x1[tt % 2][:], ALU.mult,
                            ALU.add).then_inc(sems["f1_d1"], 1)
                        vsync(v, v.reduce_sum(f_st[:, 1:2], f_o[:], AX.X))
                        vsync(v, v.tensor_single_scalar(
                            f_st[:, 0:1], f_st[:, 1:2], 1.0 / D, ALU.mult))
                        v.tensor_scalar(f_o[:], f_o[:], f_st[:, 0:1],
                                        None, ALU.subtract).then_inc(sems["f1_d1"], 1)
                        v.wait_ge(sems["f1_a1"], tt + 2)
                        vsync(v, v.tensor_single_scalar(
                            f_st[:, 2:3], f_st[:, 4:5], 1.0 / D, ALU.mult))
                        v.tensor_single_scalar(f_st[:, 2:3], f_st[:, 2:3],
                                               EPS_LN,
                                               ALU.add).then_inc(sems["f1_d2"], 1)
                        v.wait_ge(sems["f1_a2"], tt + 1)
                        vsync(v, v.reciprocal(f_st[:, 3:4], f_st[:, 5:6]))
                        if tt >= 2:
                            v.wait_ge(sems["f1_st"], 16 * (tt - 1))
                        v.tensor_scalar(f_out[tt % 2][:], f_o[:],
                                        f_st[:, 3:4], None, ALU.mult
                                        ).then_inc(sems["f1_dn"], 1)

    return nc


def _consts():
    ident = np.eye(128, dtype=np.float32).astype(ml_dtypes.bfloat16)
    jj, ii = np.meshgrid(np.arange(128), np.arange(128), indexing="ij")
    mask = (jj <= ii).astype(np.uint8)
    ones1 = np.ones((128, 1), dtype=np.float32)
    return ident, mask, ones1


def kernel(L0_attn_x, L0_attn_Wq, L0_attn_Wk, L0_attn_Wv, L0_attn_Wg,
           L0_attn_Wo, L0_attn_gamma, L0_attn_beta, L0_ffn_w_gate_up,
           L0_ffn_w_proj):
    ident, mask, ones1 = _consts()
    nc = build()
    xs = np.asarray(L0_attn_x, dtype=np.float32)
    shared = dict(
        wq=np.ascontiguousarray(L0_attn_Wq, dtype=np.float32),
        wk=np.ascontiguousarray(L0_attn_Wk, dtype=np.float32),
        wv=np.ascontiguousarray(L0_attn_Wv, dtype=np.float32),
        wg=np.ascontiguousarray(L0_attn_Wg, dtype=np.float32),
        wo=np.ascontiguousarray(L0_attn_Wo, dtype=np.float32),
        gamma=np.ascontiguousarray(L0_attn_gamma, dtype=np.float32),
        beta=np.ascontiguousarray(L0_attn_beta, dtype=np.float32),
        wgu=np.ascontiguousarray(L0_ffn_w_gate_up, dtype=np.float32),
        wpr=np.ascontiguousarray(L0_ffn_w_proj, dtype=np.float32),
        identc=ident, maskc=mask, ones1c=ones1,
    )
    in_maps = [dict(x=np.ascontiguousarray(xs[i]), **shared) for i in range(8)]
    res = run_bass_kernel_spmd(nc, in_maps, core_ids=list(range(8)))
    return np.stack([res.results[i]["out"] for i in range(8)], axis=0)


# version 10 (final)
# speedup vs baseline: 1.0396x; 1.0228x over previous
import contextlib
import numpy as np
import ml_dtypes
import concourse.bass as bass
import concourse.mybir as mybir
from concourse.bass_utils import run_bass_kernel_spmd

F32 = mybir.dt.float32
BF16 = mybir.dt.bfloat16
AF = mybir.ActivationFunctionType
ALU = mybir.AluOpType
AX = mybir.AxisListType

D = 2048
L = 4096
H = 16
ND = 16
NT = 32
HF = 8192
CH = 128
SBW = 32
NCH = 32
QSC = float(128.0 ** -0.5)
EPS_LN = 1e-5
EPS_RMS = float(np.finfo(np.float32).eps)
VW = [SBW * (s + 1) for s in range(4)]
VOFF = [0, 1024, 3072, 6144]
KTVW = 10240
NPAIR = 64


def build(debug=False):
    nc = bass.Bass("TRN2", target_bir_lowering=False)

    def dram(name, shape, dt, kind="Internal"):
        if debug and kind == "Internal":
            kind = "ExternalOutput"
        return nc.dram_tensor(name, shape, dt, kind=kind).ap()

    x = dram("x", [L, D], F32, "ExternalInput")
    wq = dram("wq", [D, D], F32, "ExternalInput")
    wk = dram("wk", [D, D], F32, "ExternalInput")
    wv = dram("wv", [D, D], F32, "ExternalInput")
    wg = dram("wg", [D, D], F32, "ExternalInput")
    wo = dram("wo", [D, D], F32, "ExternalInput")
    gamma = dram("gamma", [D], F32, "ExternalInput")
    beta = dram("beta", [D], F32, "ExternalInput")
    wgu = dram("wgu", [2 * HF, D], F32, "ExternalInput")
    wpr = dram("wpr", [D, HF], F32, "ExternalInput")
    identc = dram("identc", [128, 128], BF16, "ExternalInput")
    maskc = dram("maskc", [128, 128], mybir.dt.uint8, "ExternalInput")
    ones1c = dram("ones1c", [128, 1], F32, "ExternalInput")
    out = dram("out", [L, D], F32, "ExternalOutput")

    xbf = dram("xbf", [L, D], BF16)
    wqb = dram("wqb", [D, D], BF16)
    wkb = dram("wkb", [D, D], BF16)
    wvb = dram("wvb", [D, D], BF16)
    wgb = dram("wgb", [D, D], BF16)
    wob = dram("wob", [D, D], BF16)
    wguq = dram("wguq", [2 * HF, D], BF16)
    wprq = dram("wprq", [D, HF], BF16)
    qt = dram("qt", [D, L], BF16)
    kt = dram("kt", [D, L], BF16)
    gsp = dram("gsp", [D, L], F32)
    vtok = dram("vtok", [H, L, 128], BF16)
    ysp = dram("ysp", [H, L, 128], BF16)
    lny = dram("lny", [L, D], BF16)
    x1f = dram("x1f", [L, D], F32)
    x1b = dram("x1b", [L, D], BF16)
    hsp = dram("hsp", [NPAIR, NT, 128, 128], BF16)
    prt = dram("prt", [L, D], F32)
    ssqd = dram("ssqd", [L], F32)

    g = contextlib.ExitStack()
    with g:
        sems = {}

        def sem(name):
            sems[name] = g.enter_context(nc.semaphore(name))
            return sems[name]

        vself = g.enter_context(nc.semaphore("vself"))
        vctr = [0]

        def vsync(v, instr):
            vctr[0] += 1
            instr.then_inc(vself, 1)
            v.wait_ge(vself, vctr[0])
            return instr

        identsb = g.enter_context(nc.sbuf_tensor("identsb", [128, 128], BF16))
        masksb = g.enter_context(nc.sbuf_tensor("masksb", [128, 128],
                                                 mybir.dt.uint8))
        ones1sb = g.enter_context(nc.sbuf_tensor("ones1sb", [128, 1], F32))

        s_wg = sem("s_wg")
        s_x = sem("s_x")
        s_wc = sem("s_wc")
        sc0 = sem("sc0")

        pstk = contextlib.ExitStack()
        with pstk:
            xts = pstk.enter_context(
                nc.sbuf_tensor("xts", [128, ND * L], BF16, is_transpose_dest=True))
            wgT = [pstk.enter_context(
                nc.sbuf_tensor(f"wgT{i}", [128, ND * 512], BF16,
                               is_transpose_dest=True)) for i in range(2)]
            psA = pstk.enter_context(nc.psum_tensor("psA", [128, 4096], F32))
            sxts = sem("sxts")

            gstk = contextlib.ExitStack()
            with gstk:
                gstg = [gstk.enter_context(
                    nc.sbuf_tensor(f"gstg{i}", [128, L], F32)) for i in range(2)]
                gab = [gstk.enter_context(
                    nc.sbuf_tensor(f"gab{i}", [128, 512], F32)) for i in range(2)]
                geb = [gstk.enter_context(
                    nc.sbuf_tensor(f"geb{i}", [128, 512], F32)) for i in range(2)]
                glb = [gstk.enter_context(
                    nc.sbuf_tensor(f"glb{i}", [128, 512], F32)) for i in range(2)]

                g_ld = sem("g_ld")
                g_pe = sem("g_pe")
                g_ln = sem("g_ln")
                g_sb = sem("g_sb")
                g_st = sem("g_st")

                with nc.Block() as block:
                    @block.sync
                    def _(sy):
                        sy.dma_start(identsb[:], identc).then_inc(sc0, 16)
                        sy.dma_start(masksb[:], maskc).then_inc(sc0, 16)
                        sy.dma_start(ones1sb[:], ones1c).then_inc(sc0, 16)
                        for grp in range(2):
                            sy.wait_ge(s_wg, 16)
                            for d in range(ND):
                                sy.dma_start_transpose(
                                    wgT[grp % 2][:, d * 512:(d + 1) * 512],
                                    wgb[grp * 512:(grp + 1) * 512,
                                        d * 128:(d + 1) * 128]).then_inc(g_ld, 16)
                        sy.wait_ge(s_x, 16 * ND)
                        for d in range(ND):
                            sy.dma_start_transpose(
                                xts[:, d * L:(d + 1) * L],
                                xbf[:, d * 128:(d + 1) * 128]).then_inc(sxts, 16)
                        for grp in range(2, 4):
                            sy.wait_ge(g_pe, 32 * (grp - 1))
                            for d in range(ND):
                                sy.dma_start_transpose(
                                    wgT[grp % 2][:, d * 512:(d + 1) * 512],
                                    wgb[grp * 512:(grp + 1) * 512,
                                        d * 128:(d + 1) * 128]).then_inc(g_ld, 16)
                        sy.wait_ge(sc0, 48)

                    @block.tensor
                    def _(te):
                        te.wait_ge(sxts, 16 * ND)
                        for dout in range(ND):
                            grp, di = dout // 4, dout % 4
                            if di == 0:
                                te.wait_ge(g_ld, 16 * ND * (grp + 1))
                            if dout > 0:
                                te.wait_ge(g_ln, 8 * dout)
                                te.wait_ge(g_sb, 8 * dout)
                            for d in range(ND):
                                for tc in range(8):
                                    mm = te.matmul(
                                        psA[:, tc * 512:(tc + 1) * 512],
                                        wgT[grp % 2][:, d * 512 + di * 128:
                                                     d * 512 + di * 128 + 128],
                                        xts[:, d * L + tc * 512:
                                            d * L + tc * 512 + 512],
                                        start=(d == 0), stop=(d == ND - 1))
                                    if d == ND - 1:
                                        mm.then_inc(g_pe, 1)

                    @block.scalar
                    def _(sc):
                        for dout in range(ND):
                            for tc in range(8):
                                i = dout * 8 + tc
                                sc.wait_ge(g_pe, i + 1)
                                ps = psA[:, tc * 512:(tc + 1) * 512]
                                sc.activation(gab[tc % 2][:], ps,
                                              AF.Abs)
                                sc.activation(geb[tc % 2][:], gab[tc % 2][:],
                                              AF.Exp, scale=-1.0)
                                if i >= 2:
                                    sc.wait_ge(g_sb, i - 1)
                                sc.activation(glb[tc % 2][:], geb[tc % 2][:],
                                              AF.Ln, bias=1.0).then_inc(g_ln, 1)

                    @block.vector
                    def _(v):
                        for dout in range(ND):
                            if dout >= 2:
                                v.wait_ge(g_st, 16 * (dout - 1))
                            for tc in range(8):
                                i = dout * 8 + tc
                                v.wait_ge(g_pe, i + 1)
                                v.wait_ge(g_ln, i + 1)
                                ps = psA[:, tc * 512:(tc + 1) * 512]
                                v.scalar_tensor_tensor(
                                    gstg[dout % 2][:, tc * 512:(tc + 1) * 512],
                                    ps, 0.0, glb[tc % 2][:],
                                    ALU.min, ALU.subtract).then_inc(g_sb, 1)

                    @block.gpsimd
                    def _(gp):
                        gp.dma_start(wgb, wg).then_inc(s_wg, 16)
                        for d in range(ND):
                            gp.dma_start(xbf[:, d * 128:(d + 1) * 128],
                                         x[:, d * 128:(d + 1) * 128]
                                         ).then_inc(s_x, 16)
                        casts = {0: (wqb, wq), 4: (wkb, wk),
                                 8: (wvb, wv), 12: (wob, wo)}
                        for dout in range(ND):
                            if dout in casts:
                                dst_, src_ = casts[dout]
                                gp.dma_start(dst_, src_).then_inc(s_wc, 16)
                            gp.wait_ge(g_sb, 8 * (dout + 1))
                            gp.dma_start(gsp[dout * 128:(dout + 1) * 128, :],
                                         gstg[dout % 2][:]).then_inc(g_st, 16)
                        gp.wait_ge(g_st, 16 * ND)

            qstk = contextlib.ExitStack()
            with qstk:
                ostg = [qstk.enter_context(
                    nc.sbuf_tensor(f"ostg{i}", [128, L], BF16)) for i in range(2)]
                qwt = [qstk.enter_context(
                    nc.sbuf_tensor(f"qwt{i}", [128, D], BF16)) for i in range(2)]
                qwh = [qstk.enter_context(
                    nc.sbuf_tensor(f"qwh{i}", [128, D], BF16)) for i in range(2)]
                qred = qstk.enter_context(nc.sbuf_tensor("qred", [128, 32], F32))
                qsbf = [qstk.enter_context(nc.sbuf_tensor(f"qsbf{i}", [128, 32],
                                                          BF16)) for i in range(2)]
                qrq = qstk.enter_context(nc.sbuf_tensor("qrq", [128, 32], F32))
                qrr = qstk.enter_context(nc.sbuf_tensor("qrr", [128, D], BF16))
                qlt = qstk.enter_context(nc.sbuf_tensor("qlt", [128, D], BF16))
                qgt = qstk.enter_context(nc.sbuf_tensor("qgt", [128, D], BF16))

                CFGS = [("q", wqb, qt, QSC, 0), ("k", wkb, kt, None, 64)]
                QS_ = {}
                for tag, _, _, _, _ in CFGS:
                    for snm in ("ld", "pe", "ev", "st", "qld", "qdv", "qst",
                                "qpw"):
                        QS_[f"{tag}_{snm}"] = sem(f"{tag}_{snm}")

                with nc.Block() as block:
                    @block.sync
                    def _(sy):
                        for pi_, (tag, wsrc, dst, scale, tile0) in enumerate(CFGS):
                            sld, qdv, qst = (QS_[f"{tag}_ld"], QS_[f"{tag}_qdv"],
                                             QS_[f"{tag}_qst"])
                            qpw = QS_[f"{tag}_qpw"]
                            sy.wait_ge(s_wc, 64)
                            for grp in range(4):
                                if pi_ == 1 and grp < 2:
                                    sy.wait_ge(QS_["q_pe"], 96 + 32 * grp)
                                if grp >= 2:
                                    sy.wait_ge(QS_[f"{tag}_pe"], 32 * (grp - 1))
                                for d in range(ND):
                                    sy.dma_start_transpose(
                                        wgT[grp % 2][:, d * 512:(d + 1) * 512],
                                        wsrc[grp * 512:(grp + 1) * 512,
                                             d * 128:(d + 1) * 128]
                                    ).then_inc(sld, 16)
                                lo = [0, 8, 16, 40][grp]
                                hi = [8, 16, 40, 64][grp]
                                for i in range(lo, hi):
                                    sy.wait_ge(qpw, i + 1)
                                    sy.dma_start(
                                        wguq[(tile0 + i) * 128:
                                             (tile0 + i + 1) * 128, :],
                                        qwh[i % 2][:]).then_inc(qst, 16)
                            sy.wait_ge(qst, 16 * 64)

                    @block.tensor
                    def _(te):
                        te.wait_ge(sxts, 16 * ND)
                        for pi_, (tag, wsrc, dst, scale, tile0) in enumerate(CFGS):
                            sld, spe, sev = (QS_[f"{tag}_ld"], QS_[f"{tag}_pe"],
                                             QS_[f"{tag}_ev"])
                            if pi_ == 1:
                                te.wait_ge(QS_["q_ev"], 128)
                            for dout in range(ND):
                                grp, di = dout // 4, dout % 4
                                if di == 0:
                                    te.wait_ge(sld, 16 * ND * (grp + 1))
                                if dout > 0:
                                    te.wait_ge(sev, 8 * dout)
                                for d in range(ND):
                                    for tc in range(8):
                                        mm = te.matmul(
                                            psA[:, tc * 512:(tc + 1) * 512],
                                            wgT[grp % 2][:, d * 512 + di * 128:
                                                         d * 512 + di * 128 + 128],
                                            xts[:, d * L + tc * 512:
                                                d * L + tc * 512 + 512],
                                            start=(d == 0), stop=(d == ND - 1))
                                        if d == ND - 1:
                                            mm.then_inc(spe, 1)

                    @block.scalar
                    def _(sc):
                        for pi_, (tag, wsrc, dst, scale, tile0) in enumerate(CFGS):
                            spe, sev, sst = (QS_[f"{tag}_pe"], QS_[f"{tag}_ev"],
                                             QS_[f"{tag}_st"])
                            if pi_ == 1:
                                sc.wait_ge(QS_["q_st"], 16 * 16)
                            for dout in range(ND):
                                if dout >= 2:
                                    sc.wait_ge(sst, 16 * (dout - 1))
                                for tc in range(8):
                                    sc.wait_ge(spe, dout * 8 + tc + 1)
                                    ps = psA[:, tc * 512:(tc + 1) * 512]
                                    dsl = ostg[dout % 2][:, tc * 512:(tc + 1) * 512]
                                    if scale is None:
                                        sc.copy(dsl, ps).then_inc(sev, 1)
                                    else:
                                        sc.mul(dsl, ps, scale).then_inc(sev, 1)

                    @block.gpsimd
                    def _(gp):
                        for pi_, (tag, wsrc, dst, scale, tile0) in enumerate(CFGS):
                            sev, sst, qld = (QS_[f"{tag}_ev"], QS_[f"{tag}_st"],
                                             QS_[f"{tag}_qld"])
                            qdv, qst, qpw = (QS_[f"{tag}_qdv"], QS_[f"{tag}_qst"],
                                             QS_[f"{tag}_qpw"])

                            def poolw(i):
                                gp.wait_ge(qdv, i + 1)
                                if i >= 2:
                                    gp.wait_ge(qst, 16 * (i - 1))
                                gp.tensor_tensor(
                                    qwh[i % 2][:].rearrange("p (a j) -> p a j",
                                                            a=32, j=64),
                                    qgt[:].rearrange("p (a j) -> p a j",
                                                     a=32, j=64),
                                    qsbf[i % 2][:].unsqueeze(2).broadcast_to(
                                        [128, 32, 64]),
                                    ALU.mult).then_inc(qpw, 1)

                            if pi_ == 1:
                                gp.wait_ge(QS_["q_qdv"], 64)
                            for dout in range(ND):
                                gp.wait_ge(sev, 8 * (dout + 1))
                                gp.dma_start(dst[dout * 128:(dout + 1) * 128, :],
                                             ostg[dout % 2][:]).then_inc(sst, 16)
                                for i in range(4 * dout, 4 * dout + 4):
                                    if i >= 2:
                                        gp.wait_ge(qdv, i - 1)
                                    gp.dma_start(
                                        qwt[i % 2][:],
                                        wgu[(tile0 + i) * 128:
                                            (tile0 + i + 1) * 128, :]
                                    ).then_inc(qld, 16)
                                    if i >= 2:
                                        poolw(i - 2)
                            poolw(62)
                            poolw(63)
                            gp.wait_ge(sst, 16 * ND)

                    @block.vector
                    def _(v):
                        for pi_, (tag, wsrc, dst, scale, tile0) in enumerate(CFGS):
                            qld, qdv = QS_[f"{tag}_qld"], QS_[f"{tag}_qdv"]
                            qpw = QS_[f"{tag}_qpw"]
                            if pi_ == 1:
                                v.wait_ge(QS_["q_qst"], 16 * 64)
                            for i in range(64):
                                v.wait_ge(qld, 16 * (i + 1))
                                w3 = qwt[i % 2][:].rearrange("p (a j) -> p a j",
                                                             a=32, j=64)
                                vsync(v, v.reduce_sum(
                                    qred[:].rearrange("p (a o) -> p a o",
                                                      a=32, o=1),
                                    w3, AX.X, apply_absolute_value=True))
                                vsync(v, v.tensor_single_scalar(
                                    qred[:], qred[:], 1.0 / 64, ALU.mult))
                                vsync(v, v.tensor_single_scalar(
                                    qsbf[i % 2][:], qred[:], 1e-8, ALU.max))
                                vsync(v, v.reciprocal(qrq[:], qsbf[i % 2][:]))
                                v.tensor_tensor(
                                    qrr[:].rearrange("p (a j) -> p a j",
                                                     a=32, j=64),
                                    w3,
                                    qrq[:].unsqueeze(2).broadcast_to(
                                        [128, 32, 64]), ALU.mult)
                                v.tensor_single_scalar(qlt[:], qrr[:], -0.5,
                                                       ALU.is_lt)
                                if i >= 1:
                                    v.wait_ge(qpw, i)
                                v.tensor_single_scalar(qgt[:], qrr[:], 0.5,
                                                       ALU.is_gt)
                                v.tensor_sub(qgt[:], qgt[:],
                                             qlt[:]).then_inc(qdv, 1)

            vstk = contextlib.ExitStack()
            with vstk:
                vstg = [vstk.enter_context(
                    nc.sbuf_tensor(f"vstg{i}", [128, D], BF16)) for i in range(2)]
                wvT2 = [vstk.enter_context(
                    nc.sbuf_tensor(f"wvT2{i}", [128, ND * 512], BF16,
                                   is_transpose_dest=True)) for i in range(2)]
                sv_ld = sem("sv_ld")
                sv_pe = sem("sv_pe")
                sv_ev = sem("sv_ev")
                sv_st = sem("sv_st")

                with nc.Block() as block:
                    @block.sync
                    def _(sy):
                        sy.wait_ge(s_wc, 64)
                        wdst = [wgT[0], wgT[1], wvT2[0], wvT2[1]]
                        for vc in range(4):
                            for d in range(ND):
                                sy.dma_start_transpose(
                                    wdst[vc][:, d * 512:(d + 1) * 512],
                                    wvb[vc * 512:(vc + 1) * 512,
                                        d * 128:(d + 1) * 128]).then_inc(sv_ld, 16)

                    @block.tensor
                    def _(te):
                        te.wait_ge(sv_ld, 16 * 4 * ND)
                        wsrc = [wgT[0], wgT[1], wvT2[0], wvT2[1]]
                        for tt in range(NT):
                            if tt >= 2:
                                te.wait_ge(sv_ev, 4 * (tt - 1))
                            for d in range(ND):
                                for vc in range(4):
                                    bank = (tt % 2) * 4 + vc
                                    mm = te.matmul(
                                        psA[:, bank * 512:(bank + 1) * 512],
                                        xts[:, d * L + tt * 128:
                                            d * L + tt * 128 + 128],
                                        wsrc[vc][:, d * 512:(d + 1) * 512],
                                        start=(d == 0), stop=(d == ND - 1))
                                    if d == ND - 1:
                                        mm.then_inc(sv_pe, 1)

                    @block.scalar
                    def _(sc):
                        for tt in range(NT):
                            if tt >= 2:
                                sc.wait_ge(sv_st, 16 * (tt - 1))
                            for vc in range(4):
                                sc.wait_ge(sv_pe, tt * 4 + vc + 1)
                                bank = (tt % 2) * 4 + vc
                                sc.copy(vstg[tt % 2][:, vc * 512:(vc + 1) * 512],
                                        psA[:, bank * 512:(bank + 1) * 512]
                                        ).then_inc(sv_ev, 1)

                    @block.gpsimd
                    def _(gp):
                        vdst = vtok.rearrange("h t v -> t h v")
                        for tt in range(NT):
                            gp.wait_ge(sv_ev, 4 * (tt + 1))
                            gp.dma_start(
                                vdst[tt * 128:(tt + 1) * 128, :, :],
                                vstg[tt % 2][:].rearrange("p (h v) -> p h v",
                                                          h=H, v=128)
                            ).then_inc(sv_st, 16)
                        gp.wait_ge(sv_st, 16 * NT)

        astk = contextlib.ExitStack()
        with astk:
            a_g = astk.enter_context(nc.sbuf_tensor("a_g", [128, L], F32))
            a_q = [astk.enter_context(nc.sbuf_tensor(f"a_q{i}", [128, L], BF16))
                   for i in range(2)]
            a_k = [astk.enter_context(nc.sbuf_tensor(f"a_k{i}", [128, L], BF16))
                   for i in range(2)]
            a_gg = astk.enter_context(nc.sbuf_tensor("a_gg", [128, L + 1], F32))
            p_ar = [astk.enter_context(nc.sbuf_tensor(f"p_ar{i}", [128, 2048], F32))
                    for i in range(2)]
            a_e = [astk.enter_context(nc.sbuf_tensor(f"a_e{i}", [128, 2048], BF16))
                   for i in range(2)]
            a_qt = [astk.enter_context(nc.sbuf_tensor(f"a_qt{i}", [128, L], BF16))
                    for i in range(2)]
            a_qh = [astk.enter_context(nc.sbuf_tensor(f"a_qh{i}", [128, L], BF16))
                    for i in range(2)]
            a_kh = [astk.enter_context(nc.sbuf_tensor(f"a_kh{i}", [128, L], BF16))
                    for i in range(2)]
            a_kv = [astk.enter_context(nc.sbuf_tensor(f"a_kv{i}", [128, KTVW], BF16))
                    for i in range(2)]
            a_dc = [astk.enter_context(nc.sbuf_tensor(f"a_dc{i}", [128, NCH], F32))
                    for i in range(2)]
            a_da = astk.enter_context(nc.sbuf_tensor("a_da", [128, NCH], F32))
            b_v = [astk.enter_context(nc.sbuf_tensor(f"b_v{i}", [128, L], BF16))
                   for i in range(2)]
            b_s = [astk.enter_context(nc.sbuf_tensor(f"b_s{i}", [128, 128], BF16))
                   for i in range(2)]
            b_a = [astk.enter_context(nc.sbuf_tensor(f"b_a{i}", [128, 128], BF16))
                   for i in range(2)]
            b_z = astk.enter_context(nc.sbuf_tensor("b_z", [128, 128], BF16))
            b_kt = [astk.enter_context(nc.sbuf_tensor(f"b_kt{i}", [128, 128], BF16))
                    for i in range(2)]
            b_y = astk.enter_context(nc.sbuf_tensor("b_y", [128, 1024], BF16))
            pAm = [astk.enter_context(nc.psum_tensor(f"pAm{i}", [128, 128], F32))
                   for i in range(2)]
            pY = [astk.enter_context(nc.psum_tensor(f"pY{i}", [128, 128], F32))
                  for i in range(2)]
            pS = [astk.enter_context(nc.psum_tensor(f"pS{i}", [128, 128], F32))
                  for i in range(2)]
            pT = [astk.enter_context(nc.psum_tensor(f"pT{i}", [128, 128], BF16))
                  for i in range(2)]

            sgl = sem("sgl")
            sql = sem("sql")
            sa_pg = sem("sa_pg")
            sa_ag = sem("sa_ag")
            sa_ex = sem("sa_ex")
            sa_mu = sem("sa_mu")
            bl = sem("bl")
            bpa = sem("bpa")
            bpt = sem("bpt")
            bpy = sem("bpy")
            bps = sem("bps")
            bda = sem("bda")
            bds = sem("bds")
            bkt = sem("bkt")
            bay = sem("bay")
            byst = sem("byst")
            NK = 32 * H

            def SD(K):
                return K + (K // 32) + 2

            VSPEC = [(0, 0, 32), (1, 0, 32), (2, 0, 16), (2, 1, 16),
                     (3, 0, 16), (3, 1, 16)]

            def stage_arg(v, h, k, ab=None):
                if k < 6:
                    which, hf = k // 2, k % 2
                    base = hf * 2048
                    ob = ab[k % 2]
                    i0 = a_gg[:, 1 + base:1 + base + 2048]
                    if which == 0:
                        ref = a_gg[:, base:base + 2048].rearrange(
                            "p (c j) -> p c j", c=16, j=128)[:, :, 0:1]
                        return v.tensor_tensor(
                            ob[:].rearrange("p (c j) -> p c j", c=16, j=128),
                            i0.rearrange("p (c j) -> p c j", c=16, j=128),
                            ref.broadcast_to([128, 16, 128]), ALU.subtract)
                    if which == 1:
                        ref = a_gg[:, base:base + 2048].rearrange(
                            "p (b j) -> p b j", b=64, j=32)[:, :, 0:1]
                        return v.tensor_tensor(
                            ob[:].rearrange("p (b j) -> p b j", b=64, j=32),
                            i0.rearrange("p (b j) -> p b j", b=64, j=32),
                            ref.broadcast_to([128, 64, 32]), ALU.subtract)
                    ref = a_gg[:, 1 + base:1 + base + 2048].rearrange(
                        "p (c j) -> p c j", c=16, j=128)[:, :, 127:128]
                    return v.tensor_tensor(
                        ob[:].rearrange("p (c j) -> p c j", c=16, j=128),
                        ref.broadcast_to([128, 16, 128]),
                        i0.rearrange("p (c j) -> p c j", c=16, j=128),
                        ALU.subtract)
                if k < 12:
                    s, part, ncb = VSPEC[k - 6]
                    w = VW[s]
                    cs = part * 16
                    gj = a_gg[:, 1 + cs * 128:1 + (cs + ncb) * 128].rearrange(
                        "p (c j) -> p c j", c=ncb, j=128)[:, :, 0:w]
                    ref = a_gg[:, 0:L].rearrange(
                        "p (c j) -> p c j", c=NCH, j=128)[:, cs:cs + ncb,
                                                          SBW * s:SBW * s + 1]
                    ob = ab[k % 2]
                    return v.tensor_tensor(
                        ob[:, 0:ncb * w].rearrange(
                            "p (c j) -> p c j", c=ncb, j=w),
                        ref.broadcast_to([128, ncb, w]), gj, ALU.subtract)
                ends = a_gg[:, 1:1 + L].rearrange(
                    "p (c j) -> p c j", c=NCH, j=128)[:, :, 127:128]
                starts = a_gg[:, 0:L].rearrange(
                    "p (c j) -> p c j", c=NCH, j=128)[:, :, 0:1]
                return v.tensor_tensor(
                    a_da[:].rearrange("p (c o) -> p c o", c=NCH, o=1),
                    ends, starts, ALU.subtract)

            def stage_w(k):
                if k < 6:
                    return 2048
                if k < 12:
                    s, part, ncb = VSPEC[k - 6]
                    return ncb * VW[s]
                return NCH

            def stage_mul(v, h, k):
                hb = h % 2
                if k < 6:
                    which, hf = k // 2, k % 2
                    srcqk = a_q[hb] if which < 2 else a_k[hb]
                    dst = (a_qt, a_qh, a_kh)[which][hb]
                    return v.tensor_tensor(dst[:, hf * 2048:(hf + 1) * 2048],
                                           srcqk[:, hf * 2048:(hf + 1) * 2048],
                                           a_e[k % 2][:], ALU.mult)
                s, part, ncb = VSPEC[k - 6]
                w = VW[s]
                cs = part * 16
                ks = a_k[hb][:, cs * 128:(cs + ncb) * 128].rearrange(
                    "p (c j) -> p c j", c=ncb, j=128)[:, :, 0:w]
                return v.tensor_tensor(
                    a_kv[hb][:, VOFF[s] + cs * w:
                             VOFF[s] + (cs + ncb) * w].rearrange(
                        "p (c j) -> p c j", c=ncb, j=w),
                    ks,
                    a_e[k % 2][:, 0:ncb * w].rearrange("p (c j) -> p c j",
                                                       c=ncb, j=w),
                    ALU.mult)


            def prep_dve_slots(h):
                def guard(v, h=h):
                    if h >= 1:
                        v.wait_ge(bpy, 32 * (h - 1))
                        v.wait_ge(bps, 32 * (h - 1))
                        v.wait_ge(sa_pg, 12 * h)
                    v.wait_ge(sgl, 16 * (h + 1))
                    v.memset(a_gg[:, 0:1], 0.0)

                def scanq(qtr):
                    def op(v, h=h, qtr=qtr):
                        ini = 0.0 if qtr == 0 else a_gg[:, qtr * 1024:qtr * 1024 + 1]
                        vsync(v, v.tensor_tensor_scan(
                            a_gg[:, 1 + qtr * 1024:1 + (qtr + 1) * 1024],
                            a_g[:, qtr * 1024:(qtr + 1) * 1024],
                            a_g[:, qtr * 1024:(qtr + 1) * 1024],
                            ini, ALU.add, ALU.bypass))
                    return op

                def arg(k):
                    def op(v, h=h, k=k):
                        if h >= 1:
                            v.wait_ge(sa_ex, 13 * h)
                        stage_arg(v, h, k).then_inc(sa_ag, 1)
                    return op

                def mul(k):
                    def op(v, h=h, k=k):
                        if k == 0:
                            v.wait_ge(sql, 32 * (h + 1))
                        v.wait_ge(sa_ex, 13 * h + k + 1)
                        stage_mul(v, h, k).then_inc(sa_mu, 1)
                    return op

                slots = {1: [guard], 2: [scanq(0), scanq(1)],
                         3: [scanq(2), scanq(3)], 4: [arg(12)]}
                for k in range(12):
                    slots.setdefault(6 + 2 * k, []).append(mul(k))
                return slots

            def prep_act_ops(h):
                ops = []
                for k in range(13):
                    def ex(sc, h=h, k=k):
                        if k < 12:
                            sc.wait_ge(sa_pg, 12 * h + k + 1)
                        else:
                            sc.wait_ge(sa_ag, h + 1)
                        if k >= 2 and k < 12:
                            sc.wait_ge(sa_mu, 12 * h + k - 1)
                        elif k == 0 and h >= 1:
                            sc.wait_ge(sa_mu, 12 * h)
                        if k < 12:
                            w = stage_w(k)
                            sc.activation(a_e[k % 2][:, 0:w],
                                          p_ar[k % 2][:, 0:w],
                                          AF.Exp).then_inc(sa_ex, 1)
                        else:
                            if h >= 2:
                                sc.wait_ge(bds, 33 * (h - 1))
                            sc.activation(a_dc[h % 2][:], a_da[:],
                                          AF.Exp).then_inc(sa_ex, 1)
                    ops.append(ex)
                return ops

            def emit_A(te, K):
                h, c = K // 32, K % 32
                te.wait_ge(bda, K + 1)
                for s in range(4):
                    w = VW[s]
                    mm = te.matmul(
                        pAm[K % 2][0:w, SBW * s:SBW * s + SBW],
                        a_kv[h % 2][:, VOFF[s] + c * w:VOFF[s] + (c + 1) * w],
                        a_qh[h % 2][:, c * 128 + SBW * s:c * 128 + SBW * s + SBW],
                        start=True, stop=True)
                    if s == 3:
                        mm.then_inc(bpa, 1)

            with nc.Block() as block:
                @block.sync
                def _(sy):
                    for h in range(H):
                        if h >= 1:
                            sy.wait_ge(sa_ag, h)
                        sy.dma_start(a_g[:],
                                     gsp[h * 128:(h + 1) * 128, :]).then_inc(sgl, 16)
                        if h >= 2:
                            sy.wait_ge(sa_mu, 12 * (h - 1))
                        sy.dma_start(a_q[h % 2][:],
                                     qt[h * 128:(h + 1) * 128, :]).then_inc(sql, 16)
                        sy.dma_start(a_k[h % 2][:],
                                     kt[h * 128:(h + 1) * 128, :]).then_inc(sql, 16)
                        if h >= 2:
                            sy.wait_ge(bpy, 32 * (h - 1))
                            sy.wait_ge(bps, 32 * (h - 1))
                        sy.dma_start(
                            b_v[h % 2][:].rearrange("p (c v) -> p c v",
                                                    c=NCH, v=128),
                            vtok[h, :, :].rearrange("(c p) v -> p c v",
                                                    c=NCH, p=128)).then_inc(bl, 16)

                @block.gpsimd
                def _(gp):
                    for h in range(H):
                        gp.wait_ge(sa_ag, h + 1)
                        for k in range(12):
                            if k >= 2:
                                gp.wait_ge(sa_ex, 13 * h + k - 1)
                            elif h >= 1:
                                gp.wait_ge(sa_ex, 13 * (h - 1) + 11 + k)
                            stage_arg(gp, h, k, p_ar).then_inc(sa_pg, 1)

                @block.vector
                def _(v):
                    v.memset(b_z[:], 0.0)
                    v.memset(pAm[0][:], 0.0).then_inc(bda, 1)
                    v.memset(pAm[1][:], 0.0).then_inc(bda, 1)
                    for c2 in range(1, 30):
                        for op in prep_dve_slots(0).get(c2, []):
                            op(v)
                    v.wait_ge(bpa, 1)
                    v.select(b_a[0][:], masksb[:], pAm[0][:],
                             b_z[:]).then_inc(bda, 1)
                    for h in range(H):
                        pq = prep_dve_slots(h + 1) if h + 1 < H else {}
                        v.wait_ge(bpy, 32 * h)
                        v.memset(b_s[0][:], 0.0).then_inc(bds, 1)
                        for c in range(NCH):
                            K = 32 * h + c
                            v.wait_ge(bps, K + 1)
                            v.wait_ge(bpy, K)
                            v.scalar_tensor_tensor(
                                b_s[(c + 1) % 2][:], b_s[c % 2][:],
                                a_dc[h % 2][:, c:c + 1], pS[K % 2][:],
                                ALU.mult, ALU.add).then_inc(bds, 1)
                            v.wait_ge(bds, SD(K))
                            if K + 1 < NK:
                                v.wait_ge(bpa, K + 2)
                                v.wait_ge(bpy, K)
                                v.select(b_a[(K + 1) % 2][:], masksb[:],
                                         pAm[(K + 1) % 2][:],
                                         b_z[:]).then_inc(bda, 1)
                            for op in pq.get(c, []):
                                op(v)

                @block.scalar
                def _(sc):
                    for op in prep_act_ops(0):
                        op(sc)
                    for h in range(H):
                        aq = prep_act_ops(h + 1) if h + 1 < H else []
                        aq = ({5 + 2 * k: [aq[k]] for k in range(12)}
                              | {28: [aq[12]]}) if aq else {}
                        for c in range(NCH):
                            K = 32 * h + c
                            sc.wait_ge(bpt, K + 1)
                            if K >= 2:
                                sc.wait_ge(bps, K - 1)
                            sc.copy(b_kt[K % 2][:], pT[K % 2][:]).then_inc(bkt, 1)
                            sc.wait_ge(bpy, K + 1)
                            if K // 4 >= 1:
                                sc.wait_ge(byst, 16 * (K // 4 - 1))
                            sc.copy(b_y[:, (K % 8) * 128:(K % 8 + 1) * 128],
                                    pY[K % 2][:]).then_inc(bay, 1)
                            if K % 4 == 3:
                                hb = ((K // 4) % 2) * 512
                                sc.dma_start(
                                    ysp[h, (c - 3) * 128:(c + 1) * 128, :].rearrange(
                                        "(b p) v -> p b v", b=4, p=128),
                                    b_y[:, hb:hb + 512].rearrange(
                                        "p (b v) -> p b v", b=4, v=128)
                                ).then_inc(byst, 16)
                            for op in aq.get(c, []):
                                op(sc)
                    sc.wait_ge(byst, 16 * (NK // 4))

                @block.tensor
                def _(te):
                    te.wait_ge(sa_mu, 12)
                    te.wait_ge(sa_ex, 13)
                    te.wait_ge(bl, 16)
                    emit_A(te, 0)
                    te.wait_ge(bkt, 0)
                    te.transpose(pT[0][:], a_kh[0][:, 0:128],
                                 identsb[:]).then_inc(bpt, 1)
                    for K in range(NK):
                        h, c = K // 32, K % 32
                        if c == 0 and h > 0:
                            te.wait_ge(sa_mu, 12 * (h + 1))
                            te.wait_ge(sa_ex, 13 * (h + 1))
                            te.wait_ge(bl, 16 * (h + 1))
                        te.wait_ge(bkt, K + 1)
                        if K >= 2:
                            te.wait_ge(bds, SD(K - 2))
                        te.matmul(pS[K % 2][:], b_kt[K % 2][:],
                                  b_v[h % 2][:, c * 128:(c + 1) * 128],
                                  start=True, stop=True).then_inc(bps, 1)
                        if K + 1 < NK:
                            K1 = K + 1
                            h1, c1 = K1 // 32, K1 % 32
                            if c1 == 0:
                                te.wait_ge(sa_mu, 12 * (h1 + 1))
                                te.wait_ge(sa_ex, 13 * (h1 + 1))
                                te.wait_ge(bl, 16 * (h1 + 1))
                            emit_A(te, K1)
                            if K1 >= 2:
                                te.wait_ge(bkt, K1 - 1)
                            te.transpose(pT[K1 % 2][:],
                                         a_kh[h1 % 2][:, c1 * 128:(c1 + 1) * 128],
                                         identsb[:]).then_inc(bpt, 1)
                        te.wait_ge(bda, K + 3)
                        te.wait_ge(bds, SD(K - 1) if c > 0 else 33 * h + 1)
                        if K >= 1:
                            te.wait_ge(bay, K - 1)
                        te.matmul(pY[K % 2][:], b_a[K % 2][:],
                                  b_v[h % 2][:, c * 128:(c + 1) * 128],
                                  start=True, stop=False)
                        te.matmul(pY[K % 2][:], a_qt[h % 2][:, c * 128:(c + 1) * 128],
                                  b_s[c % 2][:], start=False,
                                  stop=True).then_inc(bpy, 1)
        castk = contextlib.ExitStack()
        with castk:
            c_y = [castk.enter_context(nc.sbuf_tensor(f"c_y{i}", [128, D], BF16))
                   for i in range(2)]
            c_yc = castk.enter_context(nc.sbuf_tensor("c_yc", [128, D], F32))
            c_t1 = castk.enter_context(nc.sbuf_tensor("c_t1", [128, D], F32))
            c_junk = castk.enter_context(nc.sbuf_tensor("c_junk", [128, D], BF16))
            c_ln = [castk.enter_context(nc.sbuf_tensor(f"c_ln{i}", [128, D], BF16))
                    for i in range(2)]
            c_st = castk.enter_context(nc.sbuf_tensor("c_st", [128, 8], F32))
            c_gam = castk.enter_context(nc.sbuf_tensor("c_gam", [128, D], F32))
            c_bet = castk.enter_context(nc.sbuf_tensor("c_bet", [128, D], F32))
            pwt2 = castk.enter_context(nc.sbuf_tensor("pwt2", [128, HF], BF16))
            pwh2 = castk.enter_context(nc.sbuf_tensor("pwh2", [128, HF], BF16))
            pred2 = castk.enter_context(nc.sbuf_tensor("pred2", [128, 128], F32))
            psbf2 = castk.enter_context(nc.sbuf_tensor("psbf2", [128, 128], BF16))
            prq2 = castk.enter_context(nc.sbuf_tensor("prq2", [128, 128], F32))
            prr2 = castk.enter_context(nc.sbuf_tensor("prr2", [128, HF], BF16))
            plt2 = castk.enter_context(nc.sbuf_tensor("plt2", [128, HF], BF16))
            pgt2 = castk.enter_context(nc.sbuf_tensor("pgt2", [128, HF], BF16))

            ca_ld = sem("ca_ld")
            ca_d1 = sem("ca_d1")
            ca_a1 = sem("ca_a1")
            ca_d2 = sem("ca_d2")
            ca_a2 = sem("ca_a2")
            ca_dn = sem("ca_dn")
            ca_st = sem("ca_st")
            cq_ld = sem("cq_ld")
            cq_dv = sem("cq_dv")
            cq_pl = sem("cq_pl")
            cq_st = sem("cq_st")

            with nc.Block() as block:
                @block.sync
                def _(sy):
                    sy.dma_start(c_gam[:],
                                 gamma.unsqueeze(0).broadcast_to([128, D])
                                 ).then_inc(ca_ld, 16)
                    sy.dma_start(c_bet[:],
                                 beta.unsqueeze(0).broadcast_to([128, D])
                                 ).then_inc(ca_ld, 16)
                    ysrc = ysp.rearrange("h t v -> t h v")
                    for tt in range(NT):
                        if tt >= 2:
                            sy.wait_ge(ca_dn, tt - 1)
                        sy.dma_start(
                            c_y[tt % 2][:].rearrange("p (h v) -> p h v", h=H, v=128),
                            ysrc[tt * 128:(tt + 1) * 128, :, :]).then_inc(ca_ld, 16)
                        if tt % 4 == 0 and 1 <= tt // 4:
                            i = tt // 4 - 1
                            sy.wait_ge(cq_pl, i + 1)
                            sy.dma_start(wprq[(8 + i) * 128:(9 + i) * 128, :],
                                         pwh2[:]).then_inc(cq_st, 16)
                    sy.wait_ge(cq_pl, 8)
                    sy.dma_start(wprq[15 * 128:16 * 128, :],
                                 pwh2[:]).then_inc(cq_st, 16)
                    sy.wait_ge(cq_st, 16 * 8)

                @block.vector
                def _(v):
                    for tt in range(NT):
                        v.wait_ge(ca_ld, 16 * (tt + 3))
                        vsync(v, v.reduce_sum(c_st[:, 1:2], c_y[tt % 2][:], AX.X))
                        vsync(v, v.tensor_single_scalar(c_st[:, 0:1],
                                                        c_st[:, 1:2],
                                                        1.0 / D, ALU.mult))
                        v.tensor_scalar(c_yc[:], c_y[tt % 2][:], c_st[:, 0:1],
                                        None, ALU.subtract).then_inc(ca_d1, 1)
                        v.wait_ge(ca_a1, tt + 1)
                        vsync(v, v.tensor_single_scalar(c_st[:, 2:3],
                                                        c_st[:, 4:5],
                                                        1.0 / D, ALU.mult))
                        v.tensor_single_scalar(c_st[:, 2:3], c_st[:, 2:3],
                                               EPS_LN, ALU.add).then_inc(ca_d2, 1)
                        v.wait_ge(ca_a2, tt + 1)
                        vsync(v, v.reciprocal(c_st[:, 3:4], c_st[:, 5:6]))
                        v.scalar_tensor_tensor(c_t1[:], c_yc[:], c_st[:, 3:4],
                                               c_gam[:], ALU.mult, ALU.mult)
                        if tt >= 2:
                            v.wait_ge(ca_st, 16 * (tt - 1))
                        v.tensor_tensor(c_ln[tt % 2][:], c_t1[:], c_bet[:],
                                        ALU.add).then_inc(ca_dn, 1)
                        if tt % 4 == 3 and tt // 4 < 8:
                            i = tt // 4
                            v.wait_ge(cq_ld, 16 * (i + 1))
                            w3 = pwt2[:].rearrange("p (a j) -> p a j",
                                                   a=128, j=64)
                            vsync(v, v.reduce_sum(
                                pred2[:].rearrange("p (a o) -> p a o", a=128, o=1),
                                w3, AX.X, apply_absolute_value=True))
                            vsync(v, v.tensor_single_scalar(pred2[:], pred2[:],
                                                            1.0 / 64, ALU.mult))
                            if i >= 1:
                                v.wait_ge(cq_pl, i)
                            vsync(v, v.tensor_single_scalar(psbf2[:], pred2[:],
                                                            1e-8, ALU.max))
                            vsync(v, v.reciprocal(prq2[:], psbf2[:]))
                            v.tensor_tensor(
                                prr2[:].rearrange("p (a j) -> p a j", a=128, j=64),
                                w3,
                                prq2[:].unsqueeze(2).broadcast_to([128, 128, 64]),
                                ALU.mult)
                            v.tensor_single_scalar(plt2[:], prr2[:], -0.5,
                                                   ALU.is_lt)
                            v.tensor_single_scalar(pgt2[:], prr2[:], 0.5,
                                                   ALU.is_gt)
                            v.tensor_sub(pgt2[:], pgt2[:],
                                         plt2[:]).then_inc(cq_dv, 1)

                @block.scalar
                def _(sc):
                    for tt in range(NT):
                        sc.wait_ge(ca_d1, tt + 1)
                        sc.activation(c_junk[:], c_yc[:], AF.Square,
                                      accum_out=c_st[:, 4:5]).then_inc(ca_a1, 1)
                        sc.wait_ge(ca_d2, tt + 1)
                        sc.activation(c_st[:, 5:6], c_st[:, 2:3],
                                      AF.Sqrt).then_inc(ca_a2, 1)

                @block.gpsimd
                def _(gp):
                    for tt in range(NT):
                        gp.wait_ge(ca_dn, tt + 1)
                        gp.dma_start(lny[tt * 128:(tt + 1) * 128, :],
                                     c_ln[tt % 2][:]).then_inc(ca_st, 16)
                        if tt % 4 == 0 and tt // 4 < 8:
                            i = tt // 4
                            if i >= 1:
                                gp.wait_ge(cq_dv, i)
                            gp.dma_start(pwt2[:],
                                         wpr[(8 + i) * 128:(9 + i) * 128, :]
                                         ).then_inc(cq_ld, 16)
                        if tt % 4 == 3 and tt // 4 < 8:
                            i = tt // 4
                            gp.wait_ge(cq_dv, i + 1)
                            if i >= 1:
                                gp.wait_ge(cq_st, 16 * i)
                            gp.tensor_tensor(
                                pwh2[:].rearrange("p (a j) -> p a j", a=128, j=64),
                                pgt2[:].rearrange("p (a j) -> p a j", a=128, j=64),
                                psbf2[:].unsqueeze(2).broadcast_to([128, 128, 64]),
                                ALU.mult).then_inc(cq_pl, 1)
                    gp.wait_ge(ca_st, 16 * NT)

        dstk = contextlib.ExitStack()
        with dstk:
            d_lnyT = dstk.enter_context(
                nc.sbuf_tensor("d_lnyT", [128, ND * 2048], BF16,
                               is_transpose_dest=True))
            d_woT = dstk.enter_context(
                nc.sbuf_tensor("d_woT", [128, ND * 2048], BF16,
                               is_transpose_dest=True))
            d_x = [dstk.enter_context(nc.sbuf_tensor(f"d_x{i}", [128, D], F32))
                   for i in range(2)]
            d_rc = dstk.enter_context(nc.sbuf_tensor("d_rc", [128, D], F32))
            d_junk = dstk.enter_context(nc.sbuf_tensor("d_junk", [128, D], BF16))
            d_x1 = [dstk.enter_context(nc.sbuf_tensor(f"d_x1{i}", [128, D], F32))
                    for i in range(2)]
            d_x1b = [dstk.enter_context(nc.sbuf_tensor(f"d_x1b{i}", [128, D], BF16))
                     for i in range(2)]
            d_st = dstk.enter_context(nc.sbuf_tensor("d_st", [128, 8], F32))
            qw2t = dstk.enter_context(nc.sbuf_tensor("qw2t", [128, 2048], BF16))
            qw2h = [dstk.enter_context(nc.sbuf_tensor(f"qw2h{i}", [128, 2048], BF16))
                    for i in range(2)]
            q2red = dstk.enter_context(nc.sbuf_tensor("q2red", [128, 32], F32))
            q2sbf = dstk.enter_context(nc.sbuf_tensor("q2sbf", [128, 32], BF16))
            q2rq = dstk.enter_context(nc.sbuf_tensor("q2rq", [128, 32], F32))
            q2rr = dstk.enter_context(nc.sbuf_tensor("q2rr", [128, 2048], BF16))
            q2lt = dstk.enter_context(nc.sbuf_tensor("q2lt", [128, 2048], BF16))
            q2gt = dstk.enter_context(nc.sbuf_tensor("q2gt", [128, 2048], BF16))
            psB = dstk.enter_context(nc.psum_tensor("psB", [128, 4096], F32))

            da_wld = sem("da_wld")
            da_yld = sem("da_yld")
            da_xld = sem("da_xld")
            da_pe = sem("da_pe")
            da_d1 = sem("da_d1")
            da_a1 = sem("da_a1")
            da_d2 = sem("da_d2")
            da_a2 = sem("da_a2")
            da_dnv = sem("da_dnv")
            da_dna = sem("da_dna")
            da_st = sem("da_st")
            q2_ld = sem("q2_ld")
            q2_dv = sem("q2_dv")
            q2_pl = sem("q2_pl")
            q2_st = sem("q2_st")

            with nc.Block() as block:
                @block.sync
                def _(sy):
                    def q2store(u):
                        i, qq = u // 4, u % 4
                        sy.wait_ge(q2_pl, u + 1)
                        sy.dma_start(
                            wprq[i * 128:(i + 1) * 128,
                                 qq * 2048:(qq + 1) * 2048],
                            qw2h[u % 2][:]).then_inc(q2_st, 16)

                    sy.wait_ge(s_wc, 64)
                    for d in range(ND):
                        sy.dma_start_transpose(
                            d_woT[:, d * 2048:(d + 1) * 2048],
                            wob[:, d * 128:(d + 1) * 128]).then_inc(da_wld, 16)
                    for d in range(ND):
                        sy.dma_start_transpose(
                            d_lnyT[:, d * 2048:(d + 1) * 2048],
                            lny[0:2048, d * 128:(d + 1) * 128]).then_inc(da_yld, 16)
                    for u in range(14):
                        q2store(u)
                    sy.wait_ge(da_pe, 64)
                    for d in range(ND):
                        sy.dma_start_transpose(
                            d_lnyT[:, d * 2048:(d + 1) * 2048],
                            lny[2048:4096, d * 128:(d + 1) * 128]).then_inc(da_yld, 16)
                    for u in range(14, 32):
                        q2store(u)
                    sy.wait_ge(q2_st, 16 * 32)

                @block.tensor
                def _(te):
                    te.wait_ge(da_wld, 16 * ND)
                    for tt in range(NT):
                        th, ti = tt // 16, tt % 16
                        te.wait_ge(da_yld, 16 * ND * (th + 1))
                        if tt >= 2:
                            te.wait_ge(da_d1, 2 * (tt - 1))
                        for d in range(ND):
                            for oc in range(4):
                                bank = (tt % 2) * 4 + oc
                                mm = te.matmul(
                                    psB[:, bank * 512:(bank + 1) * 512],
                                    d_lnyT[:, d * 2048 + ti * 128:
                                           d * 2048 + ti * 128 + 128],
                                    d_woT[:, d * 2048 + oc * 512:
                                          d * 2048 + oc * 512 + 512],
                                    start=(d == 0), stop=(d == ND - 1))
                                if d == ND - 1:
                                    mm.then_inc(da_pe, 1)

                @block.vector
                def _(v):
                    for tt in range(NT):
                        v.wait_ge(da_pe, 4 * (tt + 1))
                        v.wait_ge(da_xld, 16 * (tt + 1))
                        bank0 = (tt % 2) * 4
                        v.tensor_tensor(d_rc[:],
                                        psB[:, bank0 * 512:(bank0 + 4) * 512],
                                        d_x[tt % 2][:], ALU.add).then_inc(da_d1, 1)
                        vsync(v, v.reduce_sum(d_st[:, 1:2], d_rc[:], AX.X))
                        vsync(v, v.tensor_single_scalar(d_st[:, 0:1],
                                                        d_st[:, 1:2],
                                                        1.0 / D, ALU.mult))
                        v.tensor_scalar(d_rc[:], d_rc[:], d_st[:, 0:1], None,
                                        ALU.subtract).then_inc(da_d1, 1)
                        v.wait_ge(da_a1, tt + 1)
                        vsync(v, v.tensor_single_scalar(d_st[:, 2:3],
                                                        d_st[:, 4:5],
                                                        1.0 / D, ALU.mult))
                        v.tensor_single_scalar(d_st[:, 2:3], d_st[:, 2:3],
                                               EPS_LN, ALU.add).then_inc(da_d2, 1)
                        v.wait_ge(da_a2, tt + 1)
                        vsync(v, v.reciprocal(d_st[:, 3:4], d_st[:, 5:6]))
                        if tt >= 2:
                            v.wait_ge(da_st, 32 * (tt - 1))
                        v.tensor_scalar(d_x1[tt % 2][:], d_rc[:], d_st[:, 3:4],
                                        None, ALU.mult).then_inc(da_dnv, 1)
                        u = tt
                        v.wait_ge(q2_ld, 16 * (u + 1))
                        w3 = qw2t[:].rearrange("p (a j) -> p a j", a=32, j=64)
                        vsync(v, v.reduce_sum(
                            q2red[:].rearrange("p (a o) -> p a o", a=32, o=1),
                            w3, AX.X, apply_absolute_value=True))
                        vsync(v, v.tensor_single_scalar(q2red[:], q2red[:],
                                                        1.0 / 64, ALU.mult))
                        if u >= 1:
                            v.wait_ge(q2_pl, u)
                        vsync(v, v.tensor_single_scalar(q2sbf[:], q2red[:],
                                                        1e-8, ALU.max))
                        vsync(v, v.reciprocal(q2rq[:], q2sbf[:]))
                        v.tensor_tensor(q2rr[:].rearrange("p (a j) -> p a j",
                                                          a=32, j=64), w3,
                                        q2rq[:].unsqueeze(2).broadcast_to(
                                            [128, 32, 64]), ALU.mult)
                        v.tensor_single_scalar(q2lt[:], q2rr[:], -0.5, ALU.is_lt)
                        v.tensor_single_scalar(q2gt[:], q2rr[:], 0.5, ALU.is_gt)
                        v.tensor_sub(q2gt[:], q2gt[:],
                                     q2lt[:]).then_inc(q2_dv, 1)

                @block.scalar
                def _(sc):
                    for tt in range(NT):
                        sc.wait_ge(da_d1, 2 * tt + 2)
                        sc.activation(d_junk[:], d_rc[:], AF.Square,
                                      accum_out=d_st[:, 4:5]).then_inc(da_a1, 1)
                        sc.wait_ge(da_d2, tt + 1)
                        sc.activation(d_st[:, 5:6], d_st[:, 2:3],
                                      AF.Sqrt).then_inc(da_a2, 1)
                        sc.wait_ge(da_dnv, tt + 1)
                        if tt >= 2:
                            sc.wait_ge(da_st, 32 * (tt - 1))
                        sc.copy(d_x1b[tt % 2][:],
                                d_x1[tt % 2][:]).then_inc(da_dna, 1)

                @block.gpsimd
                def _(gp):
                    gp.dma_start(d_x[0][:], x[0:128, :]).then_inc(da_xld, 16)
                    gp.dma_start(d_x[1][:], x[128:256, :]).then_inc(da_xld, 16)
                    gp.dma_start(qw2t[:],
                                 wpr[0:128, 0:2048]).then_inc(q2_ld, 16)
                    for tt in range(NT):
                        if tt + 1 < NT:
                            u = tt + 1
                            i, qq = u // 4, u % 4
                            gp.wait_ge(q2_dv, u)
                            gp.dma_start(
                                qw2t[:],
                                wpr[i * 128:(i + 1) * 128,
                                    qq * 2048:(qq + 1) * 2048]).then_inc(q2_ld, 16)
                        gp.wait_ge(q2_dv, tt + 1)
                        if tt >= 2:
                            gp.wait_ge(q2_st, 16 * (tt - 1))
                        gp.tensor_tensor(qw2h[tt % 2][:].rearrange(
                                             "p (a j) -> p a j", a=32, j=64),
                                         q2gt[:].rearrange("p (a j) -> p a j",
                                                           a=32, j=64),
                                         q2sbf[:].unsqueeze(2).broadcast_to(
                                             [128, 32, 64]),
                                         ALU.mult).then_inc(q2_pl, 1)
                        gp.wait_ge(da_dnv, tt + 1)
                        gp.wait_ge(da_dna, tt + 1)
                        gp.dma_start(x1f[tt * 128:(tt + 1) * 128, :],
                                     d_x1[tt % 2][:]).then_inc(da_st, 16)
                        gp.dma_start(x1b[tt * 128:(tt + 1) * 128, :],
                                     d_x1b[tt % 2][:]).then_inc(da_st, 16)
                        if tt + 2 < NT:
                            gp.wait_ge(da_d1, 2 * tt + 1)
                            gp.dma_start(d_x[tt % 2][:],
                                         x[(tt + 2) * 128:(tt + 3) * 128, :]
                                         ).then_inc(da_xld, 16)
                    gp.wait_ge(da_st, 32 * NT)

        estk = contextlib.ExitStack()
        with estk:
            e_x1T = estk.enter_context(
                nc.sbuf_tensor("e_x1T", [128, ND * L], BF16,
                               is_transpose_dest=True))
            e_w = [estk.enter_context(
                nc.sbuf_tensor(f"e_w{i}", [128, ND * 512], BF16,
                               is_transpose_dest=True)) for i in range(2)]
            e_s1 = estk.enter_context(nc.sbuf_tensor("e_s1", [128, L], BF16))
            e_h = [estk.enter_context(nc.sbuf_tensor(f"e_h{i}", [128, L], BF16))
                   for i in range(2)]
            e_acc = estk.enter_context(nc.sbuf_tensor("e_acc", [128, L], F32))
            e_ssq = estk.enter_context(nc.sbuf_tensor("e_ssq", [1, 512], F32))
            cstk7 = contextlib.ExitStack()

            ea_xld = sem("ea_xld")
            psC = cstk7.enter_context(nc.psum_tensor("psC", [128, 4096], F32))
            ea_wld = sem("ea_wld")
            ea_pe = sem("ea_pe")
            ea_sl = sem("ea_sl")
            ea_dh = sem("ea_dh")
            ea_da = sem("ea_da")
            ea_st = sem("ea_st")

            with nc.Block() as block:
                @block.sync
                def _(sy):
                    for d in range(ND):
                        sy.dma_start_transpose(
                            e_x1T[:, d * L:(d + 1) * L],
                            x1b[:, d * 128:(d + 1) * 128]).then_inc(ea_xld, 16)
                    for grp in range(32):
                        if grp >= 2:
                            sy.wait_ge(ea_pe, 32 * (grp - 1))
                        for d in range(ND):
                            sy.dma_start_transpose(
                                e_w[grp % 2][:, d * 512:d * 512 + 256],
                                wguq[grp * 256:(grp + 1) * 256,
                                     d * 128:(d + 1) * 128]).then_inc(ea_wld, 16)
                            sy.dma_start_transpose(
                                e_w[grp % 2][:, d * 512 + 256:(d + 1) * 512],
                                wguq[HF + grp * 256:HF + (grp + 1) * 256,
                                     d * 128:(d + 1) * 128]).then_inc(ea_wld, 16)

                @block.tensor
                def _(te):
                    te.wait_ge(ea_xld, 16 * ND)
                    for pair in range(NPAIR):
                        grp, pi = pair // 2, pair % 2
                        te.wait_ge(ea_wld, 16 * 32 * (grp + 1))
                        for kind in range(2):
                            for d in range(ND):
                                for tc in range(8):
                                    if d == 0:
                                        if kind == 0 and pair > 0:
                                            te.wait_ge(ea_dh,
                                                       8 * (pair - 1) + tc + 1)
                                        if kind == 1:
                                            te.wait_ge(ea_sl,
                                                       8 * pair + tc + 1)
                                    col = d * 512 + kind * 256 + pi * 128
                                    mm = te.matmul(
                                        psC[:, tc * 512:(tc + 1) * 512],
                                        e_w[grp % 2][:, col:col + 128],
                                        e_x1T[:, d * L + tc * 512:
                                              d * L + tc * 512 + 512],
                                        start=(d == 0), stop=(d == ND - 1))
                                    if d == ND - 1:
                                        mm.then_inc(ea_pe, 1)

                @block.scalar
                def _(sc):
                    for pair in range(NPAIR):
                        if pair >= 1:
                            sc.wait_ge(ea_da, pair)
                        for tc in range(8):
                            sc.wait_ge(ea_pe, pair * 16 + tc + 1)
                            sc.activation(e_s1[:, tc * 512:(tc + 1) * 512],
                                          psC[:, tc * 512:(tc + 1) * 512],
                                          AF.Silu).then_inc(ea_sl, 1)

                @block.vector
                def _(v):
                    v.memset(e_acc[:], 0.0)
                    for pair in range(NPAIR):
                        if pair >= 2:
                            v.wait_ge(ea_st, 16 * (pair - 1))
                        for tc in range(8):
                            v.wait_ge(ea_pe, pair * 16 + 8 + tc + 1)
                            v.tensor_tensor(
                                e_h[pair % 2][:, tc * 512:(tc + 1) * 512],
                                e_s1[:, tc * 512:(tc + 1) * 512],
                                psC[:, tc * 512:(tc + 1) * 512],
                                ALU.mult).then_inc(ea_dh, 1)
                        v.tensor_tensor(e_s1[:], e_h[pair % 2][:],
                                        e_h[pair % 2][:], ALU.mult)
                        v.tensor_tensor(e_acc[:], e_acc[:], e_s1[:],
                                        ALU.add).then_inc(ea_da, 1)

                @block.gpsimd
                def _(gp):
                    for pair in range(NPAIR):
                        gp.wait_ge(ea_dh, 8 * (pair + 1))
                        gp.dma_start(
                            hsp[pair, :, :, :].rearrange("t h j -> h t j"),
                            e_h[pair % 2][:].rearrange("p (t j) -> p t j",
                                                       t=NT, j=128)
                        ).then_inc(ea_st, 16)
                    gp.wait_ge(ea_st, 16 * NPAIR)

            cstk7.close()
            with (nc.psum_tensor("psR", [1, 512], F32) as psR,
                  nc.semaphore("er_pe") as er_pe,
                  nc.semaphore("er_ev") as er_ev,
                  nc.semaphore("er_st") as er_st):
                with nc.Block() as block:
                    @block.tensor
                    def _(te):
                        for oc in range(8):
                            if oc >= 1:
                                te.wait_ge(er_ev, oc)
                            te.matmul(psR[:], ones1sb[:],
                                      e_acc[:, oc * 512:(oc + 1) * 512],
                                      start=True, stop=True).then_inc(er_pe, 1)

                    @block.scalar
                    def _(sc):
                        for oc in range(8):
                            sc.wait_ge(er_pe, oc + 1)
                            if oc >= 1:
                                sc.wait_ge(er_st, 16 * oc)
                            sc.copy(e_ssq[0:1, :], psR[:]).then_inc(er_ev, 1)

                    @block.gpsimd
                    def _(gp):
                        for oc in range(8):
                            gp.wait_ge(er_ev, oc + 1)
                            gp.dma_start(
                                ssqd[oc * 512:(oc + 1) * 512].unsqueeze(0),
                                e_ssq[0:1, :]).then_inc(er_st, 16)
                        gp.wait_ge(er_st, 16 * 8)

        fstk = contextlib.ExitStack()
        with fstk:
            f_wp = fstk.enter_context(
                nc.sbuf_tensor("f_wp", [128, 32 * 2048], BF16,
                               is_transpose_dest=True))
            f_h = [fstk.enter_context(
                nc.sbuf_tensor(f"f_h{i}", [128, 32 * 128], BF16)) for i in range(2)]
            f_pr = [fstk.enter_context(nc.sbuf_tensor(f"f_pr{i}", [128, D], F32))
                    for i in range(2)]
            f_x1 = [fstk.enter_context(nc.sbuf_tensor(f"f_x1{i}", [128, D], F32))
                    for i in range(2)]
            f_o = fstk.enter_context(nc.sbuf_tensor("f_o", [128, D], F32))
            f_junk = fstk.enter_context(nc.sbuf_tensor("f_junk", [128, D], BF16))
            f_out = [fstk.enter_context(nc.sbuf_tensor(f"f_out{i}", [128, D], F32))
                     for i in range(2)]
            f_inv = fstk.enter_context(nc.sbuf_tensor("f_inv", [128, NT], F32))
            f_sq = fstk.enter_context(nc.sbuf_tensor("f_sq", [128, NT], F32))
            f_st = fstk.enter_context(nc.sbuf_tensor("f_st", [128, 8], F32))
            psD = fstk.enter_context(nc.psum_tensor("psD", [128, 4096], F32))

            for nm in ("f0_wld", "f0_hld", "f0_pe", "f0_ev", "f0_st",
                       "f1_wld", "f1_hld", "f1_pe", "f1_xld", "f1_ivld",
                       "f1_d1", "f1_a1", "f1_d2", "f1_a2", "f1_dn", "f1_iv",
                       "f1_st"):
                sem(nm)

            with nc.Block() as block:
                @block.sync
                def _(sy):
                    for half in range(2):
                        s_wld = sems[f"f{half}_wld"]
                        s_hld = sems[f"f{half}_hld"]
                        s_pe = sems[f"f{half}_pe"]
                        if half == 1:
                            for tt in range(NT):
                                sy.dma_start(
                                    f_inv[:, tt:tt + 1],
                                    ssqd[tt * 128:(tt + 1) * 128].unsqueeze(1)
                                ).then_inc(sems["f1_ivld"], 16)
                            sy.wait_ge(sems["f0_pe"], 4 * NT)
                        for ht in range(32):
                            sy.dma_start_transpose(
                                f_wp[:, ht * 2048:(ht + 1) * 2048],
                                wprq[:, (half * 32 + ht) * 128:
                                     (half * 32 + ht + 1) * 128]
                            ).then_inc(s_wld, 16)
                        for tt in range(NT):
                            if tt >= 2:
                                sy.wait_ge(s_pe, 4 * (tt - 1))
                            sy.dma_start(
                                f_h[tt % 2][:].rearrange("p (b t) -> p b t",
                                                         b=32, t=128),
                                hsp[half * 32:(half + 1) * 32, tt, :, :].rearrange(
                                    "b p t -> p b t")
                            ).then_inc(s_hld, 16)

                @block.tensor
                def _(te):
                    for half in range(2):
                        s_wld = sems[f"f{half}_wld"]
                        s_hld = sems[f"f{half}_hld"]
                        s_pe = sems[f"f{half}_pe"]
                        if half == 1:
                            te.wait_ge(sems["f0_ev"], 4 * NT)
                        te.wait_ge(s_wld, 16 * 32)
                        for tt in range(NT):
                            te.wait_ge(s_hld, 16 * (tt + 1))
                            if tt >= 2:
                                if half == 0:
                                    te.wait_ge(sems["f0_ev"], 4 * (tt - 1))
                                else:
                                    te.wait_ge(sems["f1_d1"], 2 * (tt - 1))
                            for ht in range(32):
                                for oc in range(4):
                                    bank = (tt % 2) * 4 + oc
                                    mm = te.matmul(
                                        psD[:, bank * 512:(bank + 1) * 512],
                                        f_h[tt % 2][:, ht * 128:(ht + 1) * 128],
                                        f_wp[:, ht * 2048 + oc * 512:
                                             ht * 2048 + oc * 512 + 512],
                                        start=(ht == 0), stop=(ht == 31))
                                    if ht == 31:
                                        mm.then_inc(s_pe, 1)

                @block.gpsimd
                def _(gp):
                    for tt in range(NT):
                        gp.wait_ge(sems["f0_ev"], 4 * (tt + 1))
                        gp.dma_start(prt[tt * 128:(tt + 1) * 128, :],
                                     f_pr[tt % 2][:]).then_inc(sems["f0_st"], 16)
                    gp.wait_ge(sems["f0_st"], 16 * NT)
                    for tt in range(NT):
                        if tt >= 2:
                            gp.wait_ge(sems["f1_d1"], 2 * (tt - 1))
                        gp.dma_start(f_pr[tt % 2][:],
                                     prt[tt * 128:(tt + 1) * 128, :]
                                     ).then_inc(sems["f1_xld"], 16)
                        gp.dma_start(f_x1[tt % 2][:],
                                     x1f[tt * 128:(tt + 1) * 128, :]
                                     ).then_inc(sems["f1_xld"], 16)
                        gp.wait_ge(sems["f1_dn"], tt + 1)
                        gp.dma_start(out[tt * 128:(tt + 1) * 128, :],
                                     f_out[tt % 2][:]).then_inc(sems["f1_st"], 16)
                    gp.wait_ge(sems["f1_st"], 16 * NT)

                @block.scalar
                def _(sc):
                    for tt in range(NT):
                        if tt >= 2:
                            sc.wait_ge(sems["f0_st"], 16 * (tt - 1))
                        for oc in range(4):
                            sc.wait_ge(sems["f0_pe"], 4 * tt + oc + 1)
                            bank = (tt % 2) * 4 + oc
                            sc.copy(f_pr[tt % 2][:, oc * 512:(oc + 1) * 512],
                                    psD[:, bank * 512:(bank + 1) * 512]
                                    ).then_inc(sems["f0_ev"], 1)
                    sc.wait_ge(sems["f1_iv"], 1)
                    sc.activation(f_sq[:], f_inv[:],
                                  AF.Sqrt).then_inc(sems["f1_a1"], 1)
                    for tt in range(NT):
                        sc.wait_ge(sems["f1_d1"], 2 * tt + 2)
                        sc.activation(f_junk[:], f_o[:], AF.Square,
                                      accum_out=f_st[:, 4:5]
                                      ).then_inc(sems["f1_a1"], 1)
                        sc.wait_ge(sems["f1_d2"], tt + 1)
                        sc.activation(f_st[:, 5:6], f_st[:, 2:3],
                                      AF.Sqrt).then_inc(sems["f1_a2"], 1)

                @block.vector
                def _(v):
                    v.wait_ge(sems["f1_ivld"], 16 * NT)
                    vsync(v, v.tensor_single_scalar(f_inv[:], f_inv[:],
                                                    1.0 / HF, ALU.mult))
                    v.tensor_single_scalar(f_inv[:], f_inv[:], EPS_RMS,
                                           ALU.add).then_inc(sems["f1_iv"], 1)
                    v.wait_ge(sems["f1_a1"], 1)
                    vsync(v, v.reciprocal(f_inv[:], f_sq[:]))
                    for tt in range(NT):
                        v.wait_ge(sems["f1_pe"], 4 * (tt + 1))
                        v.wait_ge(sems["f1_xld"], 32 * (tt + 1))
                        bank0 = (tt % 2) * 4
                        v.tensor_tensor(
                            f_o[:], psD[:, bank0 * 512:(bank0 + 4) * 512],
                            f_pr[tt % 2][:], ALU.add)
                        v.scalar_tensor_tensor(
                            f_o[:], f_o[:], f_inv[:, tt:tt + 1],
                            f_x1[tt % 2][:], ALU.mult,
                            ALU.add).then_inc(sems["f1_d1"], 1)
                        vsync(v, v.reduce_sum(f_st[:, 1:2], f_o[:], AX.X))
                        vsync(v, v.tensor_single_scalar(
                            f_st[:, 0:1], f_st[:, 1:2], 1.0 / D, ALU.mult))
                        v.tensor_scalar(f_o[:], f_o[:], f_st[:, 0:1],
                                        None, ALU.subtract).then_inc(sems["f1_d1"], 1)
                        v.wait_ge(sems["f1_a1"], tt + 2)
                        vsync(v, v.tensor_single_scalar(
                            f_st[:, 2:3], f_st[:, 4:5], 1.0 / D, ALU.mult))
                        v.tensor_single_scalar(f_st[:, 2:3], f_st[:, 2:3],
                                               EPS_LN,
                                               ALU.add).then_inc(sems["f1_d2"], 1)
                        v.wait_ge(sems["f1_a2"], tt + 1)
                        vsync(v, v.reciprocal(f_st[:, 3:4], f_st[:, 5:6]))
                        if tt >= 2:
                            v.wait_ge(sems["f1_st"], 16 * (tt - 1))
                        v.tensor_scalar(f_out[tt % 2][:], f_o[:],
                                        f_st[:, 3:4], None, ALU.mult
                                        ).then_inc(sems["f1_dn"], 1)

    return nc


def _consts():
    ident = np.eye(128, dtype=np.float32).astype(ml_dtypes.bfloat16)
    jj, ii = np.meshgrid(np.arange(128), np.arange(128), indexing="ij")
    mask = (jj <= ii).astype(np.uint8)
    ones1 = np.ones((128, 1), dtype=np.float32)
    return ident, mask, ones1


def kernel(L0_attn_x, L0_attn_Wq, L0_attn_Wk, L0_attn_Wv, L0_attn_Wg,
           L0_attn_Wo, L0_attn_gamma, L0_attn_beta, L0_ffn_w_gate_up,
           L0_ffn_w_proj):
    ident, mask, ones1 = _consts()
    nc = build()
    xs = np.asarray(L0_attn_x, dtype=np.float32)
    shared = dict(
        wq=np.ascontiguousarray(L0_attn_Wq, dtype=np.float32),
        wk=np.ascontiguousarray(L0_attn_Wk, dtype=np.float32),
        wv=np.ascontiguousarray(L0_attn_Wv, dtype=np.float32),
        wg=np.ascontiguousarray(L0_attn_Wg, dtype=np.float32),
        wo=np.ascontiguousarray(L0_attn_Wo, dtype=np.float32),
        gamma=np.ascontiguousarray(L0_attn_gamma, dtype=np.float32),
        beta=np.ascontiguousarray(L0_attn_beta, dtype=np.float32),
        wgu=np.ascontiguousarray(L0_ffn_w_gate_up, dtype=np.float32),
        wpr=np.ascontiguousarray(L0_ffn_w_proj, dtype=np.float32),
        identc=ident, maskc=mask, ones1c=ones1,
    )
    in_maps = [dict(x=np.ascontiguousarray(xs[i]), **shared) for i in range(8)]
    res = run_bass_kernel_spmd(nc, in_maps, core_ids=list(range(8)))
    return np.stack([res.results[i]["out"] for i in range(8)], axis=0)
